# version 11
# baseline (speedup 1.0000x reference)
import numpy as np
import ml_dtypes
from contextlib import ExitStack
import concourse.bass as bass
import concourse.mybir as mybir
from concourse.bass_utils import run_bass_kernel_spmd

F32 = mybir.dt.float32
BF16 = mybir.dt.bfloat16
AF = mybir.ActivationFunctionType
ALU = mybir.AluOpType
AX = mybir.AxisListType

MAGIC = 12582912.0
SQRT128 = float(np.float32(np.sqrt(np.float32(128.0))))
EPS = 1e-5
NTOK = 1024
NBW = 512
NB = 2
ECH = 16
H = 16
MCH = 32
ICH = 44
GW_NUMEL = 11264 * 2048
DW_NUMEL = 2048 * 5632

DEBUG = False
PD1_SERIAL = False


def build_core_program(debug=False):
    nc = bass.Bass("TRN2", target_bir_lowering=False)

    xT = nc.dram_tensor("xT", [2048, NTOK], F32, kind="ExternalInput").ap()
    wqT = nc.dram_tensor("wqT", [2048, 2048], F32, kind="ExternalInput").ap()
    ktT = nc.dram_tensor("ktT", [2048, 4096], F32, kind="ExternalInput").ap()
    vW = nc.dram_tensor("vW", [4096, 2048], F32, kind="ExternalInput").ap()
    wgT = nc.dram_tensor("wgT", [2048, 2048], F32, kind="ExternalInput").ap()
    woT = nc.dram_tensor("woT", [2048, 2048], F32, kind="ExternalInput").ap()
    gwT = nc.dram_tensor("gwT", [2048, 11264], F32, kind="ExternalInput").ap()
    dwT = nc.dram_tensor("dwT", [5632, 2048], F32, kind="ExternalInput").ap()
    cnsI = nc.dram_tensor("cns", [128, 16], F32, kind="ExternalInput").ap()
    onesI = nc.dram_tensor("onesf", [128, 128], F32, kind="ExternalInput").ap()
    identI = nc.dram_tensor("identf", [128, 128], F32, kind="ExternalInput").ap()
    gNI = nc.dram_tensor("gN128", [128, 16], F32, kind="ExternalInput").ap()
    dNI = nc.dram_tensor("dN128", [128, 44], F32, kind="ExternalInput").ap()

    outT = nc.dram_tensor("outT", [2048, NTOK], F32, kind="ExternalOutput").ap()
    attS = nc.dram_tensor("attS", [128, 16, NTOK], BF16, kind="Internal").ap()
    rowsc = nc.dram_tensor("rowsc", [8, 128], F32, kind="Internal").ap()

    dbg = {}
    if debug:
        dbg["xn"] = nc.dram_tensor("dbg_xn", [128, ECH, NTOK], BF16, kind="ExternalOutput").ap()
        dbg["xq"] = nc.dram_tensor("dbg_xq", [128, ECH, NTOK], BF16, kind="ExternalOutput").ap()
        dbg["a0"] = nc.dram_tensor("dbg_a0", [128, H, NTOK], BF16, kind="ExternalOutput").ap()
        dbg["att"] = nc.dram_tensor("dbg_att", [128, 16, NTOK], BF16, kind="ExternalOutput").ap()
        dbg["h"] = nc.dram_tensor("dbg_h", [128, ICH, NTOK], BF16, kind="ExternalOutput").ap()
        dbg["rows"] = nc.dram_tensor("dbg_rows", [1, 6 * NTOK], F32, kind="ExternalOutput").ap()
        dbg["wstat"] = nc.dram_tensor("dbg_wstat", [1, 8], F32, kind="ExternalOutput").ap()
        dbg["xq2"] = nc.dram_tensor("dbg_xq2", [128, ECH, NTOK], BF16, kind="ExternalOutput").ap()
        dbg["cg"] = nc.dram_tensor("dbg_cg", [128, NTOK], F32, kind="ExternalOutput").ap()
        dbg["wq"] = nc.dram_tensor("dbg_wq", [128, 2, ECH, 128], BF16, kind="ExternalOutput").ap()
        dbg["gwf"] = nc.dram_tensor("dbg_gwf", [128, 2, ECH, 128], F32, kind="ExternalOutput").ap()
        dbg["sw"] = nc.dram_tensor("dbg_sw", [128, 2], F32, kind="ExternalOutput").ap()

    st = ExitStack()
    with st:
        cns = st.enter_context(nc.sbuf_tensor("cns_s", [128, 16], F32))
        ones = st.enter_context(nc.sbuf_tensor("ones_s", [128, 128], F32))
        ident = st.enter_context(nc.sbuf_tensor("ident_s", [128, 128], F32))
        gN = st.enter_context(nc.sbuf_tensor("gN_s", [128, 16], F32))
        dN = st.enter_context(nc.sbuf_tensor("dN_s", [128, 44], F32))
        rowt = st.enter_context(nc.sbuf_tensor("rows_s", [1, 6 * NTOK], F32))
        wstat = st.enter_context(nc.sbuf_tensor("wstat_s", [1, 8], F32))
        sw128 = st.enter_context(nc.sbuf_tensor("sw128_s", [128, 2], F32))
        tmp8 = st.enter_context(nc.sbuf_tensor("tmp8_s", [8, 128], F32))

        def ROW(r, sl=None):
            sl = sl if sl is not None else slice(0, NTOK)
            return rowt[0:1, r * NTOK + sl.start: r * NTOK + sl.stop]

        sxq = ExitStack()
        xq_bf = sxq.enter_context(nc.sbuf_tensor("xq_bf", [128, ECH, NTOK], BF16,
                                                 side="right"))

        dW = st.enter_context(nc.semaphore("dW"))
        dX = st.enter_context(nc.semaphore("dX"))
        tS = st.enter_context(nc.semaphore("tS"))
        sS = st.enter_context(nc.semaphore("sS"))
        vS = st.enter_context(nc.semaphore("vS"))
        gS = st.enter_context(nc.semaphore("gS"))
        dSa = st.enter_context(nc.semaphore("dSa"))
        dSb = st.enter_context(nc.semaphore("dSb"))
        dQ = [st.enter_context(nc.semaphore(f"dQ{i}")) for i in range(4)]
        dP = [st.enter_context(nc.semaphore(f"dP{i}")) for i in range(2)]
        dXs = [st.enter_context(nc.semaphore(f"dXs{i}")) for i in range(2)]
        C = dict(dW=0, dX=0, tS=0, sS=0, vS=0, gS=0, dSa=0, dSb=0,
                 dQ0=0, dQ1=0, dQ2=0, dQ3=0, dP0=0, dP1=0, dXs0=0, dXs1=0)

        nc.const_aps.register(F32, EPS, cns[0:1, 0:1])
        nc.const_aps.register(F32, SQRT128, cns[0:1, 1:2])

        R_M, R_RSTD, R_R2, R_T, R_T2, R_T3 = 0, 1, 2, 3, 4, 5
        R_MX = R_M

        _sc = nc.named_scope("P0"); _sc.__enter__()
        with nc.Block() as blk:
            @blk.gpsimd
            def _(g):
                g.dma_start(cns[:, :], cnsI).then_inc(dW, 16)
                g.dma_start(ones[:, :], onesI).then_inc(dW, 16)
                g.dma_start(ident[:, :], identI).then_inc(dW, 16)
                g.dma_start(gN[:, :], gNI).then_inc(dW, 16)
                g.dma_start(dN[:, :], dNI).then_inc(dW, 16)
                C['dW'] += 80
                g.wait_ge(dW, C['dW'])

        _sc.__exit__(None, None, None)
        s2 = ExitStack()
        with s2:
            xn_b = s2.enter_context(nc.sbuf_tensor("xn_b", [128, ECH, NTOK], BF16))

            _sc = nc.named_scope("PA"); _sc.__enter__()
            pa = ExitStack()
            with pa:
                xA = pa.enter_context(nc.sbuf_tensor("xA", [128, NB, ECH, NBW], F32))
                xsq = pa.enter_context(nc.sbuf_tensor("xsq", [128, ECH, NBW], F32))
                maxacc = pa.enter_context(nc.sbuf_tensor("maxacc", [128, 2, NBW], F32))
                mrep = pa.enter_context(nc.sbuf_tensor("mrep", [128, NTOK], F32))
                rrep = pa.enter_context(nc.sbuf_tensor("rrep", [128, NTOK], F32))
                c1rep = pa.enter_context(nc.sbuf_tensor("c1rep", [128, NTOK], F32))
                mxTM = pa.enter_context(nc.sbuf_tensor("mxTM", [128, 8], F32))
                psS = [pa.enter_context(nc.psum_tensor(f"psS{i}", [1, NBW], F32))
                       for i in range(2)]
                psS2 = [pa.enter_context(nc.psum_tensor(f"psS2{i}", [1, NBW], F32))
                        for i in range(2)]
                psR0 = pa.enter_context(nc.psum_tensor("psR0", [128, NBW], F32))
                psR1 = pa.enter_context(nc.psum_tensor("psR1", [128, NBW], F32))
                psT = pa.enter_context(nc.psum_tensor("psT", [128, NBW], F32))
                psT2 = pa.enter_context(nc.psum_tensor("psT2", [8, 128], F32))

                x3 = xT.rearrange("(c p) n -> p c n", p=128)
                q0b, q1b = C['dQ0'], C['dQ1']
                tSb, sSb, vSb, dWb = C['tS'], C['sS'], C['vS'], C['dW']

                with nc.Block() as blk:
                    @blk.sync
                    def _(sy):
                        for nb in range(NB):
                            sy.dma_start(xA[:, nb, :, :],
                                         x3[:, :, nb * NBW:(nb + 1) * NBW]
                                         ).then_inc(dQ[nb], 16)
                    C['dQ0'] = q0b + 16
                    C['dQ1'] = q1b + 16

                    @blk.scalar
                    def _(s):
                        for nb in range(NB):
                            s.wait_ge(dQ[nb], (q0b if nb == 0 else q1b) + 16)
                            if nb == 1:
                                s.wait_ge(tS, tSb + 1)
                            s.activation(xsq[:, :, :], xA[:, nb, :, :], AF.Square
                                         ).then_inc(sS, 1)
                        for nb in range(NB):
                            s.wait_ge(tS, tSb + nb + 1)
                            nbs = slice(nb * NBW, (nb + 1) * NBW)
                            s.copy(ROW(R_T, nbs), psS[nb][:, :])
                            s.copy(ROW(R_T2, nbs), psS2[nb][:, :]).then_inc(sS, 1)
                        s.wait_ge(vS, vSb + 1)
                        s.activation(ROW(R_RSTD), ROW(R_T), AF.Sqrt).then_inc(sS, 1)
                        s.wait_ge(vS, vSb + 2)
                        s.activation(ROW(R_T), ROW(R_R2), AF.Sqrt).then_inc(sS, 1)
                        s.wait_ge(tS, tSb + 4)
                        s.copy(mrep[:, 0:NBW], psR0[:, :])
                        s.copy(mrep[:, NBW:NTOK], psR1[:, :]).then_inc(sS, 1)
                        s.wait_ge(tS, tSb + 6)
                        s.copy(rrep[:, 0:NBW], psR0[:, :])
                        s.copy(rrep[:, NBW:NTOK], psR1[:, :]).then_inc(sS, 1)
                        for nb in range(NB):
                            s.wait_ge(vS, vSb + 3 + 2 * nb + 1)
                            s.copy(xn_b[:, :, nb * NBW:(nb + 1) * NBW], xA[:, nb, :, :]
                                   ).then_inc(sS, 1)
                        s.wait_ge(tS, tSb + 6 + 8 + 1)
                        s.copy(tmp8[:, :], psT2[:, :]).then_inc(sS, 1)
                        s.wait_ge(tS, tSb + 17)
                        s.copy(c1rep[:, 0:NBW], psR0[:, :])
                        s.copy(c1rep[:, NBW:NTOK], psR1[:, :]).then_inc(sS, 1)
                        for nb in range(NB):
                            s.wait_ge(vS, vSb + 10 + nb)
                            s.activation(xA[:, nb, :, :], xA[:, nb, :, :],
                                         AF.Copy, bias=MAGIC).then_inc(sS, 1)

                    @blk.tensor
                    def _(t):
                        for nb in range(NB):
                            t.wait_ge(sS, sSb + nb + 1)
                            for c in range(ECH):
                                t.matmul(psS[nb][:, :], ones[:, 0:1], xA[:, nb, c, :],
                                         start=(c == 0), stop=(c == ECH - 1),
                                         skip_group_check=True)
                            for c in range(ECH):
                                ins = t.matmul(psS2[nb][:, :], ones[:, 0:1], xsq[:, c, :],
                                               start=(c == 0), stop=(c == ECH - 1),
                                               skip_group_check=True)
                            ins.then_inc(tS, 1)
                        t.wait_ge(vS, vSb + 1)
                        t.wait_ge(sS, sSb + 4)
                        t.matmul(psR0[:, :], ones[0:1, :], ROW(R_M, slice(0, NBW)),
                                 start=True, stop=True).then_inc(tS, 1)
                        t.matmul(psR1[:, :], ones[0:1, :], ROW(R_M, slice(NBW, NTOK)),
                                 start=True, stop=True).then_inc(tS, 1)
                        t.wait_ge(vS, vSb + 3)
                        t.wait_ge(sS, sSb + 7)
                        t.matmul(psR0[:, :], ones[0:1, :], ROW(R_RSTD, slice(0, NBW)),
                                 start=True, stop=True).then_inc(tS, 1)
                        t.matmul(psR1[:, :], ones[0:1, :], ROW(R_RSTD, slice(NBW, NTOK)),
                                 start=True, stop=True).then_inc(tS, 1)
                        for nb in range(NB):
                            t.wait_ge(vS, vSb + 3 + 2 * nb + 2)
                            if nb == 1:
                                t.wait_ge(sS, sSb + 8)
                            ps = psT if nb == 0 else psR0
                            for c4 in range(4):
                                t.transpose(ps[:, c4 * 128:(c4 + 1) * 128],
                                            maxacc[:, nb, c4 * 128:(c4 + 1) * 128],
                                            ident[:, :]).then_inc(tS, 1)
                        t.wait_ge(vS, vSb + 8)
                        t.transpose(psT2[:, :], mxTM[:, 0:8], ident[:, :]).then_inc(tS, 1)
                        t.wait_ge(vS, vSb + 9)
                        t.wait_ge(sS, sSb + 8)
                        t.matmul(psR0[:, :], ones[0:1, :], ROW(R_T2, slice(0, NBW)),
                                 start=True, stop=True).then_inc(tS, 1)
                        t.matmul(psR1[:, :], ones[0:1, :], ROW(R_T2, slice(NBW, NTOK)),
                                 start=True, stop=True).then_inc(tS, 1)

                    @blk.vector
                    def _(v):
                        v.wait_ge(sS, sSb + 4)
                        m = ROW(R_M); t_ = ROW(R_T); t2 = ROW(R_T2)
                        r2 = ROW(R_R2); t3 = ROW(R_T3); rstd = ROW(R_RSTD)
                        v.tensor_scalar(m, t_, 1.0 / 2048.0, None, ALU.mult)
                        v.tensor_scalar(t2, t2, 1.0 / 2048.0, None, ALU.mult)
                        v.tensor_mul(t_, m, m)
                        v.tensor_sub(t2, t2, t_)
                        v.tensor_scalar(t_, t2, EPS, None, ALU.add).then_inc(vS, 1)
                        v.wait_ge(sS, sSb + 5)
                        v.reciprocal(rstd, rstd)
                        v.tensor_mul(t3, rstd, rstd)
                        v.tensor_mul(t3, t3, t_)
                        v.tensor_scalar(t3, t3, -0.5, 1.5, ALU.mult, ALU.add)
                        v.tensor_mul(rstd, rstd, t3)
                        v.tensor_mul(r2, rstd, rstd)
                        v.tensor_mul(r2, r2, t2)
                        v.tensor_scalar(r2, r2, EPS, None, ALU.add).then_inc(vS, 1)
                        v.wait_ge(sS, sSb + 6)
                        v.reciprocal(t_, t_)
                        v.tensor_mul(t3, t_, t_)
                        v.tensor_mul(t3, t3, r2)
                        v.tensor_scalar(t3, t3, -0.5, 1.5, ALU.mult, ALU.add)
                        v.tensor_mul(r2, t_, t3).then_inc(vS, 1)
                        for nb in range(NB):
                            v.wait_ge(sS, sSb + 8)
                            nbs = slice(nb * NBW, (nb + 1) * NBW)
                            for c in range(ECH):
                                v.tensor_sub(xA[:, nb, c, :], xA[:, nb, c, :],
                                             mrep[:, nbs])
                                ins = v.tensor_mul(xA[:, nb, c, :], xA[:, nb, c, :],
                                                   rrep[:, nbs])
                            ins.then_inc(vS, 1)
                            for c in range(ECH):
                                v.tensor_scalar(xsq[:, c, :], xA[:, nb, c, :],
                                                gN[:, c:c + 1], None, ALU.mult)
                            v.reduce_max(maxacc[:, nb, :],
                                         bass.AP(xsq, 0, [[ECH * NBW, 128],
                                                          [1, NBW], [NBW, ECH]]),
                                         axis=AX.X, apply_absolute_value=True
                                         ).then_inc(vS, 1)
                        for nb in range(NB):
                            t_off = tSb + 6 + 4 * nb
                            ps = psT if nb == 0 else psR0
                            v.wait_ge(tS, t_off + 4)
                            for c4 in range(4):
                                ins = v.reduce_max(mxTM[:, nb * 4 + c4:nb * 4 + c4 + 1],
                                                   ps[:, c4 * 128:(c4 + 1) * 128],
                                                   axis=AX.X)
                        ins.then_inc(vS, 1)
                        v.wait_ge(dW, dWb + 32)
                        mx = ROW(R_MX)
                        v.tensor_mul(t_, mx, r2)
                        v.tensor_scalar(ROW(R_RSTD), t_, EPS, None, ALU.max)
                        v.reciprocal(t_, ROW(R_RSTD))
                        v.tensor_mul(t_, t_, r2)
                        v.tensor_scalar(t2, t_, 127.0, None, ALU.mult
                                        ).then_inc(vS, 1)
                        for nb in range(NB):
                            v.wait_ge(sS, sSb + 12)
                            nbs = slice(nb * NBW, (nb + 1) * NBW)
                            for c in range(ECH):
                                v.tensor_scalar(xA[:, nb, c, :], xA[:, nb, c, :],
                                                gN[:, c:c + 1], None, ALU.mult)
                            for c in range(ECH):
                                ins = v.tensor_mul(xA[:, nb, c, :], xA[:, nb, c, :],
                                                   c1rep[:, nbs])
                            ins.then_inc(vS, 1)
                        for nb in range(NB):
                            v.wait_ge(sS, sSb + 13 + nb)
                            nbs = slice(nb * NBW, (nb + 1) * NBW)
                            v.tensor_scalar(xA[:, nb, :, :], xA[:, nb, :, :],
                                            -MAGIC, 127.0, ALU.add, ALU.min)
                            v.tensor_scalar(xq_bf[:, :, nbs], xA[:, nb, :, :],
                                            -128.0, None, ALU.max).then_inc(vS, 1)

                    @blk.gpsimd
                    def _(g):
                        g.wait_ge(sS, sSb + 11)
                        g.dma_start(rowsc[:, :], tmp8[:, :]).then_inc(dW, 16)
                        g.wait_ge(dW, dWb + 16)
                        g.dma_start(ROW(R_MX),
                                    rowsc.rearrange("a b -> (a b)")[None, 0:NTOK]
                                    ).then_inc(dW, 16)
                    C['dW'] = dWb + 32

                C['tS'] = tSb + 17
                C['sS'] = sSb + 14
                C['vS'] = vSb + 13

                if debug:
                    with nc.Block() as blk:
                        @blk.sync
                        def _(sy):
                            sy.dma_start(dbg["xn"], xn_b[:, :, :]).then_inc(dX, 16)
                            sy.dma_start(dbg["xq"], xq_bf[:, :, :]).then_inc(dX, 16)
                            sy.dma_start(dbg["rows"], rowt[:, :]).then_inc(dX, 16)
                            C['dX'] += 48
                            sy.wait_ge(dX, C['dX'])

            _sc.__exit__(None, None, None)
            _sc = nc.named_scope("PC"); _sc.__enter__()
            NSG = 88
            NSD = 44
            NST = NSG + NSD
            pc = ExitStack()
            with pc:
                sacc = pc.enter_context(nc.sbuf_tensor("sacc", [128, 160], F32))
                pbuf = pc.enter_context(nc.sbuf_tensor("pbuf", [128, 2, 2048], F32))
                a0 = pc.enter_context(nc.sbuf_tensor("a0", [128, H, NTOK], BF16))
                qw_b = pc.enter_context(nc.sbuf_tensor("qw_b", [128, 2, ECH, 128], BF16))
                kt_b = pc.enter_context(nc.sbuf_tensor("kt_b", [128, 2, MCH, 128], BF16))
                v_b = pc.enter_context(nc.sbuf_tensor("v_b", [128, 2, MCH, 128], BF16))
                q_b = pc.enter_context(nc.sbuf_tensor("q_b", [128, NB, NBW], BF16))
                eb = pc.enter_context(nc.sbuf_tensor("eb", [128, 2, 2 * NBW], BF16))
                psQ = [pc.enter_context(nc.psum_tensor(f"psQ{i}", [128, NBW], F32))
                       for i in range(2)]
                psE2 = [pc.enter_context(nc.psum_tensor(f"psE2{i}", [128, 2 * NBW], F32))
                        for i in range(2)]
                psA = [pc.enter_context(nc.psum_tensor(f"psA{i}", [128, NBW], F32))
                       for i in range(2)]

                wq3 = wqT.rearrange("(c p) f -> p c f", p=128)
                v3 = vW.rearrange("(c p) f -> p c f", p=128)
                gw3s = gwT.rearrange("(c p) f -> p c f", p=128)
                dw3s = dwT.rearrange("(c p) e -> p c e", p=128)

                dWb, tSb, sSb, vSb = C['dW'], C['tS'], C['sS'], C['vS']
                p0b, p1b = C['dP0'], C['dP1']
                q0b, q1b = C['dQ0'], C['dQ1']

                def headload_cnt(h):
                    return (p0b if h % 2 == 0 else p1b) + 48 * (h // 2 + 1)

                def posE2(mc):
                    return 3 if mc == 0 else 2 * mc + 2

                def posA2(mc):
                    return 2 * mc + 5 if mc <= 30 else 66

                vsched = []
                for h in range(H):
                    vsched.append(("qe", h, 0))
                    vsched.append(("qe", h, 1))
                    for u in range((NST * h) // H, (NST * (h + 1)) // H):
                        vsched.append(("red", u))
                    vsched.append(("pae", h, 0))
                    vsched.append(("pae", h, 1))
                vpos = {ev: k + 1 for k, ev in enumerate(vsched)}

                with nc.Block() as blk:
                    @blk.gpsimd
                    def _(g):
                        for h in range(H):
                            if h >= 2:
                                g.wait_ge(tS, tSb + 66 * (h - 1))
                            p = h % 2
                            g.dma_start(qw_b[:, p, :, :],
                                        wq3[:, :, h * 128:(h + 1) * 128]).then_inc(dP[p], 16)
                            g.dma_start(kt_b[:, p, :, :],
                                        ktT[h * 128:(h + 1) * 128, :]
                                        .rearrange("p (c d) -> p c d", d=128)).then_inc(dP[p], 16)
                            g.dma_start(v_b[:, p, :, :],
                                        v3[:, :, h * 128:(h + 1) * 128]).then_inc(dP[p], 16)
                    C['dP0'] = p0b + 48 * (H // 2)
                    C['dP1'] = p1b + 48 * (H // 2)

                    @blk.sync
                    def _(sy):
                        for u in range(NST):
                            if u >= 2:
                                sy.wait_ge(vS, vSb + vpos[("red", u - 2)])
                            if u < NSG:
                                src = gw3s[:, :, u * 128:(u + 1) * 128]
                                dst = pbuf[:, u % 2, :].rearrange("p (c f) -> p c f", f=128)
                            else:
                                src = dw3s[:, u - NSG, :]
                                dst = pbuf[:, u % 2, :]
                            sy.dma_start(dst, src).then_inc(dQ[u % 2], 16)
                    C['dQ0'] = q0b + 16 * (NST // 2)
                    C['dQ1'] = q1b + 16 * (NST // 2)

                    @blk.tensor
                    def _(t):
                        for h in range(H):
                            tSh = tSb + 66 * h
                            sSh = sSb + 32 * h
                            p = h % 2
                            t.wait_ge(sS, sSh)
                            if h >= 1:
                                t.wait_ge(vS, vSb + vpos[("pae", h - 1, 1)])
                            t.wait_ge(dP[p], headload_cnt(h))
                            for nb in range(NB):
                                for c in range(ECH):
                                    ins = t.matmul(psQ[nb][:, :], qw_b[:, p, c, :],
                                                   xn_b[:, c, nb * NBW:(nb + 1) * NBW],
                                                   start=(c == 0), stop=(c == ECH - 1),
                                                   skip_group_check=True)
                                ins.then_inc(tS, 1)
                            for mc in range(MCH):
                                if mc >= 2:
                                    t.wait_ge(sS, sSh + mc - 1)
                                elif mc == 0:
                                    t.wait_ge(vS, vSb + vpos[("qe", h, 1)])
                                t.matmul(psE2[mc % 2][:, 0:NBW], kt_b[:, p, mc, :],
                                         q_b[:, 0, :], start=True, stop=True,
                                         skip_group_check=True)
                                t.matmul(psE2[mc % 2][:, NBW:2 * NBW], kt_b[:, p, mc, :],
                                         q_b[:, 1, :], start=True, stop=True,
                                         skip_group_check=True).then_inc(tS, 1)
                                if mc >= 1:
                                    amc = mc - 1
                                    t.wait_ge(sS, sSh + amc + 1)
                                    t.matmul(psA[0][:, :], v_b[:, p, amc, :],
                                             eb[:, amc % 2, 0:NBW], start=(amc == 0),
                                             stop=False, skip_group_check=True)
                                    t.matmul(psA[1][:, :], v_b[:, p, amc, :],
                                             eb[:, amc % 2, NBW:2 * NBW], start=(amc == 0),
                                             stop=False, skip_group_check=True
                                             ).then_inc(tS, 1)
                            amc = MCH - 1
                            t.wait_ge(sS, sSh + amc + 1)
                            t.matmul(psA[0][:, :], v_b[:, p, amc, :],
                                     eb[:, amc % 2, 0:NBW], start=False, stop=True,
                                     skip_group_check=True)
                            t.matmul(psA[1][:, :], v_b[:, p, amc, :],
                                     eb[:, amc % 2, NBW:2 * NBW], start=False, stop=True,
                                     skip_group_check=True).then_inc(tS, 1)
                    C['tS'] = tSb + 66 * H

                    @blk.scalar
                    def _(s):
                        for h in range(H):
                            tSh = tSb + 66 * h
                            for mc in range(MCH):
                                s.wait_ge(tS, tSh + posE2(mc))
                                s.activation(eb[:, mc % 2, :], psE2[mc % 2][:, :],
                                             AF.Silu, scale=SQRT128).then_inc(sS, 1)
                    C['sS'] = sSb + 32 * H

                    @blk.vector
                    def _(v):
                        for ev in vsched:
                            if ev[0] == "qe":
                                _, h, nb = ev
                                v.wait_ge(tS, tSb + 66 * h + nb + 1)
                                v.tensor_copy(q_b[:, nb, :], psQ[nb][:, :]).then_inc(vS, 1)
                            elif ev[0] == "pae":
                                _, h, nb = ev
                                v.wait_ge(tS, tSb + 66 * h + 66)
                                v.tensor_scalar(a0[:, h, nb * NBW:(nb + 1) * NBW],
                                                psA[nb][:, :], SQRT128, None, ALU.mult
                                                ).then_inc(vS, 1)
                            else:
                                u = ev[1]
                                v.wait_ge(dQ[u % 2],
                                          (q0b if u % 2 == 0 else q1b) + 16 * (u // 2 + 1))
                                col = u if u < NSG else 96 + (u - NSG)
                                v.reduce_sum(sacc[:, col:col + 1], pbuf[:, u % 2, :],
                                             axis=AX.X, apply_absolute_value=True
                                             ).then_inc(vS, 1)
                    C['vS'] = vSb + len(vsched)

                gw_b = pc.enter_context(nc.sbuf_tensor("gw_b", [128, 2, ECH, 128], BF16))
                gt = pc.enter_context(nc.sbuf_tensor("gt", [128, 2, NBW], BF16))
                wg3 = wgT.rearrange("(c p) f -> p c f", p=128)
                dWb, tSb, sSb, vSb = C['dW'], C['tS'], C['sS'], C['vS']
                p0b, p1b = C['dP0'], C['dP1']
                with nc.Block() as blk:
                    @blk.gpsimd
                    def _(g):
                        for cb in range(ECH):
                            if cb >= 2:
                                g.wait_ge(tS, tSb + 2 * (cb - 1))
                            g.dma_start(gw_b[:, cb % 2, :, :],
                                        wg3[:, :, cb * 128:(cb + 1) * 128]
                                        ).then_inc(dP[cb % 2], 16)
                    C['dP0'] = p0b + 16 * (ECH // 2)
                    C['dP1'] = p1b + 16 * (ECH // 2)

                    @blk.tensor
                    def _(t):
                        for cb in range(ECH):
                            t.wait_ge(dP[cb % 2],
                                      (p0b if cb % 2 == 0 else p1b) + 16 * (cb // 2 + 1))
                            if cb >= 1:
                                t.wait_ge(sS, sSb + 2 * cb)
                            for nb in range(NB):
                                for c in range(ECH):
                                    ins = t.matmul(psQ[nb][:, :], gw_b[:, cb % 2, c, :],
                                                   xn_b[:, c, nb * NBW:(nb + 1) * NBW],
                                                   start=(c == 0), stop=(c == ECH - 1),
                                                   skip_group_check=True)
                                ins.then_inc(tS, 1)
                    C['tS'] = tSb + 2 * ECH

                    @blk.scalar
                    def _(s):
                        for cb in range(ECH):
                            for nb in range(NB):
                                s.wait_ge(tS, tSb + 2 * cb + nb + 1)
                                if cb >= 1:
                                    s.wait_ge(vS, vSb + 2 * (cb - 1) + nb + 1)
                                s.activation(gt[:, nb, :], psQ[nb][:, :], AF.Silu
                                             ).then_inc(sS, 1)
                    C['sS'] = sSb + 2 * ECH

                    @blk.vector
                    def _(v):
                        for cb in range(ECH):
                            for nb in range(NB):
                                v.wait_ge(sS, sSb + 2 * cb + nb + 1)
                                sl = a0[:, cb, nb * NBW:(nb + 1) * NBW]
                                v.tensor_mul(sl, sl, gt[:, nb, :]).then_inc(vS, 1)
                    C['vS'] = vSb + 2 * ECH

                if debug:
                    with nc.Block() as blk:
                        @blk.sync
                        def _(sy):
                            sy.dma_start(dbg["a0"], a0[:, :, :]).then_inc(dX, 16)
                            C['dX'] += 16
                            sy.wait_ge(dX, C['dX'])

                wo_b = gw_b
                att_b = gt
                wo3 = woT.rearrange("(c p) f -> p c f", p=128)
                dWb, tSb, sSb = C['dW'], C['tS'], C['sS']
                dSb_base = [C['dSa'], C['dSb']]
                dS_nb = [dSa, dSb]
                p0b, p1b = C['dP0'], C['dP1']

                emit_seq = []
                wo_pos = {}
                wo_load_pos = {}
                nload = 0
                nstore = 0
                for cb in range(ECH):
                    nload += 1
                    wo_load_pos[cb] = nload
                    emit_seq.append(("L", cb))
                    if cb >= 1:
                        for nb in range(NB):
                            nstore += 1
                            wo_pos[(cb - 1, nb)] = nstore
                            emit_seq.append(("S", (cb - 1, nb)))
                for nb in range(NB):
                    nstore += 1
                    wo_pos[(ECH - 1, nb)] = nstore
                    emit_seq.append(("S", (ECH - 1, nb)))

                with nc.Block() as blk:
                    @blk.gpsimd
                    def _(g):
                        for kind, idx in emit_seq:
                            if kind == "L":
                                cb = idx
                                if cb >= 2:
                                    g.wait_ge(tS, tSb + 2 * (cb - 1))
                                g.dma_start(wo_b[:, cb % 2, :, :],
                                            wo3[:, :, cb * 128:(cb + 1) * 128]
                                            ).then_inc(dP[cb % 2], 16)
                            else:
                                cb, nb = idx
                                g.wait_ge(sS, sSb + 2 * cb + nb + 1)
                                g.dma_start(attS[:, cb, nb * NBW:(nb + 1) * NBW],
                                            att_b[:, nb, :]).then_inc(dS_nb[nb], 16)
                    C['dP0'] = p0b + 16 * (ECH // 2)
                    C['dP1'] = p1b + 16 * (ECH // 2)
                    C['dSa'] = dSb_base[0] + 16 * ECH
                    C['dSb'] = dSb_base[1] + 16 * ECH

                    @blk.tensor
                    def _(t):
                        for cb in range(ECH):
                            t.wait_ge(dP[cb % 2],
                                      (p0b if cb % 2 == 0 else p1b) + 16 * (cb // 2 + 1))
                            if cb >= 1:
                                t.wait_ge(sS, sSb + 2 * cb)
                            for nb in range(NB):
                                for vc in range(ECH):
                                    ins = t.matmul(psQ[nb][:, :], wo_b[:, cb % 2, vc, :],
                                                   a0[:, vc, nb * NBW:(nb + 1) * NBW],
                                                   start=(vc == 0), stop=(vc == ECH - 1),
                                                   skip_group_check=True)
                                ins.then_inc(tS, 1)
                    C['tS'] = tSb + 2 * ECH

                    @blk.scalar
                    def _(s):
                        for cb in range(ECH):
                            for nb in range(NB):
                                s.wait_ge(tS, tSb + 2 * cb + nb + 1)
                                if cb >= 1:
                                    s.wait_ge(dS_nb[nb], dSb_base[nb] + 16 * cb)
                                s.copy(att_b[:, nb, :], psQ[nb][:, :]).then_inc(sS, 1)
                    C['sS'] = sSb + 2 * ECH

                with nc.Block() as blk:
                    @blk.gpsimd
                    def _(g):
                        g.wait_ge(dSa, C['dSa'])
                        g.wait_ge(dSb, C['dSb'])

                if True:
                    psW = psQ[0]
                    psR128 = psQ[1]

                    with nc.Block() as blk:
                        @blk.vector
                        def _(v):
                            v.reduce_sum(sw128[:, 0:1], sacc[:, 0:NSG], axis=AX.X)
                            v.reduce_sum(sw128[:, 1:2], sacc[:, 96:96 + NSD], axis=AX.X)

                    with nc.Block() as blk:
                        @blk.tensor
                        def _(t):
                            t.matmul(psW[0:1, 0:2], ones[:, 0:1], sw128[:, 0:2],
                                     start=True, stop=True)

                    with nc.Block() as blk:
                        @blk.scalar
                        def _(s):
                            s.copy(wstat[0:1, 0:2], psW[0:1, 0:2])

                    with nc.Block() as blk:
                        @blk.vector
                        def _(v):
                            v.tensor_scalar(wstat[0:1, 2:3], wstat[0:1, 0:1],
                                            1.0 / GW_NUMEL, EPS, ALU.mult, ALU.max)
                            v.tensor_scalar(wstat[0:1, 4:5], wstat[0:1, 1:2],
                                            1.0 / DW_NUMEL, EPS, ALU.mult, ALU.max)
                            v.reciprocal(wstat[0:1, 6:7], wstat[0:1, 2:3])
                            v.reciprocal(wstat[0:1, 7:8], wstat[0:1, 4:5])

                    with nc.Block() as blk:
                        @blk.tensor
                        def _(t):
                            t.matmul(psR128[:, 0:2], ones[0:1, :], wstat[0:1, 6:8],
                                     start=True, stop=True)

                    with nc.Block() as blk:
                        @blk.scalar
                        def _(s):
                            s.copy(sw128[:, 0:2], psR128[:, 0:2])

                if debug:
                    with nc.Block() as blk:
                        @blk.sync
                        def _(sy):
                            sy.dma_start(dbg["att"], attS).then_inc(dX, 16)
                            sy.dma_start(dbg["wstat"], wstat[0:1, :]).then_inc(dX, 16)
                            C['dX'] += 32
                            sy.wait_ge(dX, C['dX'])

        _sc.__exit__(None, None, None)
        _sc = nc.named_scope("PD01"); _sc.__enter__()
        pd0 = ExitStack()
        with pd0:
            h_b = pd0.enter_context(nc.sbuf_tensor("h_b", [128, ICH, NTOK], BF16))
            c2rep = pd0.enter_context(nc.sbuf_tensor("c2rep", [128, NTOK], F32))
            d2rep = pd0.enter_context(nc.sbuf_tensor("d2rep", [128, NTOK], F32))

            pd1 = ExitStack()
            with pd1:
                cgrep = d2rep
                gwf = pd1.enter_context(nc.sbuf_tensor("gwf", [128, 2, ECH, 128], F32))
                wqg = pd1.enter_context(nc.sbuf_tensor("wqg", [128, 2, ECH, 128], BF16))
                tg = pd1.enter_context(nc.sbuf_tensor("tg", [128, NBW], F32))
                tu = pd1.enter_context(nc.sbuf_tensor("tu", [128, NBW], F32))
                sg = pd1.enter_context(nc.sbuf_tensor("sg", [128, NBW], F32))
                tmv = tg
                mxv = pd1.enter_context(nc.sbuf_tensor("mxv", [128, 2, NBW], F32))
                mnv = pd1.enter_context(nc.sbuf_tensor("mnv", [128, 2, NBW], F32))
                mxTM2 = pd1.enter_context(nc.sbuf_tensor("mxTM2", [128, 8], F32))
                hsq = pd1.enter_context(nc.sbuf_tensor("hsq", [128, 2, 2, NBW], F32))

                cgs = ExitStack()
                with cgs:
                    psCa = cgs.enter_context(nc.psum_tensor("psCa", [128, NBW], F32))
                    psCb = cgs.enter_context(nc.psum_tensor("psCb", [128, NBW], F32))
                    with nc.Block() as blk:
                        @blk.vector
                        def _(v):
                            v.tensor_scalar(ROW(R_T3), ROW(R_RSTD),
                                            wstat[0:1, 2:3], 1.0 / 127.0,
                                            ALU.mult, ALU.mult)

                    with nc.Block() as blk:
                        @blk.tensor
                        def _(t):
                            t.matmul(psCa[:, :], ones[0:1, :], ROW(R_T3, slice(0, NBW)),
                                     start=True, stop=True)
                            t.matmul(psCb[:, :], ones[0:1, :], ROW(R_T3, slice(NBW, NTOK)),
                                     start=True, stop=True)

                    with nc.Block() as blk:
                        @blk.scalar
                        def _(s):
                            s.copy(cgrep[:, 0:NBW], psCa[:, :])
                            s.copy(cgrep[:, NBW:NTOK], psCb[:, :])

                gw3b = gwT.rearrange("(c p) f -> p c f", p=128)
                dXb, tSb, sSb, vSb = C['dX'], C['tS'], C['sS'], C['vS']
                NJ = ICH
                NS = 2 * NJ
                q0b, q1b = C['dQ0'], C['dQ1']

                vsched = [("vq", 0), ("vq", 1)]
                for j in range(NJ):
                    for s2 in (2 * j + 2, 2 * j + 3):
                        if s2 < NS:
                            vsched.append(("vq", s2))
                    for nb in range(NB):
                        vsched.append(("v1", j, nb))
                        vsched.append(("v2", j, nb))
                vpos = {ev: k + 1 for k, ev in enumerate(vsched)}

                def vq_cnt(s):
                    return vSb + vpos[("vq", s)]

                def v1_cnt(j, nb):
                    return vSb + vpos[("v1", j, nb)]

                def v2_cnt(j, nb):
                    return vSb + vpos[("v2", j, nb)]

                ssched = [("p3", 0), ("p3", 1)]
                for j in range(NJ):
                    for s2 in (2 * j + 2, 2 * j + 3):
                        if s2 < NS:
                            ssched.append(("p3", s2))
                    for nb in range(NB):
                        ssched.append(("silu", j, nb))
                spos = {ev: k + 1 for k, ev in enumerate(ssched)}

                def p3_cnt(s):
                    return sSb + spos[("p3", s)]

                def silu_cnt(j, nb):
                    return sSb + spos[("silu", j, nb)]

                def t1_cnt(s, nb):
                    j, hf = s // 2, s % 2
                    return tSb + 4 * j + (1 + nb if hf == 0 else 3 + nb)

                def fcol(s):
                    j, hf = s // 2, s % 2
                    return (j if hf == 0 else ICH + j) * 128

                lps = ExitStack()
                with lps:
                    psG = [[lps.enter_context(nc.psum_tensor(f"psG{p}{n}", [128, NBW], F32))
                            for n in range(2)] for p in range(2)]
                    psU = [[lps.enter_context(nc.psum_tensor(f"psU{p}{n}", [128, NBW], F32))
                            for n in range(2)] for p in range(2)]

                    with nc.Block() as blk:
                        @blk.sync
                        def _(sy):
                            for s_ in range(NS):
                                if s_ >= 2:
                                    sy.wait_ge(sS, p3_cnt(s_ - 2))
                                sy.dma_start(gwf[:, s_ % 2, :, :],
                                             gw3b[:, :, fcol(s_):fcol(s_) + 128]
                                             ).then_inc(dQ[s_ % 2], 16)
                        C['dQ0'] = q0b + 16 * (NS // 2)
                        C['dQ1'] = q1b + 16 * (NS // 2)

                        @blk.vector
                        def _(v):
                            v.memset(mxv[:, 0, :], 0.0)
                            v.memset(mxv[:, 1, :], 0.0)
                            v.memset(mnv[:, 0, :], 0.0)
                            v.memset(mnv[:, 1, :], 0.0)
                            for ev in vsched:
                                if ev[0] == "vq":
                                    s_ = ev[1]
                                    v.wait_ge(dQ[s_ % 2],
                                              (q0b if s_ % 2 == 0 else q1b)
                                              + 16 * (s_ // 2 + 1))
                                    v.tensor_scalar(gwf[:, s_ % 2, :, :],
                                                    gwf[:, s_ % 2, :, :],
                                                    sw128[:, 0:1], 1.0, ALU.mult, ALU.min)
                                    v.tensor_scalar(gwf[:, s_ % 2, :, :],
                                                    gwf[:, s_ % 2, :, :],
                                                    -1.0, MAGIC, ALU.max, ALU.add
                                                    ).then_inc(vS, 1)
                                elif ev[0] == "v1":
                                    _, j, nb = ev
                                    jp = j % 2
                                    v.wait_ge(tS, t1_cnt(2 * j + 1, nb))
                                    v.tensor_mul(tg[:, :], psG[jp][nb][:, :],
                                                 cgrep[:, nb * NBW:(nb + 1) * NBW])
                                    v.tensor_mul(tu[:, :], psU[jp][nb][:, :],
                                                 cgrep[:, nb * NBW:(nb + 1) * NBW]
                                                 ).then_inc(vS, 1)
                                else:
                                    _, j, nb = ev
                                    v.wait_ge(sS, silu_cnt(j, nb))
                                    hsl = h_b[:, j, nb * NBW:(nb + 1) * NBW]
                                    v.tensor_mul(hsl, sg[:, :], tu[:, :])
                                    v.tensor_scalar(tmv[:, :], hsl, dN[:, j:j + 1],
                                                    None, ALU.mult)
                                    v.tensor_tensor(mxv[:, nb, :], mxv[:, nb, :],
                                                    tmv[:, :], ALU.max)
                                    v.tensor_tensor(mnv[:, nb, :], mnv[:, nb, :],
                                                    tmv[:, :], ALU.min
                                                    ).then_inc(vS, 1)
                        C['vS'] = vSb + len(vsched)

                        @blk.scalar
                        def _(s):
                            for ev in ssched:
                                if ev[0] == "p3":
                                    s_ = ev[1]
                                    s.wait_ge(vS, vq_cnt(s_))
                                    if s_ >= 2:
                                        s.wait_ge(tS, t1_cnt(s_ - 2, 1))
                                    s.activation(wqg[:, s_ % 2, :, :],
                                                 gwf[:, s_ % 2, :, :],
                                                 AF.Copy, bias=-MAGIC).then_inc(sS, 1)
                                else:
                                    _, j, nb = ev
                                    s.wait_ge(vS, v1_cnt(j, nb))
                                    if nb == 0 and j >= 1:
                                        s.wait_ge(vS, v2_cnt(j - 1, 1))
                                    s.activation(sg[:, :], tg[:, :], AF.Silu
                                                 ).then_inc(sS, 1)
                        C['sS'] = sSb + len(ssched)

                        @blk.tensor
                        def _(t):
                            for s_ in range(NS):
                                j, hf = s_ // 2, s_ % 2
                                jp = j % 2
                                t.wait_ge(sS, p3_cnt(s_))
                                ps = psG[jp] if hf == 0 else psU[jp]
                                for nb in range(NB):
                                    if j >= 2:
                                        t.wait_ge(vS, v1_cnt(j - 2, nb))
                                    for c in range(ECH):
                                        ins = t.matmul(ps[nb][:, :], wqg[:, s_ % 2, c, :],
                                                       xq_bf[:, c, nb * NBW:(nb + 1) * NBW],
                                                       start=(c == 0), stop=(c == ECH - 1),
                                                       skip_group_check=True)
                                    ins.then_inc(tS, 1)
                        C['tS'] = tSb + 4 * NJ

                _sc.__exit__(None, None, None)
                _sc = nc.named_scope("PD1fin"); _sc.__enter__()
                fin = ExitStack()
                with fin:
                    psSh = [fin.enter_context(nc.psum_tensor(f"psSh{i}", [1, NBW], F32))
                            for i in range(2)]
                    psFa = fin.enter_context(nc.psum_tensor("psFa", [128, NBW], F32))
                    psFb = fin.enter_context(nc.psum_tensor("psFb", [128, NBW], F32))
                    psFc = fin.enter_context(nc.psum_tensor("psFc", [8, 128], F32))
                    psRa = fin.enter_context(nc.psum_tensor("psRa2", [128, NBW], F32))
                    psRb = fin.enter_context(nc.psum_tensor("psRb2", [128, NBW], F32))

                    sSb2, tSb2 = C['sS'], C['tS']
                    with nc.Block() as blk:
                        @blk.scalar
                        def _(s):
                            for j in range(ICH):
                                for nb in range(NB):
                                    if j >= 2:
                                        s.wait_ge(tS, tSb2 + 2 * (j - 2) + nb + 1)
                                    s.activation(hsq[:, j % 2, nb, :],
                                                 h_b[:, j, nb * NBW:(nb + 1) * NBW],
                                                 AF.Square).then_inc(sS, 1)

                        @blk.tensor
                        def _(t):
                            for j in range(ICH):
                                for nb in range(NB):
                                    t.wait_ge(sS, sSb2 + 2 * j + nb + 1)
                                    t.matmul(psSh[nb][:, :], ones[:, 0:1],
                                             hsq[:, j % 2, nb, :],
                                             start=(j == 0), stop=(j == ICH - 1),
                                             skip_group_check=True).then_inc(tS, 1)
                    C['sS'] = sSb2 + 2 * ICH
                    C['tS'] = tSb2 + 2 * ICH

                    with nc.Block() as blk:
                        @blk.scalar
                        def _(s):
                            s.copy(ROW(R_T, slice(0, NBW)), psSh[0][:, :])
                            s.copy(ROW(R_T, slice(NBW, NTOK)), psSh[1][:, :])

                    with nc.Block() as blk:
                        @blk.vector
                        def _(v):
                            for nb in range(NB):
                                v.tensor_scalar(mnv[:, nb, :], mnv[:, nb, :], -1.0,
                                                None, ALU.mult)
                                v.tensor_tensor(mxv[:, nb, :], mxv[:, nb, :],
                                                mnv[:, nb, :], ALU.max)

                    with nc.Block() as blk:
                        @blk.tensor
                        def _(t):
                            for nb in range(NB):
                                ps = psFa if nb == 0 else psFb
                                for c4 in range(4):
                                    t.transpose(ps[:, c4 * 128:(c4 + 1) * 128],
                                                mxv[:, nb, c4 * 128:(c4 + 1) * 128],
                                                ident[:, :])

                    with nc.Block() as blk:
                        @blk.vector
                        def _(v):
                            for nb in range(NB):
                                ps = psFa if nb == 0 else psFb
                                for c4 in range(4):
                                    v.reduce_max(mxTM2[:, nb * 4 + c4:nb * 4 + c4 + 1],
                                                 ps[:, c4 * 128:(c4 + 1) * 128], axis=AX.X)

                    with nc.Block() as blk:
                        @blk.tensor
                        def _(t):
                            t.transpose(psFc[:, :], mxTM2[:, 0:8], ident[:, :])

                    with nc.Block() as blk:
                        @blk.scalar
                        def _(s):
                            s.copy(tmp8[:, :], psFc[:, :])

                    with nc.Block() as blk:
                        @blk.gpsimd
                        def _(g):
                            g.dma_start(rowsc[:, :], tmp8[:, :]).then_inc(dW, 16)
                            C['dW'] += 16
                            g.wait_ge(dW, C['dW'])
                            g.dma_start(ROW(R_MX),
                                        rowsc.rearrange("a b -> (a b)")[None, 0:NTOK]
                                        ).then_inc(dW, 16)
                            C['dW'] += 16
                            g.wait_ge(dW, C['dW'])

                    with nc.Block() as blk:
                        @blk.vector
                        def _(v):
                            hs = ROW(R_T)
                            v.tensor_scalar(hs, hs, 1.0 / 5632.0, EPS, ALU.mult, ALU.add)

                    with nc.Block() as blk:
                        @blk.scalar
                        def _(s):
                            s.activation(ROW(R_T2), ROW(R_T), AF.Sqrt)

                    with nc.Block() as blk:
                        @blk.vector
                        def _(v):
                            w = ROW(R_T)
                            r = ROW(R_T2)
                            t3 = ROW(R_T3)
                            mx = ROW(R_MX)
                            v.reciprocal(r, r)
                            v.tensor_mul(t3, r, r)
                            v.tensor_mul(t3, t3, w)
                            v.tensor_scalar(t3, t3, -0.5, 1.5, ALU.mult, ALU.add)
                            v.tensor_mul(r, r, t3)
                            v.tensor_mul(t3, mx, r)
                            v.tensor_scalar(t3, t3, EPS, None, ALU.max)
                            v.reciprocal(w, t3)
                            v.tensor_mul(w, w, r)
                            v.tensor_scalar(ROW(R_T), w, 127.0, None, ALU.mult)
                            v.tensor_scalar(t3, t3, wstat[0:1, 4:5], 1.0 / 127.0,
                                            ALU.mult, ALU.mult)

                    with nc.Block() as blk:
                        @blk.tensor
                        def _(t):
                            t.matmul(psRa[:, :], ones[0:1, :], ROW(R_T, slice(0, NBW)),
                                     start=True, stop=True)
                            t.matmul(psRb[:, :], ones[0:1, :], ROW(R_T, slice(NBW, NTOK)),
                                     start=True, stop=True)
                            t.matmul(psFa[:, :], ones[0:1, :], ROW(R_T3, slice(0, NBW)),
                                     start=True, stop=True)
                            t.matmul(psFb[:, :], ones[0:1, :], ROW(R_T3, slice(NBW, NTOK)),
                                     start=True, stop=True)

                    with nc.Block() as blk:
                        @blk.scalar
                        def _(s):
                            s.copy(c2rep[:, 0:NBW], psRa[:, :])
                            s.copy(c2rep[:, NBW:NTOK], psRb[:, :])
                            s.copy(d2rep[:, 0:NBW], psFa[:, :])
                            s.copy(d2rep[:, NBW:NTOK], psFb[:, :])

                if debug:
                    with nc.Block() as blk:
                        @blk.sync
                        def _(sy):
                            sy.dma_start(dbg["h"], h_b[:, :, :]).then_inc(dX, 16)
                            sy.dma_start(dbg["wq"], wqg[:, :, :, :]).then_inc(dX, 16)
                            sy.dma_start(dbg["gwf"], gwf[:, :, :, :]).then_inc(dX, 16)
                            sy.dma_start(dbg["sw"], sw128[:, :]).then_inc(dX, 16)
                            C['dX'] += 64
                            sy.wait_ge(dX, C['dX'])

            sxq.close()

            _sc.__exit__(None, None, None)
            _sc = nc.named_scope("PD2"); _sc.__enter__()
            pd2 = ExitStack()
            with pd2:
                hq_b = pd2.enter_context(nc.sbuf_tensor("hq_b", [128, ICH, NBW], BF16))
                tqa = pd2.enter_context(nc.sbuf_tensor("tqa", [128, 3, NBW], F32))
                tqb = pd2.enter_context(nc.sbuf_tensor("tqb", [128, 2, NBW], F32))
                dwf = pd2.enter_context(nc.sbuf_tensor("dwf", [128, 3, 8, 128], F32))
                dwq = pd2.enter_context(nc.sbuf_tensor("dwq", [128, 3, 8, 128], BF16))
                xtile = pd2.enter_context(nc.sbuf_tensor("xtile", [128, 2, NBW], F32))
                atile = pd2.enter_context(nc.sbuf_tensor("atile", [128, 2, NBW], BF16))
                otile = pd2.enter_context(nc.sbuf_tensor("otile", [128, 2, NBW], F32))
                psD = [pd2.enter_context(nc.psum_tensor(f"psD{i}", [128, NBW], F32))
                       for i in range(8)]

                for ng in range(NB):
                    ngs = slice(ng * NBW, (ng + 1) * NBW)

                    vb, sb = C['vS'], C['sS']
                    with nc.Block() as blk:
                        @blk.vector
                        def _(v, vb=vb, sb=sb, ngs=ngs):
                            for i in range(ICH + 2):
                                if i < ICH:
                                    p = i % 3
                                    if i >= 3:
                                        v.wait_ge(sS, sb + i - 2)
                                    v.tensor_scalar(tqa[:, p, :], h_b[:, i, ngs],
                                                    dN[:, i:i + 1], None, ALU.mult)
                                    v.tensor_mul(tqa[:, p, :], tqa[:, p, :],
                                                 c2rep[:, ngs]).then_inc(vS, 1)
                                if i >= 2:
                                    ib = i - 2
                                    pb_ = ib % 2
                                    v.wait_ge(sS, sb + ib + 1)
                                    v.tensor_scalar(tqb[:, pb_, :], tqb[:, pb_, :],
                                                    -MAGIC, 127.0, ALU.add, ALU.min)
                                    v.tensor_scalar(hq_b[:, ib, :], tqb[:, pb_, :],
                                                    -128.0, None, ALU.max).then_inc(vS, 1)

                        @blk.scalar
                        def _(s, vb=vb, sb=sb):
                            apos = {}
                            k = 0
                            for i in range(ICH + 2):
                                if i < ICH:
                                    k += 1
                                    apos[i] = k
                                if i >= 2:
                                    k += 1
                            for i in range(ICH):
                                s.wait_ge(vS, vb + apos[i])
                                s.activation(tqb[:, i % 2, :], tqa[:, i % 3, :], AF.Copy,
                                             bias=MAGIC).then_inc(sS, 1)
                    C['vS'] = vb + 2 * ICH
                    C['sS'] = sb + ICH

                    for eg in range(2):
                        qb4 = [C[f'dQ{i}'] for i in range(4)]
                        vb, sb, gb, tb = C['vS'], C['sS'], C['gS'], C['tS']
                        with nc.Block() as blk:
                            @blk.sync
                            def _(sy, sb=sb, eg=eg):
                                for i in range(ICH):
                                    if i >= 3:
                                        sy.wait_ge(sS, sb + i - 2)
                                    src = dwT[i * 128:(i + 1) * 128,
                                              eg * 1024:(eg + 1) * 1024].rearrange(
                                        "p (c e) -> p c e", e=128)
                                    sy.dma_start(dwf[:, i % 3, :, :], src
                                                 ).then_inc(dQ[i % 3], 16)
                            for i in range(3):
                                C[f'dQ{i}'] = qb4[i] + 16 * len(range(i, ICH, 3))

                            @blk.vector
                            def _(v, qb4=qb4):
                                for i in range(ICH):
                                    v.wait_ge(dQ[i % 3], qb4[i % 3] + 16 * (i // 3 + 1))
                                    v.tensor_scalar(dwf[:, i % 3, :, :], dwf[:, i % 3, :, :],
                                                    sw128[:, 1:2], 1.0, ALU.mult, ALU.min)
                                    v.tensor_scalar(dwf[:, i % 3, :, :], dwf[:, i % 3, :, :],
                                                    -1.0, MAGIC, ALU.max, ALU.add
                                                    ).then_inc(vS, 1)
                            C['vS'] = vb + ICH

                            @blk.scalar
                            def _(s, vb=vb, tb=tb):
                                for i in range(ICH):
                                    s.wait_ge(vS, vb + i + 1)
                                    if i >= 3:
                                        s.wait_ge(tS, tb + i - 2)
                                    s.activation(dwq[:, i % 3, :, :], dwf[:, i % 3, :, :],
                                                 AF.Copy, bias=-MAGIC).then_inc(sS, 1)
                            C['sS'] = sb + ICH

                            @blk.tensor
                            def _(t, sb=sb):
                                for i in range(ICH):
                                    t.wait_ge(sS, sb + i + 1)
                                    for ec in range(8):
                                        ins = t.matmul(psD[ec][:, :], dwq[:, i % 3, ec, :],
                                                       hq_b[:, i, :], start=(i == 0),
                                                       stop=(i == ICH - 1),
                                                       skip_group_check=True)
                                    ins.then_inc(tS, 1)
                            C['tS'] = tb + ICH

                        q0b, q1b = C['dQ0'], C['dQ1']
                        x0b, x1b = C['dXs0'], C['dXs1']
                        vb = C['vS']
                        ev_seq = []
                        for ec in range(8):
                            ev_seq.append(("L", ec))
                            if ec >= 2:
                                ev_seq.append(("S", ec - 2))
                        ev_seq += [("S", 6), ("S", 7)]
                        with nc.Block() as blk:
                            @blk.sync
                            def _(sy, vb=vb, ng=ng, eg=eg, ngs=ngs, ev_seq=ev_seq):
                                for kind, ec in ev_seq:
                                    row = (eg * 8 + ec) * 128
                                    if kind == "L":
                                        if ec >= 2:
                                            sy.wait_ge(vS, vb + ec - 1)
                                        sy.dma_start(xtile[:, ec % 2, :],
                                                     xT[row:row + 128, ngs]
                                                     ).then_inc(dQ[ec % 2], 16)
                                        sy.dma_start(atile[:, ec % 2, :],
                                                     attS[:, eg * 8 + ec, ngs]
                                                     ).then_inc(dQ[ec % 2], 16)
                                    else:
                                        sy.wait_ge(vS, vb + ec + 1)
                                        sy.dma_start(outT[row:row + 128, ngs],
                                                     otile[:, ec % 2, :]
                                                     ).then_inc(dXs[ec % 2], 16)
                            C['dQ0'] = q0b + 32 * 4
                            C['dQ1'] = q1b + 32 * 4
                            C['dXs0'] = x0b + 16 * 4
                            C['dXs1'] = x1b + 16 * 4

                            @blk.vector
                            def _(v, q0b=q0b, q1b=q1b, x0b=x0b, x1b=x1b, vb=vb, ngs=ngs):
                                for ec in range(8):
                                    p = ec % 2
                                    v.wait_ge(dQ[p], (q0b if p == 0 else q1b)
                                              + 32 * (ec // 2 + 1))
                                    if ec >= 2:
                                        v.wait_ge(dXs[p], (x0b if p == 0 else x1b)
                                                  + 16 * (ec // 2))
                                    v.tensor_mul(otile[:, p, :], psD[ec][:, :],
                                                 d2rep[:, ngs])
                                    v.tensor_add(otile[:, p, :], otile[:, p, :],
                                                 atile[:, p, :])
                                    v.tensor_add(otile[:, p, :], otile[:, p, :],
                                                 xtile[:, p, :]).then_inc(vS, 1)
                            C['vS'] = vb + 8

        _sc.__exit__(None, None, None)
        with nc.Block() as blk:
            @blk.sync
            def _(sy):
                sy.wait_ge(dX, C['dX'])
                sy.wait_ge(dSa, C['dSa'])
                sy.wait_ge(dSb, C['dSb'])
                sy.wait_ge(dQ[0], C['dQ0'])
                sy.wait_ge(dQ[1], C['dQ1'])
                sy.wait_ge(dQ[2], C['dQ2'])
                sy.wait_ge(dQ[3], C['dQ3'])
                sy.wait_ge(dP[0], C['dP0'])
                sy.wait_ge(dP[1], C['dP1'])
                sy.wait_ge(dXs[0], C['dXs0'])
                sy.wait_ge(dXs[1], C['dXs1'])

            @blk.gpsimd
            def _(g):
                g.wait_ge(dW, C['dW'])

    return nc


_CACHED = {}


def _get_nc(debug):
    key = bool(debug)
    if key not in _CACHED:
        _CACHED[key] = build_core_program(debug=key)
    return _CACHED[key]


def kernel(L0_attn_x, L0_attn_Wq, L0_attn_k_weight, L0_attn_v_weight, L0_attn_Wg,
           L0_attn_Wout, L0_ffn_gate_w, L0_ffn_down_w, L0_ffn_gate_norm_w,
           L0_ffn_down_norm_w):
    x = np.asarray(L0_attn_x)
    B, N, E = x.shape
    xt = np.ascontiguousarray(x.reshape(N, E).T)

    wq = np.ascontiguousarray(np.asarray(L0_attn_Wq).T)
    kt = np.ascontiguousarray(np.asarray(L0_attn_k_weight).T)
    vw = np.ascontiguousarray(np.asarray(L0_attn_v_weight))
    wg = np.ascontiguousarray(np.asarray(L0_attn_Wg).T)
    wo = np.ascontiguousarray(np.asarray(L0_attn_Wout).T)
    gw = np.ascontiguousarray(np.asarray(L0_ffn_gate_w).T)
    dw = np.ascontiguousarray(np.asarray(L0_ffn_down_w).T)
    gn = np.ascontiguousarray(np.asarray(L0_ffn_gate_norm_w).reshape(16, 128).T)
    dn = np.ascontiguousarray(np.asarray(L0_ffn_down_norm_w).reshape(44, 128).T)

    cns = np.zeros((128, 16), np.float32)
    cns[:, 0] = EPS
    cns[:, 1] = SQRT128
    onesf = np.ones((128, 128), np.float32)
    identf = np.eye(128, dtype=np.float32)

    nc = _get_nc(DEBUG)
    in_maps = []
    for c in range(8):
        in_maps.append(dict(
            xT=np.ascontiguousarray(xt[:, c * NTOK:(c + 1) * NTOK]),
            wqT=wq, ktT=kt, vW=vw, wgT=wg, woT=wo, gwT=gw, dwT=dw,
            cns=cns, onesf=onesf, identf=identf, gN128=gn, dN128=dn,
        ))
    res = run_bass_kernel_spmd(nc, in_maps, core_ids=list(range(8)))
    out = np.concatenate([r["outT"] for r in res.results], axis=1)
    if DEBUG:
        kernel._dbg = res.results
    return np.ascontiguousarray(out.T).reshape(B, N, E).astype(np.float32)


# version 13
# speedup vs baseline: 1.0122x; 1.0122x over previous
import numpy as np
import ml_dtypes
from contextlib import ExitStack
import concourse.bass as bass
import concourse.mybir as mybir
from concourse.bass_utils import run_bass_kernel_spmd

F32 = mybir.dt.float32
BF16 = mybir.dt.bfloat16
AF = mybir.ActivationFunctionType
ALU = mybir.AluOpType
AX = mybir.AxisListType

MAGIC = 12582912.0
SQRT128 = float(np.float32(np.sqrt(np.float32(128.0))))
EPS = 1e-5
NTOK = 1024
NBW = 512
NB = 2
ECH = 16
H = 16
MCH = 32
ICH = 44
GW_NUMEL = 11264 * 2048
DW_NUMEL = 2048 * 5632

DEBUG = False
PD1_SERIAL = False


def build_core_program(debug=False):
    nc = bass.Bass("TRN2", target_bir_lowering=False)

    xT = nc.dram_tensor("xT", [2048, NTOK], F32, kind="ExternalInput").ap()
    wqT = nc.dram_tensor("wqT", [2048, 2048], F32, kind="ExternalInput").ap()
    ktT = nc.dram_tensor("ktT", [2048, 4096], F32, kind="ExternalInput").ap()
    vW = nc.dram_tensor("vW", [4096, 2048], F32, kind="ExternalInput").ap()
    wgT = nc.dram_tensor("wgT", [2048, 2048], F32, kind="ExternalInput").ap()
    woT = nc.dram_tensor("woT", [2048, 2048], F32, kind="ExternalInput").ap()
    gwT = nc.dram_tensor("gwT", [2048, 11264], F32, kind="ExternalInput").ap()
    dwT = nc.dram_tensor("dwT", [5632, 2048], F32, kind="ExternalInput").ap()
    cnsI = nc.dram_tensor("cns", [128, 16], F32, kind="ExternalInput").ap()
    onesI = nc.dram_tensor("onesf", [128, 128], F32, kind="ExternalInput").ap()
    identI = nc.dram_tensor("identf", [128, 128], F32, kind="ExternalInput").ap()
    gNI = nc.dram_tensor("gN128", [128, 16], F32, kind="ExternalInput").ap()
    dNI = nc.dram_tensor("dN128", [128, 44], F32, kind="ExternalInput").ap()

    outT = nc.dram_tensor("outT", [2048, NTOK], F32, kind="ExternalOutput").ap()
    attS = nc.dram_tensor("attS", [128, 16, NTOK], BF16, kind="Internal").ap()
    rowsc = nc.dram_tensor("rowsc", [8, 128], F32, kind="Internal").ap()

    dbg = {}
    if debug:
        dbg["xn"] = nc.dram_tensor("dbg_xn", [128, ECH, NTOK], BF16, kind="ExternalOutput").ap()
        dbg["xq"] = nc.dram_tensor("dbg_xq", [128, ECH, NTOK], BF16, kind="ExternalOutput").ap()
        dbg["a0"] = nc.dram_tensor("dbg_a0", [128, H, NTOK], BF16, kind="ExternalOutput").ap()
        dbg["att"] = nc.dram_tensor("dbg_att", [128, 16, NTOK], BF16, kind="ExternalOutput").ap()
        dbg["h"] = nc.dram_tensor("dbg_h", [128, ICH, NTOK], BF16, kind="ExternalOutput").ap()
        dbg["rows"] = nc.dram_tensor("dbg_rows", [1, 6 * NTOK], F32, kind="ExternalOutput").ap()
        dbg["wstat"] = nc.dram_tensor("dbg_wstat", [1, 8], F32, kind="ExternalOutput").ap()
        dbg["xq2"] = nc.dram_tensor("dbg_xq2", [128, ECH, NTOK], BF16, kind="ExternalOutput").ap()
        dbg["cg"] = nc.dram_tensor("dbg_cg", [128, NTOK], F32, kind="ExternalOutput").ap()
        dbg["wq"] = nc.dram_tensor("dbg_wq", [128, 2, ECH, 128], BF16, kind="ExternalOutput").ap()
        dbg["gwf"] = nc.dram_tensor("dbg_gwf", [128, 2, ECH, 128], F32, kind="ExternalOutput").ap()
        dbg["sw"] = nc.dram_tensor("dbg_sw", [128, 2], F32, kind="ExternalOutput").ap()

    st = ExitStack()
    with st:
        cns = st.enter_context(nc.sbuf_tensor("cns_s", [128, 16], F32))
        ones = st.enter_context(nc.sbuf_tensor("ones_s", [128, 128], F32))
        ident = st.enter_context(nc.sbuf_tensor("ident_s", [128, 128], F32))
        gN = st.enter_context(nc.sbuf_tensor("gN_s", [128, 16], F32))
        dN = st.enter_context(nc.sbuf_tensor("dN_s", [128, 44], F32))
        rowt = st.enter_context(nc.sbuf_tensor("rows_s", [1, 6 * NTOK], F32))
        wstat = st.enter_context(nc.sbuf_tensor("wstat_s", [1, 8], F32))
        sw128 = st.enter_context(nc.sbuf_tensor("sw128_s", [128, 2], F32))
        tmp8 = st.enter_context(nc.sbuf_tensor("tmp8_s", [8, 128], F32))

        def ROW(r, sl=None):
            sl = sl if sl is not None else slice(0, NTOK)
            return rowt[0:1, r * NTOK + sl.start: r * NTOK + sl.stop]

        sxq = ExitStack()
        xq_bf = sxq.enter_context(nc.sbuf_tensor("xq_bf", [128, ECH, NTOK], BF16,
                                                 side="right"))

        dW = st.enter_context(nc.semaphore("dW"))
        dX = st.enter_context(nc.semaphore("dX"))
        tS = st.enter_context(nc.semaphore("tS"))
        sS = st.enter_context(nc.semaphore("sS"))
        vS = st.enter_context(nc.semaphore("vS"))
        gS = st.enter_context(nc.semaphore("gS"))
        dSa = st.enter_context(nc.semaphore("dSa"))
        dSb = st.enter_context(nc.semaphore("dSb"))
        dQ = [st.enter_context(nc.semaphore(f"dQ{i}")) for i in range(4)]
        dP = [st.enter_context(nc.semaphore(f"dP{i}")) for i in range(2)]
        dXs = [st.enter_context(nc.semaphore(f"dXs{i}")) for i in range(2)]
        C = dict(dW=0, dX=0, tS=0, sS=0, vS=0, gS=0, dSa=0, dSb=0,
                 dQ0=0, dQ1=0, dQ2=0, dQ3=0, dP0=0, dP1=0, dXs0=0, dXs1=0)

        nc.const_aps.register(F32, EPS, cns[0:1, 0:1])
        nc.const_aps.register(F32, SQRT128, cns[0:1, 1:2])

        R_M, R_RSTD, R_R2, R_T, R_T2, R_T3 = 0, 1, 2, 3, 4, 5
        R_MX = R_M

        _sc = nc.named_scope("P0"); _sc.__enter__()
        with nc.Block() as blk:
            @blk.gpsimd
            def _(g):
                g.dma_start(cns[:, :], cnsI).then_inc(dW, 16)
                g.dma_start(ones[:, :], onesI).then_inc(dW, 16)
                g.dma_start(ident[:, :], identI).then_inc(dW, 16)
                g.dma_start(gN[:, :], gNI).then_inc(dW, 16)
                g.dma_start(dN[:, :], dNI).then_inc(dW, 16)
                C['dW'] += 80
                g.wait_ge(dW, C['dW'])

        _sc.__exit__(None, None, None)
        s2 = ExitStack()
        with s2:
            xn_b = s2.enter_context(nc.sbuf_tensor("xn_b", [128, ECH, NTOK], BF16))

            _sc = nc.named_scope("PA"); _sc.__enter__()
            pa = ExitStack()
            with pa:
                xA = pa.enter_context(nc.sbuf_tensor("xA", [128, NB, ECH, NBW], F32))
                xsq = pa.enter_context(nc.sbuf_tensor("xsq", [128, ECH, NBW], F32))
                maxacc = pa.enter_context(nc.sbuf_tensor("maxacc", [128, 2, NBW], F32))
                mrep = pa.enter_context(nc.sbuf_tensor("mrep", [128, NTOK], F32))
                rrep = pa.enter_context(nc.sbuf_tensor("rrep", [128, NTOK], F32))
                c1rep = pa.enter_context(nc.sbuf_tensor("c1rep", [128, NTOK], F32))
                mxTM = pa.enter_context(nc.sbuf_tensor("mxTM", [128, 8], F32))
                psS = [pa.enter_context(nc.psum_tensor(f"psS{i}", [1, NBW], F32))
                       for i in range(2)]
                psS2 = [pa.enter_context(nc.psum_tensor(f"psS2{i}", [1, NBW], F32))
                        for i in range(2)]
                psR0 = pa.enter_context(nc.psum_tensor("psR0", [128, NBW], F32))
                psR1 = pa.enter_context(nc.psum_tensor("psR1", [128, NBW], F32))
                psT = pa.enter_context(nc.psum_tensor("psT", [128, NBW], F32))
                psT2 = pa.enter_context(nc.psum_tensor("psT2", [8, 128], F32))

                x3 = xT.rearrange("(c p) n -> p c n", p=128)
                q0b, q1b = C['dQ0'], C['dQ1']
                tSb, sSb, vSb, dWb = C['tS'], C['sS'], C['vS'], C['dW']

                with nc.Block() as blk:
                    @blk.sync
                    def _(sy):
                        for nb in range(NB):
                            sy.dma_start(xA[:, nb, :, :],
                                         x3[:, :, nb * NBW:(nb + 1) * NBW]
                                         ).then_inc(dQ[nb], 16)
                    C['dQ0'] = q0b + 16
                    C['dQ1'] = q1b + 16

                    @blk.scalar
                    def _(s):
                        for nb in range(NB):
                            s.wait_ge(dQ[nb], (q0b if nb == 0 else q1b) + 16)
                            if nb == 1:
                                s.wait_ge(tS, tSb + 1)
                            s.activation(xsq[:, :, :], xA[:, nb, :, :], AF.Square
                                         ).then_inc(sS, 1)
                        for nb in range(NB):
                            s.wait_ge(tS, tSb + nb + 1)
                            nbs = slice(nb * NBW, (nb + 1) * NBW)
                            s.copy(ROW(R_T, nbs), psS[nb][:, :])
                            s.copy(ROW(R_T2, nbs), psS2[nb][:, :]).then_inc(sS, 1)
                        s.wait_ge(vS, vSb + 1)
                        s.activation(ROW(R_RSTD), ROW(R_T), AF.Sqrt).then_inc(sS, 1)
                        s.wait_ge(vS, vSb + 2)
                        s.activation(ROW(R_T), ROW(R_R2), AF.Sqrt).then_inc(sS, 1)
                        s.wait_ge(tS, tSb + 4)
                        s.copy(mrep[:, 0:NBW], psR0[:, :])
                        s.copy(mrep[:, NBW:NTOK], psR1[:, :]).then_inc(sS, 1)
                        s.wait_ge(tS, tSb + 6)
                        s.copy(rrep[:, 0:NBW], psR0[:, :])
                        s.copy(rrep[:, NBW:NTOK], psR1[:, :]).then_inc(sS, 1)
                        for nb in range(NB):
                            s.wait_ge(vS, vSb + 3 + 2 * nb + 1)
                            s.copy(xn_b[:, :, nb * NBW:(nb + 1) * NBW], xA[:, nb, :, :]
                                   ).then_inc(sS, 1)
                        s.wait_ge(tS, tSb + 6 + 8 + 1)
                        s.copy(tmp8[:, :], psT2[:, :]).then_inc(sS, 1)
                        s.wait_ge(tS, tSb + 17)
                        s.copy(c1rep[:, 0:NBW], psR0[:, :])
                        s.copy(c1rep[:, NBW:NTOK], psR1[:, :]).then_inc(sS, 1)
                        for nb in range(NB):
                            s.wait_ge(vS, vSb + 10 + nb)
                            s.activation(xA[:, nb, :, :], xA[:, nb, :, :],
                                         AF.Copy, bias=MAGIC).then_inc(sS, 1)

                    @blk.tensor
                    def _(t):
                        for nb in range(NB):
                            t.wait_ge(sS, sSb + nb + 1)
                            for c in range(ECH):
                                t.matmul(psS[nb][:, :], ones[:, 0:1], xA[:, nb, c, :],
                                         start=(c == 0), stop=(c == ECH - 1),
                                         skip_group_check=True)
                            for c in range(ECH):
                                ins = t.matmul(psS2[nb][:, :], ones[:, 0:1], xsq[:, c, :],
                                               start=(c == 0), stop=(c == ECH - 1),
                                               skip_group_check=True)
                            ins.then_inc(tS, 1)
                        t.wait_ge(vS, vSb + 1)
                        t.wait_ge(sS, sSb + 4)
                        t.matmul(psR0[:, :], ones[0:1, :], ROW(R_M, slice(0, NBW)),
                                 start=True, stop=True).then_inc(tS, 1)
                        t.matmul(psR1[:, :], ones[0:1, :], ROW(R_M, slice(NBW, NTOK)),
                                 start=True, stop=True).then_inc(tS, 1)
                        t.wait_ge(vS, vSb + 3)
                        t.wait_ge(sS, sSb + 7)
                        t.matmul(psR0[:, :], ones[0:1, :], ROW(R_RSTD, slice(0, NBW)),
                                 start=True, stop=True).then_inc(tS, 1)
                        t.matmul(psR1[:, :], ones[0:1, :], ROW(R_RSTD, slice(NBW, NTOK)),
                                 start=True, stop=True).then_inc(tS, 1)
                        for nb in range(NB):
                            t.wait_ge(vS, vSb + 3 + 2 * nb + 2)
                            if nb == 1:
                                t.wait_ge(sS, sSb + 8)
                            ps = psT if nb == 0 else psR0
                            for c4 in range(4):
                                t.transpose(ps[:, c4 * 128:(c4 + 1) * 128],
                                            maxacc[:, nb, c4 * 128:(c4 + 1) * 128],
                                            ident[:, :]).then_inc(tS, 1)
                        t.wait_ge(vS, vSb + 8)
                        t.transpose(psT2[:, :], mxTM[:, 0:8], ident[:, :]).then_inc(tS, 1)
                        t.wait_ge(vS, vSb + 9)
                        t.wait_ge(sS, sSb + 8)
                        t.matmul(psR0[:, :], ones[0:1, :], ROW(R_T2, slice(0, NBW)),
                                 start=True, stop=True).then_inc(tS, 1)
                        t.matmul(psR1[:, :], ones[0:1, :], ROW(R_T2, slice(NBW, NTOK)),
                                 start=True, stop=True).then_inc(tS, 1)

                    @blk.vector
                    def _(v):
                        v.wait_ge(sS, sSb + 4)
                        m = ROW(R_M); t_ = ROW(R_T); t2 = ROW(R_T2)
                        r2 = ROW(R_R2); t3 = ROW(R_T3); rstd = ROW(R_RSTD)
                        v.tensor_scalar(m, t_, 1.0 / 2048.0, None, ALU.mult)
                        v.tensor_scalar(t2, t2, 1.0 / 2048.0, None, ALU.mult)
                        v.tensor_mul(t_, m, m)
                        v.tensor_sub(t2, t2, t_)
                        v.tensor_scalar(t_, t2, EPS, None, ALU.add).then_inc(vS, 1)
                        v.wait_ge(sS, sSb + 5)
                        v.reciprocal(rstd, rstd)
                        v.tensor_mul(t3, rstd, rstd)
                        v.tensor_mul(t3, t3, t_)
                        v.tensor_scalar(t3, t3, -0.5, 1.5, ALU.mult, ALU.add)
                        v.tensor_mul(rstd, rstd, t3)
                        v.tensor_mul(r2, rstd, rstd)
                        v.tensor_mul(r2, r2, t2)
                        v.tensor_scalar(r2, r2, EPS, None, ALU.add).then_inc(vS, 1)
                        v.wait_ge(sS, sSb + 6)
                        v.reciprocal(t_, t_)
                        v.tensor_mul(t3, t_, t_)
                        v.tensor_mul(t3, t3, r2)
                        v.tensor_scalar(t3, t3, -0.5, 1.5, ALU.mult, ALU.add)
                        v.tensor_mul(r2, t_, t3).then_inc(vS, 1)
                        for nb in range(NB):
                            v.wait_ge(sS, sSb + 8)
                            nbs = slice(nb * NBW, (nb + 1) * NBW)
                            for c in range(ECH):
                                v.tensor_sub(xA[:, nb, c, :], xA[:, nb, c, :],
                                             mrep[:, nbs])
                                ins = v.tensor_mul(xA[:, nb, c, :], xA[:, nb, c, :],
                                                   rrep[:, nbs])
                            ins.then_inc(vS, 1)
                            for c in range(ECH):
                                v.tensor_scalar(xsq[:, c, :], xA[:, nb, c, :],
                                                gN[:, c:c + 1], None, ALU.mult)
                            v.reduce_max(maxacc[:, nb, :],
                                         bass.AP(xsq, 0, [[ECH * NBW, 128],
                                                          [1, NBW], [NBW, ECH]]),
                                         axis=AX.X, apply_absolute_value=True
                                         ).then_inc(vS, 1)
                        for nb in range(NB):
                            t_off = tSb + 6 + 4 * nb
                            ps = psT if nb == 0 else psR0
                            v.wait_ge(tS, t_off + 4)
                            for c4 in range(4):
                                ins = v.reduce_max(mxTM[:, nb * 4 + c4:nb * 4 + c4 + 1],
                                                   ps[:, c4 * 128:(c4 + 1) * 128],
                                                   axis=AX.X)
                        ins.then_inc(vS, 1)
                        v.wait_ge(dW, dWb + 32)
                        mx = ROW(R_MX)
                        v.tensor_mul(t_, mx, r2)
                        v.tensor_scalar(ROW(R_RSTD), t_, EPS, None, ALU.max)
                        v.reciprocal(t_, ROW(R_RSTD))
                        v.tensor_mul(t_, t_, r2)
                        v.tensor_scalar(t2, t_, 127.0, None, ALU.mult
                                        ).then_inc(vS, 1)
                        for nb in range(NB):
                            v.wait_ge(sS, sSb + 12)
                            nbs = slice(nb * NBW, (nb + 1) * NBW)
                            for c in range(ECH):
                                v.tensor_scalar(xA[:, nb, c, :], xA[:, nb, c, :],
                                                gN[:, c:c + 1], None, ALU.mult)
                            for c in range(ECH):
                                ins = v.tensor_mul(xA[:, nb, c, :], xA[:, nb, c, :],
                                                   c1rep[:, nbs])
                            ins.then_inc(vS, 1)
                        for nb in range(NB):
                            v.wait_ge(sS, sSb + 13 + nb)
                            nbs = slice(nb * NBW, (nb + 1) * NBW)
                            v.tensor_scalar(xA[:, nb, :, :], xA[:, nb, :, :],
                                            -MAGIC, 127.0, ALU.add, ALU.min)
                            v.tensor_scalar(xq_bf[:, :, nbs], xA[:, nb, :, :],
                                            -128.0, None, ALU.max).then_inc(vS, 1)

                    @blk.gpsimd
                    def _(g):
                        g.wait_ge(sS, sSb + 11)
                        g.dma_start(rowsc[:, :], tmp8[:, :]).then_inc(dW, 16)
                        g.wait_ge(dW, dWb + 16)
                        g.dma_start(ROW(R_MX),
                                    rowsc.rearrange("a b -> (a b)")[None, 0:NTOK]
                                    ).then_inc(dW, 16)
                    C['dW'] = dWb + 32

                C['tS'] = tSb + 17
                C['sS'] = sSb + 14
                C['vS'] = vSb + 13

                if debug:
                    with nc.Block() as blk:
                        @blk.sync
                        def _(sy):
                            sy.dma_start(dbg["xn"], xn_b[:, :, :]).then_inc(dX, 16)
                            sy.dma_start(dbg["xq"], xq_bf[:, :, :]).then_inc(dX, 16)
                            sy.dma_start(dbg["rows"], rowt[:, :]).then_inc(dX, 16)
                            C['dX'] += 48
                            sy.wait_ge(dX, C['dX'])

            _sc.__exit__(None, None, None)
            _sc = nc.named_scope("PC"); _sc.__enter__()
            NSG = 88
            NSD = 44
            NST = NSG + NSD
            pc = ExitStack()
            with pc:
                sacc = pc.enter_context(nc.sbuf_tensor("sacc", [128, 160], F32))
                pbuf = pc.enter_context(nc.sbuf_tensor("pbuf", [128, 2, 2048], F32))
                a0 = pc.enter_context(nc.sbuf_tensor("a0", [128, H, NTOK], BF16))
                qw_b = pc.enter_context(nc.sbuf_tensor("qw_b", [128, 2, ECH, 128], BF16))
                kt_b = pc.enter_context(nc.sbuf_tensor("kt_b", [128, 2, MCH, 128], BF16))
                v_b = pc.enter_context(nc.sbuf_tensor("v_b", [128, 2, MCH, 128], BF16))
                q_b = pc.enter_context(nc.sbuf_tensor("q_b", [128, NB, NBW], BF16))
                eb = pc.enter_context(nc.sbuf_tensor("eb", [128, 3, 2 * NBW], BF16))
                psE2 = [pc.enter_context(nc.psum_tensor(f"psE2{i}", [128, 2 * NBW], F32))
                        for i in range(3)]
                psA = [pc.enter_context(nc.psum_tensor(f"psA{i}", [128, NBW], F32))
                       for i in range(2)]
                psQ = [psE2[2][:, 0:NBW], psE2[2][:, NBW:2 * NBW]]

                wq3 = wqT.rearrange("(c p) f -> p c f", p=128)
                v3 = vW.rearrange("(c p) f -> p c f", p=128)
                gw3s = gwT.rearrange("(c p) f -> p c f", p=128)
                dw3s = dwT.rearrange("(c p) e -> p c e", p=128)

                dWb, tSb, sSb, vSb = C['dW'], C['tS'], C['sS'], C['vS']
                p0b, p1b = C['dP0'], C['dP1']
                q0b, q1b = C['dQ0'], C['dQ1']

                def headload_cnt(h):
                    return (p0b if h % 2 == 0 else p1b) + 48 * (h // 2 + 1)

                def posE2(mc):
                    return 3 if mc == 0 else 2 * mc + 2

                def posA2(mc):
                    return 2 * mc + 5 if mc <= 30 else 66

                vsched = []
                for h in range(H):
                    vsched.append(("qe", h, 0))
                    vsched.append(("qe", h, 1))
                    for u in range((NST * h) // H, (NST * (h + 1)) // H):
                        vsched.append(("red", u))
                    vsched.append(("pae", h, 0))
                    vsched.append(("pae", h, 1))
                vpos = {ev: k + 1 for k, ev in enumerate(vsched)}

                with nc.Block() as blk:
                    @blk.gpsimd
                    def _(g):
                        for h in range(H):
                            if h >= 2:
                                g.wait_ge(tS, tSb + 66 * (h - 1))
                            p = h % 2
                            g.dma_start(qw_b[:, p, :, :],
                                        wq3[:, :, h * 128:(h + 1) * 128]).then_inc(dP[p], 16)
                            g.dma_start(kt_b[:, p, :, :],
                                        ktT[h * 128:(h + 1) * 128, :]
                                        .rearrange("p (c d) -> p c d", d=128)).then_inc(dP[p], 16)
                            g.dma_start(v_b[:, p, :, :],
                                        v3[:, :, h * 128:(h + 1) * 128]).then_inc(dP[p], 16)
                    C['dP0'] = p0b + 48 * (H // 2)
                    C['dP1'] = p1b + 48 * (H // 2)

                    @blk.sync
                    def _(sy):
                        for u in range(NST):
                            if u >= 2:
                                sy.wait_ge(vS, vSb + vpos[("red", u - 2)])
                            if u < NSG:
                                src = gw3s[:, :, u * 128:(u + 1) * 128]
                                dst = pbuf[:, u % 2, :].rearrange("p (c f) -> p c f", f=128)
                            else:
                                src = dw3s[:, u - NSG, :]
                                dst = pbuf[:, u % 2, :]
                            sy.dma_start(dst, src).then_inc(dQ[u % 2], 16)
                    C['dQ0'] = q0b + 16 * (NST // 2)
                    C['dQ1'] = q1b + 16 * (NST // 2)

                    @blk.tensor
                    def _(t):
                        for h in range(H):
                            tSh = tSb + 66 * h
                            sSh = sSb + 32 * h
                            p = h % 2
                            t.wait_ge(sS, sSh)
                            if h >= 1:
                                t.wait_ge(vS, vSb + vpos[("pae", h - 1, 1)])
                            t.wait_ge(dP[p], headload_cnt(h))
                            for nb in range(NB):
                                for c in range(ECH):
                                    ins = t.matmul(psQ[nb], qw_b[:, p, c, :],
                                                   xn_b[:, c, nb * NBW:(nb + 1) * NBW],
                                                   start=(c == 0), stop=(c == ECH - 1),
                                                   skip_group_check=True)
                                ins.then_inc(tS, 1)
                            for mc in range(MCH):
                                if mc >= 3:
                                    t.wait_ge(sS, sSh + mc - 2)
                                if mc == 0:
                                    t.wait_ge(vS, vSb + vpos[("qe", h, 1)])
                                if mc == 2:
                                    t.wait_ge(vS, vSb + vpos[("qe", h, 1)])
                                t.matmul(psE2[mc % 3][:, 0:NBW], kt_b[:, p, mc, :],
                                         q_b[:, 0, :], start=True, stop=True,
                                         skip_group_check=True)
                                t.matmul(psE2[mc % 3][:, NBW:2 * NBW], kt_b[:, p, mc, :],
                                         q_b[:, 1, :], start=True, stop=True,
                                         skip_group_check=True).then_inc(tS, 1)
                                if mc >= 1:
                                    amc = mc - 1
                                    t.wait_ge(sS, sSh + amc + 1)
                                    t.matmul(psA[0][:, :], v_b[:, p, amc, :],
                                             eb[:, amc % 3, 0:NBW], start=(amc == 0),
                                             stop=False, skip_group_check=True)
                                    t.matmul(psA[1][:, :], v_b[:, p, amc, :],
                                             eb[:, amc % 3, NBW:2 * NBW], start=(amc == 0),
                                             stop=False, skip_group_check=True
                                             ).then_inc(tS, 1)
                            amc = MCH - 1
                            t.wait_ge(sS, sSh + amc + 1)
                            t.matmul(psA[0][:, :], v_b[:, p, amc, :],
                                     eb[:, amc % 3, 0:NBW], start=False, stop=True,
                                     skip_group_check=True)
                            t.matmul(psA[1][:, :], v_b[:, p, amc, :],
                                     eb[:, amc % 3, NBW:2 * NBW], start=False, stop=True,
                                     skip_group_check=True).then_inc(tS, 1)
                    C['tS'] = tSb + 66 * H

                    @blk.scalar
                    def _(s):
                        for h in range(H):
                            tSh = tSb + 66 * h
                            for mc in range(MCH):
                                s.wait_ge(tS, tSh + posE2(mc))
                                s.activation(eb[:, mc % 3, :], psE2[mc % 3][:, :],
                                             AF.Silu, scale=SQRT128).then_inc(sS, 1)
                    C['sS'] = sSb + 32 * H

                    @blk.vector
                    def _(v):
                        for ev in vsched:
                            if ev[0] == "qe":
                                _, h, nb = ev
                                v.wait_ge(tS, tSb + 66 * h + nb + 1)
                                v.tensor_copy(q_b[:, nb, :], psQ[nb]).then_inc(vS, 1)
                            elif ev[0] == "pae":
                                _, h, nb = ev
                                v.wait_ge(tS, tSb + 66 * h + 66)
                                v.tensor_scalar(a0[:, h, nb * NBW:(nb + 1) * NBW],
                                                psA[nb][:, :], SQRT128, None, ALU.mult
                                                ).then_inc(vS, 1)
                            else:
                                u = ev[1]
                                v.wait_ge(dQ[u % 2],
                                          (q0b if u % 2 == 0 else q1b) + 16 * (u // 2 + 1))
                                col = u if u < NSG else 96 + (u - NSG)
                                v.reduce_sum(sacc[:, col:col + 1], pbuf[:, u % 2, :],
                                             axis=AX.X, apply_absolute_value=True
                                             ).then_inc(vS, 1)
                    C['vS'] = vSb + len(vsched)

                gw_b = pc.enter_context(nc.sbuf_tensor("gw_b", [128, 2, ECH, 128], BF16))
                gt = pc.enter_context(nc.sbuf_tensor("gt", [128, 2, NBW], BF16))
                wg3 = wgT.rearrange("(c p) f -> p c f", p=128)
                dWb, tSb, sSb, vSb = C['dW'], C['tS'], C['sS'], C['vS']
                p0b, p1b = C['dP0'], C['dP1']
                with nc.Block() as blk:
                    @blk.gpsimd
                    def _(g):
                        for cb in range(ECH):
                            if cb >= 2:
                                g.wait_ge(tS, tSb + cb - 1)
                            g.dma_start(gw_b[:, cb % 2, :, :],
                                        wg3[:, :, cb * 128:(cb + 1) * 128]
                                        ).then_inc(dP[cb % 2], 16)
                    C['dP0'] = p0b + 16 * (ECH // 2)
                    C['dP1'] = p1b + 16 * (ECH // 2)

                    @blk.tensor
                    def _(t):
                        for cb in range(ECH):
                            t.wait_ge(dP[cb % 2],
                                      (p0b if cb % 2 == 0 else p1b) + 16 * (cb // 2 + 1))
                            if cb >= 2:
                                t.wait_ge(sS, sSb + cb - 1)
                            for nb in range(NB):
                                for c in range(ECH):
                                    ins = t.matmul(psE2[cb % 2][:, nb * NBW:(nb + 1) * NBW],
                                                   gw_b[:, cb % 2, c, :],
                                                   xn_b[:, c, nb * NBW:(nb + 1) * NBW],
                                                   start=(c == 0), stop=(c == ECH - 1),
                                                   skip_group_check=True)
                            ins.then_inc(tS, 1)
                    C['tS'] = tSb + ECH

                    @blk.scalar
                    def _(s):
                        for cb in range(ECH):
                            s.wait_ge(tS, tSb + cb + 1)
                            if cb >= 2:
                                s.wait_ge(vS, vSb + cb - 1)
                            s.activation(eb[:, cb % 2, :], psE2[cb % 2][:, :],
                                         AF.Silu).then_inc(sS, 1)
                    C['sS'] = sSb + ECH

                    @blk.vector
                    def _(v):
                        for cb in range(ECH):
                            v.wait_ge(sS, sSb + cb + 1)
                            v.tensor_mul(a0[:, cb, :], a0[:, cb, :],
                                         eb[:, cb % 2, :]).then_inc(vS, 1)
                    C['vS'] = vSb + ECH

                if debug:
                    with nc.Block() as blk:
                        @blk.sync
                        def _(sy):
                            sy.dma_start(dbg["a0"], a0[:, :, :]).then_inc(dX, 16)
                            C['dX'] += 16
                            sy.wait_ge(dX, C['dX'])

                wo_b = gw_b
                wo3 = woT.rearrange("(c p) f -> p c f", p=128)
                dWb, tSb, sSb = C['dW'], C['tS'], C['sS']
                dSb_base = [C['dSa'], C['dSb']]
                dS_p = [dSa, dSb]
                p0b, p1b = C['dP0'], C['dP1']

                with nc.Block() as blk:
                    @blk.gpsimd
                    def _(g):
                        for cb in range(ECH):
                            if cb >= 2:
                                g.wait_ge(tS, tSb + cb - 1)
                            g.dma_start(wo_b[:, cb % 2, :, :],
                                        wo3[:, :, cb * 128:(cb + 1) * 128]
                                        ).then_inc(dP[cb % 2], 16)
                            if cb >= 1:
                                g.wait_ge(sS, sSb + cb)
                                g.dma_start(attS[:, cb - 1, :], eb[:, (cb - 1) % 2, :]
                                            ).then_inc(dS_p[(cb - 1) % 2], 16)
                        g.wait_ge(sS, sSb + ECH)
                        g.dma_start(attS[:, ECH - 1, :], eb[:, (ECH - 1) % 2, :]
                                    ).then_inc(dS_p[(ECH - 1) % 2], 16)
                    C['dP0'] = p0b + 16 * (ECH // 2)
                    C['dP1'] = p1b + 16 * (ECH // 2)
                    C['dSa'] = dSb_base[0] + 16 * (ECH // 2)
                    C['dSb'] = dSb_base[1] + 16 * (ECH // 2)

                    @blk.tensor
                    def _(t):
                        for cb in range(ECH):
                            t.wait_ge(dP[cb % 2],
                                      (p0b if cb % 2 == 0 else p1b) + 16 * (cb // 2 + 1))
                            if cb >= 2:
                                t.wait_ge(sS, sSb + cb - 1)
                            for nb in range(NB):
                                for vc in range(ECH):
                                    ins = t.matmul(psE2[cb % 2][:, nb * NBW:(nb + 1) * NBW],
                                                   wo_b[:, cb % 2, vc, :],
                                                   a0[:, vc, nb * NBW:(nb + 1) * NBW],
                                                   start=(vc == 0), stop=(vc == ECH - 1),
                                                   skip_group_check=True)
                            ins.then_inc(tS, 1)
                    C['tS'] = tSb + ECH

                    @blk.scalar
                    def _(s):
                        for cb in range(ECH):
                            s.wait_ge(tS, tSb + cb + 1)
                            if cb >= 2:
                                s.wait_ge(dS_p[cb % 2],
                                          dSb_base[cb % 2] + 16 * ((cb - 2) // 2 + 1))
                            s.copy(eb[:, cb % 2, :], psE2[cb % 2][:, :]).then_inc(sS, 1)
                    C['sS'] = sSb + ECH

                with nc.Block() as blk:
                    @blk.gpsimd
                    def _(g):
                        g.wait_ge(dSa, C['dSa'])
                        g.wait_ge(dSb, C['dSb'])

                if True:
                    psW = psE2[2]
                    psR128 = psE2[2]

                    with nc.Block() as blk:
                        @blk.vector
                        def _(v):
                            v.reduce_sum(sw128[:, 0:1], sacc[:, 0:NSG], axis=AX.X)
                            v.reduce_sum(sw128[:, 1:2], sacc[:, 96:96 + NSD], axis=AX.X)

                    with nc.Block() as blk:
                        @blk.tensor
                        def _(t):
                            t.matmul(psW[0:1, 0:2], ones[:, 0:1], sw128[:, 0:2],
                                     start=True, stop=True)

                    with nc.Block() as blk:
                        @blk.scalar
                        def _(s):
                            s.copy(wstat[0:1, 0:2], psW[0:1, 0:2])

                    with nc.Block() as blk:
                        @blk.vector
                        def _(v):
                            v.tensor_scalar(wstat[0:1, 2:3], wstat[0:1, 0:1],
                                            1.0 / GW_NUMEL, EPS, ALU.mult, ALU.max)
                            v.tensor_scalar(wstat[0:1, 4:5], wstat[0:1, 1:2],
                                            1.0 / DW_NUMEL, EPS, ALU.mult, ALU.max)
                            v.reciprocal(wstat[0:1, 6:7], wstat[0:1, 2:3])
                            v.reciprocal(wstat[0:1, 7:8], wstat[0:1, 4:5])

                    with nc.Block() as blk:
                        @blk.tensor
                        def _(t):
                            t.matmul(psR128[:, 0:2], ones[0:1, :], wstat[0:1, 6:8],
                                     start=True, stop=True)

                    with nc.Block() as blk:
                        @blk.scalar
                        def _(s):
                            s.copy(sw128[:, 0:2], psR128[:, 0:2])

                if debug:
                    with nc.Block() as blk:
                        @blk.sync
                        def _(sy):
                            sy.dma_start(dbg["att"], attS).then_inc(dX, 16)
                            sy.dma_start(dbg["wstat"], wstat[0:1, :]).then_inc(dX, 16)
                            C['dX'] += 32
                            sy.wait_ge(dX, C['dX'])

        _sc.__exit__(None, None, None)
        _sc = nc.named_scope("PD01"); _sc.__enter__()
        pd0 = ExitStack()
        with pd0:
            h_b = pd0.enter_context(nc.sbuf_tensor("h_b", [128, ICH, NTOK], BF16))
            c2rep = pd0.enter_context(nc.sbuf_tensor("c2rep", [128, NTOK], F32))
            d2rep = pd0.enter_context(nc.sbuf_tensor("d2rep", [128, NTOK], F32))

            pd1 = ExitStack()
            with pd1:
                cgrep = d2rep
                gwf = pd1.enter_context(nc.sbuf_tensor("gwf", [128, 2, ECH, 128], F32))
                wqg = pd1.enter_context(nc.sbuf_tensor("wqg", [128, 2, ECH, 128], BF16))
                tg = pd1.enter_context(nc.sbuf_tensor("tg", [128, NBW], F32))
                tu = pd1.enter_context(nc.sbuf_tensor("tu", [128, NBW], F32))
                sg = pd1.enter_context(nc.sbuf_tensor("sg", [128, NBW], F32))
                tmv = tg
                mxv = pd1.enter_context(nc.sbuf_tensor("mxv", [128, 2, NBW], F32))
                mnv = pd1.enter_context(nc.sbuf_tensor("mnv", [128, 2, NBW], F32))
                mxTM2 = pd1.enter_context(nc.sbuf_tensor("mxTM2", [128, 8], F32))
                hsq = pd1.enter_context(nc.sbuf_tensor("hsq", [128, 2, 2, NBW], F32))

                cgs = ExitStack()
                with cgs:
                    psCa = cgs.enter_context(nc.psum_tensor("psCa", [128, NBW], F32))
                    psCb = cgs.enter_context(nc.psum_tensor("psCb", [128, NBW], F32))
                    with nc.Block() as blk:
                        @blk.vector
                        def _(v):
                            v.tensor_scalar(ROW(R_T3), ROW(R_RSTD),
                                            wstat[0:1, 2:3], 1.0 / 127.0,
                                            ALU.mult, ALU.mult)

                    with nc.Block() as blk:
                        @blk.tensor
                        def _(t):
                            t.matmul(psCa[:, :], ones[0:1, :], ROW(R_T3, slice(0, NBW)),
                                     start=True, stop=True)
                            t.matmul(psCb[:, :], ones[0:1, :], ROW(R_T3, slice(NBW, NTOK)),
                                     start=True, stop=True)

                    with nc.Block() as blk:
                        @blk.scalar
                        def _(s):
                            s.copy(cgrep[:, 0:NBW], psCa[:, :])
                            s.copy(cgrep[:, NBW:NTOK], psCb[:, :])

                gw3b = gwT.rearrange("(c p) f -> p c f", p=128)
                dXb, tSb, sSb, vSb = C['dX'], C['tS'], C['sS'], C['vS']
                NJ = ICH
                NS = 2 * NJ
                q0b, q1b = C['dQ0'], C['dQ1']

                vsched = [("vq", 0), ("vq", 1)]
                for j in range(NJ):
                    for s2 in (2 * j + 2, 2 * j + 3):
                        if s2 < NS:
                            vsched.append(("vq", s2))
                    for nb in range(NB):
                        vsched.append(("v1", j, nb))
                        vsched.append(("v2", j, nb))
                vpos = {ev: k + 1 for k, ev in enumerate(vsched)}

                def vq_cnt(s):
                    return vSb + vpos[("vq", s)]

                def v1_cnt(j, nb):
                    return vSb + vpos[("v1", j, nb)]

                def v2_cnt(j, nb):
                    return vSb + vpos[("v2", j, nb)]

                ssched = [("p3", 0), ("p3", 1)]
                for j in range(NJ):
                    for s2 in (2 * j + 2, 2 * j + 3):
                        if s2 < NS:
                            ssched.append(("p3", s2))
                    for nb in range(NB):
                        ssched.append(("silu", j, nb))
                spos = {ev: k + 1 for k, ev in enumerate(ssched)}

                def p3_cnt(s):
                    return sSb + spos[("p3", s)]

                def silu_cnt(j, nb):
                    return sSb + spos[("silu", j, nb)]

                def t1_cnt(s, nb):
                    j, hf = s // 2, s % 2
                    return tSb + 4 * j + (1 + nb if hf == 0 else 3 + nb)

                def fcol(s):
                    j, hf = s // 2, s % 2
                    return (j if hf == 0 else ICH + j) * 128

                lps = ExitStack()
                with lps:
                    psG = [[lps.enter_context(nc.psum_tensor(f"psG{p}{n}", [128, NBW], F32))
                            for n in range(2)] for p in range(2)]
                    psU = [[lps.enter_context(nc.psum_tensor(f"psU{p}{n}", [128, NBW], F32))
                            for n in range(2)] for p in range(2)]

                    with nc.Block() as blk:
                        @blk.sync
                        def _(sy):
                            for s_ in range(NS):
                                if s_ >= 2:
                                    sy.wait_ge(sS, p3_cnt(s_ - 2))
                                sy.dma_start(gwf[:, s_ % 2, :, :],
                                             gw3b[:, :, fcol(s_):fcol(s_) + 128]
                                             ).then_inc(dQ[s_ % 2], 16)
                        C['dQ0'] = q0b + 16 * (NS // 2)
                        C['dQ1'] = q1b + 16 * (NS // 2)

                        @blk.vector
                        def _(v):
                            v.memset(mxv[:, 0, :], 0.0)
                            v.memset(mxv[:, 1, :], 0.0)
                            v.memset(mnv[:, 0, :], 0.0)
                            v.memset(mnv[:, 1, :], 0.0)
                            for ev in vsched:
                                if ev[0] == "vq":
                                    s_ = ev[1]
                                    v.wait_ge(dQ[s_ % 2],
                                              (q0b if s_ % 2 == 0 else q1b)
                                              + 16 * (s_ // 2 + 1))
                                    v.tensor_scalar(gwf[:, s_ % 2, :, :],
                                                    gwf[:, s_ % 2, :, :],
                                                    sw128[:, 0:1], 1.0, ALU.mult, ALU.min)
                                    v.tensor_scalar(gwf[:, s_ % 2, :, :],
                                                    gwf[:, s_ % 2, :, :],
                                                    -1.0, MAGIC, ALU.max, ALU.add
                                                    ).then_inc(vS, 1)
                                elif ev[0] == "v1":
                                    _, j, nb = ev
                                    jp = j % 2
                                    v.wait_ge(tS, t1_cnt(2 * j + 1, nb))
                                    v.tensor_mul(tg[:, :], psG[jp][nb][:, :],
                                                 cgrep[:, nb * NBW:(nb + 1) * NBW])
                                    v.tensor_mul(tu[:, :], psU[jp][nb][:, :],
                                                 cgrep[:, nb * NBW:(nb + 1) * NBW]
                                                 ).then_inc(vS, 1)
                                else:
                                    _, j, nb = ev
                                    v.wait_ge(sS, silu_cnt(j, nb))
                                    hsl = h_b[:, j, nb * NBW:(nb + 1) * NBW]
                                    v.tensor_mul(hsl, sg[:, :], tu[:, :])
                                    v.tensor_scalar(tmv[:, :], hsl, dN[:, j:j + 1],
                                                    None, ALU.mult)
                                    v.tensor_tensor(mxv[:, nb, :], mxv[:, nb, :],
                                                    tmv[:, :], ALU.max)
                                    v.tensor_tensor(mnv[:, nb, :], mnv[:, nb, :],
                                                    tmv[:, :], ALU.min
                                                    ).then_inc(vS, 1)
                        C['vS'] = vSb + len(vsched)

                        @blk.scalar
                        def _(s):
                            for ev in ssched:
                                if ev[0] == "p3":
                                    s_ = ev[1]
                                    s.wait_ge(vS, vq_cnt(s_))
                                    if s_ >= 2:
                                        s.wait_ge(tS, t1_cnt(s_ - 2, 1))
                                    s.activation(wqg[:, s_ % 2, :, :],
                                                 gwf[:, s_ % 2, :, :],
                                                 AF.Copy, bias=-MAGIC).then_inc(sS, 1)
                                else:
                                    _, j, nb = ev
                                    s.wait_ge(vS, v1_cnt(j, nb))
                                    if nb == 0 and j >= 1:
                                        s.wait_ge(vS, v2_cnt(j - 1, 1))
                                    s.activation(sg[:, :], tg[:, :], AF.Silu
                                                 ).then_inc(sS, 1)
                        C['sS'] = sSb + len(ssched)

                        @blk.tensor
                        def _(t):
                            for s_ in range(NS):
                                j, hf = s_ // 2, s_ % 2
                                jp = j % 2
                                t.wait_ge(sS, p3_cnt(s_))
                                ps = psG[jp] if hf == 0 else psU[jp]
                                for nb in range(NB):
                                    if j >= 2:
                                        t.wait_ge(vS, v1_cnt(j - 2, nb))
                                    for c in range(ECH):
                                        ins = t.matmul(ps[nb][:, :], wqg[:, s_ % 2, c, :],
                                                       xq_bf[:, c, nb * NBW:(nb + 1) * NBW],
                                                       start=(c == 0), stop=(c == ECH - 1),
                                                       skip_group_check=True)
                                    ins.then_inc(tS, 1)
                        C['tS'] = tSb + 4 * NJ

                _sc.__exit__(None, None, None)
                _sc = nc.named_scope("PD1fin"); _sc.__enter__()
                fin = ExitStack()
                with fin:
                    psSh = [fin.enter_context(nc.psum_tensor(f"psSh{i}", [1, NBW], F32))
                            for i in range(2)]
                    psFa = fin.enter_context(nc.psum_tensor("psFa", [128, NBW], F32))
                    psFb = fin.enter_context(nc.psum_tensor("psFb", [128, NBW], F32))
                    psFc = fin.enter_context(nc.psum_tensor("psFc", [8, 128], F32))
                    psRa = fin.enter_context(nc.psum_tensor("psRa2", [128, NBW], F32))
                    psRb = fin.enter_context(nc.psum_tensor("psRb2", [128, NBW], F32))

                    sSb2, tSb2 = C['sS'], C['tS']
                    with nc.Block() as blk:
                        @blk.scalar
                        def _(s):
                            for j in range(ICH):
                                for nb in range(NB):
                                    if j >= 2:
                                        s.wait_ge(tS, tSb2 + 2 * (j - 2) + nb + 1)
                                    s.activation(hsq[:, j % 2, nb, :],
                                                 h_b[:, j, nb * NBW:(nb + 1) * NBW],
                                                 AF.Square).then_inc(sS, 1)

                        @blk.tensor
                        def _(t):
                            for j in range(ICH):
                                for nb in range(NB):
                                    t.wait_ge(sS, sSb2 + 2 * j + nb + 1)
                                    t.matmul(psSh[nb][:, :], ones[:, 0:1],
                                             hsq[:, j % 2, nb, :],
                                             start=(j == 0), stop=(j == ICH - 1),
                                             skip_group_check=True).then_inc(tS, 1)
                    C['sS'] = sSb2 + 2 * ICH
                    C['tS'] = tSb2 + 2 * ICH

                    with nc.Block() as blk:
                        @blk.scalar
                        def _(s):
                            s.copy(ROW(R_T, slice(0, NBW)), psSh[0][:, :])
                            s.copy(ROW(R_T, slice(NBW, NTOK)), psSh[1][:, :])

                    with nc.Block() as blk:
                        @blk.vector
                        def _(v):
                            for nb in range(NB):
                                v.tensor_scalar(mnv[:, nb, :], mnv[:, nb, :], -1.0,
                                                None, ALU.mult)
                                v.tensor_tensor(mxv[:, nb, :], mxv[:, nb, :],
                                                mnv[:, nb, :], ALU.max)

                    with nc.Block() as blk:
                        @blk.tensor
                        def _(t):
                            for nb in range(NB):
                                ps = psFa if nb == 0 else psFb
                                for c4 in range(4):
                                    t.transpose(ps[:, c4 * 128:(c4 + 1) * 128],
                                                mxv[:, nb, c4 * 128:(c4 + 1) * 128],
                                                ident[:, :])

                    with nc.Block() as blk:
                        @blk.vector
                        def _(v):
                            for nb in range(NB):
                                ps = psFa if nb == 0 else psFb
                                for c4 in range(4):
                                    v.reduce_max(mxTM2[:, nb * 4 + c4:nb * 4 + c4 + 1],
                                                 ps[:, c4 * 128:(c4 + 1) * 128], axis=AX.X)

                    with nc.Block() as blk:
                        @blk.tensor
                        def _(t):
                            t.transpose(psFc[:, :], mxTM2[:, 0:8], ident[:, :])

                    with nc.Block() as blk:
                        @blk.scalar
                        def _(s):
                            s.copy(tmp8[:, :], psFc[:, :])

                    with nc.Block() as blk:
                        @blk.gpsimd
                        def _(g):
                            g.dma_start(rowsc[:, :], tmp8[:, :]).then_inc(dW, 16)
                            C['dW'] += 16
                            g.wait_ge(dW, C['dW'])
                            g.dma_start(ROW(R_MX),
                                        rowsc.rearrange("a b -> (a b)")[None, 0:NTOK]
                                        ).then_inc(dW, 16)
                            C['dW'] += 16
                            g.wait_ge(dW, C['dW'])

                    with nc.Block() as blk:
                        @blk.vector
                        def _(v):
                            hs = ROW(R_T)
                            v.tensor_scalar(hs, hs, 1.0 / 5632.0, EPS, ALU.mult, ALU.add)

                    with nc.Block() as blk:
                        @blk.scalar
                        def _(s):
                            s.activation(ROW(R_T2), ROW(R_T), AF.Sqrt)

                    with nc.Block() as blk:
                        @blk.vector
                        def _(v):
                            w = ROW(R_T)
                            r = ROW(R_T2)
                            t3 = ROW(R_T3)
                            mx = ROW(R_MX)
                            v.reciprocal(r, r)
                            v.tensor_mul(t3, r, r)
                            v.tensor_mul(t3, t3, w)
                            v.tensor_scalar(t3, t3, -0.5, 1.5, ALU.mult, ALU.add)
                            v.tensor_mul(r, r, t3)
                            v.tensor_mul(t3, mx, r)
                            v.tensor_scalar(t3, t3, EPS, None, ALU.max)
                            v.reciprocal(w, t3)
                            v.tensor_mul(w, w, r)
                            v.tensor_scalar(ROW(R_T), w, 127.0, None, ALU.mult)
                            v.tensor_scalar(t3, t3, wstat[0:1, 4:5], 1.0 / 127.0,
                                            ALU.mult, ALU.mult)

                    with nc.Block() as blk:
                        @blk.tensor
                        def _(t):
                            t.matmul(psRa[:, :], ones[0:1, :], ROW(R_T, slice(0, NBW)),
                                     start=True, stop=True)
                            t.matmul(psRb[:, :], ones[0:1, :], ROW(R_T, slice(NBW, NTOK)),
                                     start=True, stop=True)
                            t.matmul(psFa[:, :], ones[0:1, :], ROW(R_T3, slice(0, NBW)),
                                     start=True, stop=True)
                            t.matmul(psFb[:, :], ones[0:1, :], ROW(R_T3, slice(NBW, NTOK)),
                                     start=True, stop=True)

                    with nc.Block() as blk:
                        @blk.scalar
                        def _(s):
                            s.copy(c2rep[:, 0:NBW], psRa[:, :])
                            s.copy(c2rep[:, NBW:NTOK], psRb[:, :])
                            s.copy(d2rep[:, 0:NBW], psFa[:, :])
                            s.copy(d2rep[:, NBW:NTOK], psFb[:, :])

                if debug:
                    with nc.Block() as blk:
                        @blk.sync
                        def _(sy):
                            sy.dma_start(dbg["h"], h_b[:, :, :]).then_inc(dX, 16)
                            sy.dma_start(dbg["wq"], wqg[:, :, :, :]).then_inc(dX, 16)
                            sy.dma_start(dbg["gwf"], gwf[:, :, :, :]).then_inc(dX, 16)
                            sy.dma_start(dbg["sw"], sw128[:, :]).then_inc(dX, 16)
                            C['dX'] += 64
                            sy.wait_ge(dX, C['dX'])

            sxq.close()

            _sc.__exit__(None, None, None)
            _sc = nc.named_scope("PD2"); _sc.__enter__()
            pd2 = ExitStack()
            with pd2:
                hq_b = pd2.enter_context(nc.sbuf_tensor("hq_b", [128, ICH, NBW], BF16))
                tqa = pd2.enter_context(nc.sbuf_tensor("tqa", [128, 3, NBW], F32))
                tqb = pd2.enter_context(nc.sbuf_tensor("tqb", [128, 2, NBW], F32))
                dwf = pd2.enter_context(nc.sbuf_tensor("dwf", [128, 3, 8, 128], F32))
                dwq = pd2.enter_context(nc.sbuf_tensor("dwq", [128, 3, 8, 128], BF16))
                xtile = pd2.enter_context(nc.sbuf_tensor("xtile", [128, 2, NBW], F32))
                atile = pd2.enter_context(nc.sbuf_tensor("atile", [128, 2, NBW], BF16))
                otile = pd2.enter_context(nc.sbuf_tensor("otile", [128, 2, NBW], F32))
                psD = [pd2.enter_context(nc.psum_tensor(f"psD{i}", [128, NBW], F32))
                       for i in range(8)]

                for ng in range(NB):
                    ngs = slice(ng * NBW, (ng + 1) * NBW)

                    vb, sb = C['vS'], C['sS']
                    with nc.Block() as blk:
                        @blk.vector
                        def _(v, vb=vb, sb=sb, ngs=ngs):
                            for i in range(ICH + 2):
                                if i < ICH:
                                    p = i % 3
                                    if i >= 3:
                                        v.wait_ge(sS, sb + i - 2)
                                    v.tensor_scalar(tqa[:, p, :], h_b[:, i, ngs],
                                                    dN[:, i:i + 1], None, ALU.mult)
                                    v.tensor_mul(tqa[:, p, :], tqa[:, p, :],
                                                 c2rep[:, ngs]).then_inc(vS, 1)
                                if i >= 2:
                                    ib = i - 2
                                    pb_ = ib % 2
                                    v.wait_ge(sS, sb + ib + 1)
                                    v.tensor_scalar(tqb[:, pb_, :], tqb[:, pb_, :],
                                                    -MAGIC, 127.0, ALU.add, ALU.min)
                                    v.tensor_scalar(hq_b[:, ib, :], tqb[:, pb_, :],
                                                    -128.0, None, ALU.max).then_inc(vS, 1)

                        @blk.scalar
                        def _(s, vb=vb, sb=sb):
                            apos = {}
                            k = 0
                            for i in range(ICH + 2):
                                if i < ICH:
                                    k += 1
                                    apos[i] = k
                                if i >= 2:
                                    k += 1
                            for i in range(ICH):
                                s.wait_ge(vS, vb + apos[i])
                                s.activation(tqb[:, i % 2, :], tqa[:, i % 3, :], AF.Copy,
                                             bias=MAGIC).then_inc(sS, 1)
                    C['vS'] = vb + 2 * ICH
                    C['sS'] = sb + ICH

                    for eg in range(2):
                        qb4 = [C[f'dQ{i}'] for i in range(4)]
                        vb, sb, gb, tb = C['vS'], C['sS'], C['gS'], C['tS']
                        with nc.Block() as blk:
                            @blk.sync
                            def _(sy, sb=sb, eg=eg):
                                for i in range(ICH):
                                    if i >= 3:
                                        sy.wait_ge(sS, sb + i - 2)
                                    src = dwT[i * 128:(i + 1) * 128,
                                              eg * 1024:(eg + 1) * 1024].rearrange(
                                        "p (c e) -> p c e", e=128)
                                    sy.dma_start(dwf[:, i % 3, :, :], src
                                                 ).then_inc(dQ[i % 3], 16)
                            for i in range(3):
                                C[f'dQ{i}'] = qb4[i] + 16 * len(range(i, ICH, 3))

                            @blk.vector
                            def _(v, qb4=qb4):
                                for i in range(ICH):
                                    v.wait_ge(dQ[i % 3], qb4[i % 3] + 16 * (i // 3 + 1))
                                    v.tensor_scalar(dwf[:, i % 3, :, :], dwf[:, i % 3, :, :],
                                                    sw128[:, 1:2], 1.0, ALU.mult, ALU.min)
                                    v.tensor_scalar(dwf[:, i % 3, :, :], dwf[:, i % 3, :, :],
                                                    -1.0, MAGIC, ALU.max, ALU.add
                                                    ).then_inc(vS, 1)
                            C['vS'] = vb + ICH

                            @blk.scalar
                            def _(s, vb=vb, tb=tb):
                                for i in range(ICH):
                                    s.wait_ge(vS, vb + i + 1)
                                    if i >= 3:
                                        s.wait_ge(tS, tb + i - 2)
                                    s.activation(dwq[:, i % 3, :, :], dwf[:, i % 3, :, :],
                                                 AF.Copy, bias=-MAGIC).then_inc(sS, 1)
                            C['sS'] = sb + ICH

                            @blk.tensor
                            def _(t, sb=sb):
                                for i in range(ICH):
                                    t.wait_ge(sS, sb + i + 1)
                                    for ec in range(8):
                                        ins = t.matmul(psD[ec][:, :], dwq[:, i % 3, ec, :],
                                                       hq_b[:, i, :], start=(i == 0),
                                                       stop=(i == ICH - 1),
                                                       skip_group_check=True)
                                    ins.then_inc(tS, 1)
                            C['tS'] = tb + ICH

                        q0b, q1b = C['dQ0'], C['dQ1']
                        x0b, x1b = C['dXs0'], C['dXs1']
                        vb = C['vS']
                        ev_seq = []
                        for ec in range(8):
                            ev_seq.append(("L", ec))
                            if ec >= 2:
                                ev_seq.append(("S", ec - 2))
                        ev_seq += [("S", 6), ("S", 7)]
                        with nc.Block() as blk:
                            @blk.sync
                            def _(sy, vb=vb, ng=ng, eg=eg, ngs=ngs, ev_seq=ev_seq):
                                for kind, ec in ev_seq:
                                    row = (eg * 8 + ec) * 128
                                    if kind == "L":
                                        if ec >= 2:
                                            sy.wait_ge(vS, vb + ec - 1)
                                        sy.dma_start(xtile[:, ec % 2, :],
                                                     xT[row:row + 128, ngs]
                                                     ).then_inc(dQ[ec % 2], 16)
                                        sy.dma_start(atile[:, ec % 2, :],
                                                     attS[:, eg * 8 + ec, ngs]
                                                     ).then_inc(dQ[ec % 2], 16)
                                    else:
                                        sy.wait_ge(vS, vb + ec + 1)
                                        sy.dma_start(outT[row:row + 128, ngs],
                                                     otile[:, ec % 2, :]
                                                     ).then_inc(dXs[ec % 2], 16)
                            C['dQ0'] = q0b + 32 * 4
                            C['dQ1'] = q1b + 32 * 4
                            C['dXs0'] = x0b + 16 * 4
                            C['dXs1'] = x1b + 16 * 4

                            @blk.vector
                            def _(v, q0b=q0b, q1b=q1b, x0b=x0b, x1b=x1b, vb=vb, ngs=ngs):
                                for ec in range(8):
                                    p = ec % 2
                                    v.wait_ge(dQ[p], (q0b if p == 0 else q1b)
                                              + 32 * (ec // 2 + 1))
                                    if ec >= 2:
                                        v.wait_ge(dXs[p], (x0b if p == 0 else x1b)
                                                  + 16 * (ec // 2))
                                    v.tensor_mul(otile[:, p, :], psD[ec][:, :],
                                                 d2rep[:, ngs])
                                    v.tensor_add(otile[:, p, :], otile[:, p, :],
                                                 atile[:, p, :])
                                    v.tensor_add(otile[:, p, :], otile[:, p, :],
                                                 xtile[:, p, :]).then_inc(vS, 1)
                            C['vS'] = vb + 8

        _sc.__exit__(None, None, None)
        with nc.Block() as blk:
            @blk.sync
            def _(sy):
                sy.wait_ge(dX, C['dX'])
                sy.wait_ge(dSa, C['dSa'])
                sy.wait_ge(dSb, C['dSb'])
                sy.wait_ge(dQ[0], C['dQ0'])
                sy.wait_ge(dQ[1], C['dQ1'])
                sy.wait_ge(dQ[2], C['dQ2'])
                sy.wait_ge(dQ[3], C['dQ3'])
                sy.wait_ge(dP[0], C['dP0'])
                sy.wait_ge(dP[1], C['dP1'])
                sy.wait_ge(dXs[0], C['dXs0'])
                sy.wait_ge(dXs[1], C['dXs1'])

            @blk.gpsimd
            def _(g):
                g.wait_ge(dW, C['dW'])

    return nc


_CACHED = {}


def _get_nc(debug):
    key = bool(debug)
    if key not in _CACHED:
        _CACHED[key] = build_core_program(debug=key)
    return _CACHED[key]


def kernel(L0_attn_x, L0_attn_Wq, L0_attn_k_weight, L0_attn_v_weight, L0_attn_Wg,
           L0_attn_Wout, L0_ffn_gate_w, L0_ffn_down_w, L0_ffn_gate_norm_w,
           L0_ffn_down_norm_w):
    x = np.asarray(L0_attn_x)
    B, N, E = x.shape
    xt = np.ascontiguousarray(x.reshape(N, E).T)

    wq = np.ascontiguousarray(np.asarray(L0_attn_Wq).T)
    kt = np.ascontiguousarray(np.asarray(L0_attn_k_weight).T)
    vw = np.ascontiguousarray(np.asarray(L0_attn_v_weight))
    wg = np.ascontiguousarray(np.asarray(L0_attn_Wg).T)
    wo = np.ascontiguousarray(np.asarray(L0_attn_Wout).T)
    gw = np.ascontiguousarray(np.asarray(L0_ffn_gate_w).T)
    dw = np.ascontiguousarray(np.asarray(L0_ffn_down_w).T)
    gn = np.ascontiguousarray(np.asarray(L0_ffn_gate_norm_w).reshape(16, 128).T)
    dn = np.ascontiguousarray(np.asarray(L0_ffn_down_norm_w).reshape(44, 128).T)

    cns = np.zeros((128, 16), np.float32)
    cns[:, 0] = EPS
    cns[:, 1] = SQRT128
    onesf = np.ones((128, 128), np.float32)
    identf = np.eye(128, dtype=np.float32)

    nc = _get_nc(DEBUG)
    in_maps = []
    for c in range(8):
        in_maps.append(dict(
            xT=np.ascontiguousarray(xt[:, c * NTOK:(c + 1) * NTOK]),
            wqT=wq, ktT=kt, vW=vw, wgT=wg, woT=wo, gwT=gw, dwT=dw,
            cns=cns, onesf=onesf, identf=identf, gN128=gn, dN128=dn,
        ))
    res = run_bass_kernel_spmd(nc, in_maps, core_ids=list(range(8)))
    out = np.concatenate([r["outT"] for r in res.results], axis=1)
    if DEBUG:
        kernel._dbg = res.results
    return np.ascontiguousarray(out.T).reshape(B, N, E).astype(np.float32)


# version 14
# speedup vs baseline: 1.0295x; 1.0171x over previous
import numpy as np
import ml_dtypes
from contextlib import ExitStack
import concourse.bass as bass
import concourse.mybir as mybir
from concourse.bass_utils import run_bass_kernel_spmd

F32 = mybir.dt.float32
BF16 = mybir.dt.bfloat16
AF = mybir.ActivationFunctionType
ALU = mybir.AluOpType
AX = mybir.AxisListType

MAGIC = 12582912.0
SQRT128 = float(np.float32(np.sqrt(np.float32(128.0))))
EPS = 1e-5
NTOK = 1024
NBW = 512
NB = 2
ECH = 16
H = 16
MCH = 32
ICH = 44
GW_NUMEL = 11264 * 2048
DW_NUMEL = 2048 * 5632

DEBUG = False
PD1_SERIAL = False


def build_core_program(debug=False):
    nc = bass.Bass("TRN2", target_bir_lowering=False)

    xT = nc.dram_tensor("xT", [2048, NTOK], F32, kind="ExternalInput").ap()
    wqT = nc.dram_tensor("wqT", [2048, 2048], F32, kind="ExternalInput").ap()
    ktT = nc.dram_tensor("ktT", [2048, 4096], F32, kind="ExternalInput").ap()
    vW = nc.dram_tensor("vW", [4096, 2048], F32, kind="ExternalInput").ap()
    wgT = nc.dram_tensor("wgT", [2048, 2048], F32, kind="ExternalInput").ap()
    woT = nc.dram_tensor("woT", [2048, 2048], F32, kind="ExternalInput").ap()
    gwT = nc.dram_tensor("gwT", [2048, 11264], F32, kind="ExternalInput").ap()
    dwT = nc.dram_tensor("dwT", [5632, 2048], F32, kind="ExternalInput").ap()
    cnsI = nc.dram_tensor("cns", [128, 16], F32, kind="ExternalInput").ap()
    onesI = nc.dram_tensor("onesf", [128, 128], F32, kind="ExternalInput").ap()
    identI = nc.dram_tensor("identf", [128, 128], F32, kind="ExternalInput").ap()
    gNI = nc.dram_tensor("gN128", [128, 16], F32, kind="ExternalInput").ap()
    dNI = nc.dram_tensor("dN128", [128, 44], F32, kind="ExternalInput").ap()

    outT = nc.dram_tensor("outT", [2048, NTOK], F32, kind="ExternalOutput").ap()
    attS = nc.dram_tensor("attS", [128, 16, NTOK], BF16, kind="Internal").ap()
    rowsc = nc.dram_tensor("rowsc", [8, 128], F32, kind="Internal").ap()

    dbg = {}
    if debug:
        dbg["xn"] = nc.dram_tensor("dbg_xn", [128, ECH, NTOK], BF16, kind="ExternalOutput").ap()
        dbg["xq"] = nc.dram_tensor("dbg_xq", [128, ECH, NTOK], BF16, kind="ExternalOutput").ap()
        dbg["a0"] = nc.dram_tensor("dbg_a0", [128, H, NTOK], BF16, kind="ExternalOutput").ap()
        dbg["att"] = nc.dram_tensor("dbg_att", [128, 16, NTOK], BF16, kind="ExternalOutput").ap()
        dbg["h"] = nc.dram_tensor("dbg_h", [128, ICH, NTOK], BF16, kind="ExternalOutput").ap()
        dbg["rows"] = nc.dram_tensor("dbg_rows", [1, 6 * NTOK], F32, kind="ExternalOutput").ap()
        dbg["wstat"] = nc.dram_tensor("dbg_wstat", [1, 8], F32, kind="ExternalOutput").ap()
        dbg["xq2"] = nc.dram_tensor("dbg_xq2", [128, ECH, NTOK], BF16, kind="ExternalOutput").ap()
        dbg["cg"] = nc.dram_tensor("dbg_cg", [128, NTOK], F32, kind="ExternalOutput").ap()
        dbg["wq"] = nc.dram_tensor("dbg_wq", [128, 2, ECH, 128], BF16, kind="ExternalOutput").ap()
        dbg["gwf"] = nc.dram_tensor("dbg_gwf", [128, 2, ECH, 128], F32, kind="ExternalOutput").ap()
        dbg["sw"] = nc.dram_tensor("dbg_sw", [128, 2], F32, kind="ExternalOutput").ap()

    st = ExitStack()
    with st:
        cns = st.enter_context(nc.sbuf_tensor("cns_s", [128, 16], F32))
        ones = st.enter_context(nc.sbuf_tensor("ones_s", [128, 128], F32))
        ident = st.enter_context(nc.sbuf_tensor("ident_s", [128, 128], F32))
        gN = st.enter_context(nc.sbuf_tensor("gN_s", [128, 16], F32))
        dN = st.enter_context(nc.sbuf_tensor("dN_s", [128, 44], F32))
        rowt = st.enter_context(nc.sbuf_tensor("rows_s", [1, 6 * NTOK], F32))
        wstat = st.enter_context(nc.sbuf_tensor("wstat_s", [1, 8], F32))
        sw128 = st.enter_context(nc.sbuf_tensor("sw128_s", [128, 2], F32))
        tmp8 = st.enter_context(nc.sbuf_tensor("tmp8_s", [8, 128], F32))

        def ROW(r, sl=None):
            sl = sl if sl is not None else slice(0, NTOK)
            return rowt[0:1, r * NTOK + sl.start: r * NTOK + sl.stop]

        sxq = ExitStack()
        xq_bf = sxq.enter_context(nc.sbuf_tensor("xq_bf", [128, ECH, NTOK], BF16,
                                                 side="right"))

        dW = st.enter_context(nc.semaphore("dW"))
        dX = st.enter_context(nc.semaphore("dX"))
        tS = st.enter_context(nc.semaphore("tS"))
        sS = st.enter_context(nc.semaphore("sS"))
        vS = st.enter_context(nc.semaphore("vS"))
        gS = st.enter_context(nc.semaphore("gS"))
        dSa = st.enter_context(nc.semaphore("dSa"))
        dSb = st.enter_context(nc.semaphore("dSb"))
        dQ = [st.enter_context(nc.semaphore(f"dQ{i}")) for i in range(4)]
        dP = [st.enter_context(nc.semaphore(f"dP{i}")) for i in range(2)]
        dXs = [st.enter_context(nc.semaphore(f"dXs{i}")) for i in range(2)]
        C = dict(dW=0, dX=0, tS=0, sS=0, vS=0, gS=0, dSa=0, dSb=0,
                 dQ0=0, dQ1=0, dQ2=0, dQ3=0, dP0=0, dP1=0, dXs0=0, dXs1=0)

        nc.const_aps.register(F32, EPS, cns[0:1, 0:1])
        nc.const_aps.register(F32, SQRT128, cns[0:1, 1:2])

        R_M, R_RSTD, R_R2, R_T, R_T2, R_T3 = 0, 1, 2, 3, 4, 5
        R_MX = R_M

        _sc = nc.named_scope("P0"); _sc.__enter__()
        with nc.Block() as blk:
            @blk.gpsimd
            def _(g):
                g.dma_start(cns[:, :], cnsI).then_inc(dW, 16)
                g.dma_start(ones[:, :], onesI).then_inc(dW, 16)
                g.dma_start(ident[:, :], identI).then_inc(dW, 16)
                g.dma_start(gN[:, :], gNI).then_inc(dW, 16)
                g.dma_start(dN[:, :], dNI).then_inc(dW, 16)
                C['dW'] += 80
                g.wait_ge(dW, C['dW'])

        _sc.__exit__(None, None, None)
        s2 = ExitStack()
        with s2:
            xn_b = s2.enter_context(nc.sbuf_tensor("xn_b", [128, ECH, NTOK], BF16))

            _sc = nc.named_scope("PA"); _sc.__enter__()
            pa = ExitStack()
            with pa:
                xA = pa.enter_context(nc.sbuf_tensor("xA", [128, NB, ECH, NBW], F32))
                xsq = pa.enter_context(nc.sbuf_tensor("xsq", [128, ECH, NBW], F32))
                maxacc = pa.enter_context(nc.sbuf_tensor("maxacc", [128, 2, NBW], F32))
                mrep = pa.enter_context(nc.sbuf_tensor("mrep", [128, NTOK], F32))
                rrep = pa.enter_context(nc.sbuf_tensor("rrep", [128, NTOK], F32))
                c1rep = pa.enter_context(nc.sbuf_tensor("c1rep", [128, NTOK], F32))
                mxTM = pa.enter_context(nc.sbuf_tensor("mxTM", [128, 8], F32))
                psS = [pa.enter_context(nc.psum_tensor(f"psS{i}", [1, NBW], F32))
                       for i in range(2)]
                psS2 = [pa.enter_context(nc.psum_tensor(f"psS2{i}", [1, NBW], F32))
                        for i in range(2)]
                psR0 = pa.enter_context(nc.psum_tensor("psR0", [128, NBW], F32))
                psR1 = pa.enter_context(nc.psum_tensor("psR1", [128, NBW], F32))
                psT = pa.enter_context(nc.psum_tensor("psT", [128, NBW], F32))
                psT2 = pa.enter_context(nc.psum_tensor("psT2", [8, 128], F32))

                x3 = xT.rearrange("(c p) n -> p c n", p=128)
                q0b, q1b = C['dQ0'], C['dQ1']
                tSb, sSb, vSb, dWb = C['tS'], C['sS'], C['vS'], C['dW']

                with nc.Block() as blk:
                    @blk.sync
                    def _(sy):
                        for nb in range(NB):
                            sy.dma_start(xA[:, nb, :, :],
                                         x3[:, :, nb * NBW:(nb + 1) * NBW]
                                         ).then_inc(dQ[nb], 16)
                    C['dQ0'] = q0b + 16
                    C['dQ1'] = q1b + 16

                    @blk.scalar
                    def _(s):
                        for nb in range(NB):
                            s.wait_ge(dQ[nb], (q0b if nb == 0 else q1b) + 16)
                            if nb == 1:
                                s.wait_ge(tS, tSb + 1)
                            s.activation(xsq[:, :, :], xA[:, nb, :, :], AF.Square
                                         ).then_inc(sS, 1)
                        for nb in range(NB):
                            s.wait_ge(tS, tSb + nb + 1)
                            nbs = slice(nb * NBW, (nb + 1) * NBW)
                            s.copy(ROW(R_T, nbs), psS[nb][:, :])
                            s.copy(ROW(R_T2, nbs), psS2[nb][:, :]).then_inc(sS, 1)
                        s.wait_ge(vS, vSb + 1)
                        s.activation(ROW(R_RSTD), ROW(R_T), AF.Sqrt).then_inc(sS, 1)
                        s.wait_ge(vS, vSb + 2)
                        s.activation(ROW(R_T), ROW(R_R2), AF.Sqrt).then_inc(sS, 1)
                        s.wait_ge(tS, tSb + 4)
                        s.copy(mrep[:, 0:NBW], psR0[:, :])
                        s.copy(mrep[:, NBW:NTOK], psR1[:, :]).then_inc(sS, 1)
                        s.wait_ge(tS, tSb + 6)
                        s.copy(rrep[:, 0:NBW], psR0[:, :])
                        s.copy(rrep[:, NBW:NTOK], psR1[:, :]).then_inc(sS, 1)
                        for nb in range(NB):
                            s.wait_ge(vS, vSb + 3 + 2 * nb + 1)
                            s.copy(xn_b[:, :, nb * NBW:(nb + 1) * NBW], xA[:, nb, :, :]
                                   ).then_inc(sS, 1)
                        s.wait_ge(tS, tSb + 6 + 8 + 1)
                        s.copy(tmp8[:, :], psT2[:, :]).then_inc(sS, 1)
                        s.wait_ge(tS, tSb + 17)
                        s.copy(c1rep[:, 0:NBW], psR0[:, :])
                        s.copy(c1rep[:, NBW:NTOK], psR1[:, :]).then_inc(sS, 1)
                        for nb in range(NB):
                            s.wait_ge(vS, vSb + 10 + nb)
                            s.activation(xA[:, nb, :, :], xA[:, nb, :, :],
                                         AF.Copy, bias=MAGIC).then_inc(sS, 1)

                    @blk.tensor
                    def _(t):
                        for nb in range(NB):
                            t.wait_ge(sS, sSb + nb + 1)
                            for c in range(ECH):
                                t.matmul(psS[nb][:, :], ones[:, 0:1], xA[:, nb, c, :],
                                         start=(c == 0), stop=(c == ECH - 1),
                                         skip_group_check=True)
                            for c in range(ECH):
                                ins = t.matmul(psS2[nb][:, :], ones[:, 0:1], xsq[:, c, :],
                                               start=(c == 0), stop=(c == ECH - 1),
                                               skip_group_check=True)
                            ins.then_inc(tS, 1)
                        t.wait_ge(vS, vSb + 1)
                        t.wait_ge(sS, sSb + 4)
                        t.matmul(psR0[:, :], ones[0:1, :], ROW(R_M, slice(0, NBW)),
                                 start=True, stop=True).then_inc(tS, 1)
                        t.matmul(psR1[:, :], ones[0:1, :], ROW(R_M, slice(NBW, NTOK)),
                                 start=True, stop=True).then_inc(tS, 1)
                        t.wait_ge(vS, vSb + 3)
                        t.wait_ge(sS, sSb + 7)
                        t.matmul(psR0[:, :], ones[0:1, :], ROW(R_RSTD, slice(0, NBW)),
                                 start=True, stop=True).then_inc(tS, 1)
                        t.matmul(psR1[:, :], ones[0:1, :], ROW(R_RSTD, slice(NBW, NTOK)),
                                 start=True, stop=True).then_inc(tS, 1)
                        for nb in range(NB):
                            t.wait_ge(vS, vSb + 3 + 2 * nb + 2)
                            if nb == 1:
                                t.wait_ge(sS, sSb + 8)
                            ps = psT if nb == 0 else psR0
                            for c4 in range(4):
                                t.transpose(ps[:, c4 * 128:(c4 + 1) * 128],
                                            maxacc[:, nb, c4 * 128:(c4 + 1) * 128],
                                            ident[:, :]).then_inc(tS, 1)
                        t.wait_ge(vS, vSb + 8)
                        t.transpose(psT2[:, :], mxTM[:, 0:8], ident[:, :]).then_inc(tS, 1)
                        t.wait_ge(vS, vSb + 9)
                        t.wait_ge(sS, sSb + 8)
                        t.matmul(psR0[:, :], ones[0:1, :], ROW(R_T2, slice(0, NBW)),
                                 start=True, stop=True).then_inc(tS, 1)
                        t.matmul(psR1[:, :], ones[0:1, :], ROW(R_T2, slice(NBW, NTOK)),
                                 start=True, stop=True).then_inc(tS, 1)

                    @blk.vector
                    def _(v):
                        v.wait_ge(sS, sSb + 4)
                        m = ROW(R_M); t_ = ROW(R_T); t2 = ROW(R_T2)
                        r2 = ROW(R_R2); t3 = ROW(R_T3); rstd = ROW(R_RSTD)
                        v.tensor_scalar(m, t_, 1.0 / 2048.0, None, ALU.mult)
                        v.tensor_scalar(t2, t2, 1.0 / 2048.0, None, ALU.mult)
                        v.tensor_mul(t_, m, m)
                        v.tensor_sub(t2, t2, t_)
                        v.tensor_scalar(t_, t2, EPS, None, ALU.add).then_inc(vS, 1)
                        v.wait_ge(sS, sSb + 5)
                        v.reciprocal(rstd, rstd)
                        v.tensor_mul(t3, rstd, rstd)
                        v.tensor_mul(t3, t3, t_)
                        v.tensor_scalar(t3, t3, -0.5, 1.5, ALU.mult, ALU.add)
                        v.tensor_mul(rstd, rstd, t3)
                        v.tensor_mul(r2, rstd, rstd)
                        v.tensor_mul(r2, r2, t2)
                        v.tensor_scalar(r2, r2, EPS, None, ALU.add).then_inc(vS, 1)
                        v.wait_ge(sS, sSb + 6)
                        v.reciprocal(t_, t_)
                        v.tensor_mul(t3, t_, t_)
                        v.tensor_mul(t3, t3, r2)
                        v.tensor_scalar(t3, t3, -0.5, 1.5, ALU.mult, ALU.add)
                        v.tensor_mul(r2, t_, t3).then_inc(vS, 1)
                        for nb in range(NB):
                            v.wait_ge(sS, sSb + 8)
                            nbs = slice(nb * NBW, (nb + 1) * NBW)
                            for c in range(ECH):
                                v.tensor_sub(xA[:, nb, c, :], xA[:, nb, c, :],
                                             mrep[:, nbs])
                                ins = v.tensor_mul(xA[:, nb, c, :], xA[:, nb, c, :],
                                                   rrep[:, nbs])
                            ins.then_inc(vS, 1)
                            for c in range(ECH):
                                v.tensor_scalar(xsq[:, c, :], xA[:, nb, c, :],
                                                gN[:, c:c + 1], None, ALU.mult)
                            v.reduce_max(maxacc[:, nb, :],
                                         bass.AP(xsq, 0, [[ECH * NBW, 128],
                                                          [1, NBW], [NBW, ECH]]),
                                         axis=AX.X, apply_absolute_value=True
                                         ).then_inc(vS, 1)
                        for nb in range(NB):
                            t_off = tSb + 6 + 4 * nb
                            ps = psT if nb == 0 else psR0
                            v.wait_ge(tS, t_off + 4)
                            for c4 in range(4):
                                ins = v.reduce_max(mxTM[:, nb * 4 + c4:nb * 4 + c4 + 1],
                                                   ps[:, c4 * 128:(c4 + 1) * 128],
                                                   axis=AX.X)
                        ins.then_inc(vS, 1)
                        v.wait_ge(dW, dWb + 32)
                        mx = ROW(R_MX)
                        v.tensor_mul(t_, mx, r2)
                        v.tensor_scalar(ROW(R_RSTD), t_, EPS, None, ALU.max)
                        v.reciprocal(t_, ROW(R_RSTD))
                        v.tensor_mul(t_, t_, r2)
                        v.tensor_scalar(t2, t_, 127.0, None, ALU.mult
                                        ).then_inc(vS, 1)
                        for nb in range(NB):
                            v.wait_ge(sS, sSb + 12)
                            nbs = slice(nb * NBW, (nb + 1) * NBW)
                            for c in range(ECH):
                                v.tensor_scalar(xA[:, nb, c, :], xA[:, nb, c, :],
                                                gN[:, c:c + 1], None, ALU.mult)
                            for c in range(ECH):
                                ins = v.tensor_mul(xA[:, nb, c, :], xA[:, nb, c, :],
                                                   c1rep[:, nbs])
                            ins.then_inc(vS, 1)
                        for nb in range(NB):
                            v.wait_ge(sS, sSb + 13 + nb)
                            nbs = slice(nb * NBW, (nb + 1) * NBW)
                            v.tensor_scalar(xA[:, nb, :, :], xA[:, nb, :, :],
                                            -MAGIC, 127.0, ALU.add, ALU.min)
                            v.tensor_scalar(xq_bf[:, :, nbs], xA[:, nb, :, :],
                                            -128.0, None, ALU.max).then_inc(vS, 1)

                    @blk.gpsimd
                    def _(g):
                        g.wait_ge(sS, sSb + 11)
                        g.dma_start(rowsc[:, :], tmp8[:, :]).then_inc(dW, 16)
                        g.wait_ge(dW, dWb + 16)
                        g.dma_start(ROW(R_MX),
                                    rowsc.rearrange("a b -> (a b)")[None, 0:NTOK]
                                    ).then_inc(dW, 16)
                    C['dW'] = dWb + 32

                C['tS'] = tSb + 17
                C['sS'] = sSb + 14
                C['vS'] = vSb + 13

                if debug:
                    with nc.Block() as blk:
                        @blk.sync
                        def _(sy):
                            sy.dma_start(dbg["xn"], xn_b[:, :, :]).then_inc(dX, 16)
                            sy.dma_start(dbg["xq"], xq_bf[:, :, :]).then_inc(dX, 16)
                            sy.dma_start(dbg["rows"], rowt[:, :]).then_inc(dX, 16)
                            C['dX'] += 48
                            sy.wait_ge(dX, C['dX'])

            _sc.__exit__(None, None, None)
            _sc = nc.named_scope("PC"); _sc.__enter__()
            NSG = 88
            NSD = 44
            NST = NSG + NSD
            pc = ExitStack()
            with pc:
                sacc = pc.enter_context(nc.sbuf_tensor("sacc", [128, 160], F32))
                pbuf = pc.enter_context(nc.sbuf_tensor("pbuf", [128, 2, 2048], F32))
                a0 = pc.enter_context(nc.sbuf_tensor("a0", [128, H, NTOK], BF16))
                qw_b = pc.enter_context(nc.sbuf_tensor("qw_b", [128, 2, ECH, 128], BF16))
                kt_b = pc.enter_context(nc.sbuf_tensor("kt_b", [128, 2, MCH, 128], BF16))
                v_b = pc.enter_context(nc.sbuf_tensor("v_b", [128, 2, MCH, 128], BF16))
                q_b = pc.enter_context(nc.sbuf_tensor("q_b", [128, NB, NBW], BF16))
                eb = pc.enter_context(nc.sbuf_tensor("eb", [128, 3, 2 * NBW], BF16))
                psE2 = [pc.enter_context(nc.psum_tensor(f"psE2{i}", [128, 2 * NBW], F32))
                        for i in range(3)]
                psA = [pc.enter_context(nc.psum_tensor(f"psA{i}", [128, NBW], F32))
                       for i in range(2)]
                psQ = [psE2[2][:, 0:NBW], psE2[2][:, NBW:2 * NBW]]

                wq3 = wqT.rearrange("(c p) f -> p c f", p=128)
                v3 = vW.rearrange("(c p) f -> p c f", p=128)
                gw3s = gwT.rearrange("(c p) f -> p c f", p=128)
                dw3s = dwT.rearrange("(c p) e -> p c e", p=128)

                dWb, tSb, sSb, vSb = C['dW'], C['tS'], C['sS'], C['vS']
                p0b, p1b = C['dP0'], C['dP1']
                q0b, q1b = C['dQ0'], C['dQ1']

                def headload_cnt(h):
                    return (p0b if h % 2 == 0 else p1b) + 48 * (h // 2 + 1)

                def posE2(mc):
                    return 3 if mc == 0 else 2 * mc + 2

                def posA2(mc):
                    return 2 * mc + 5 if mc <= 30 else 66

                vsched = []
                for h in range(H):
                    vsched.append(("qe", h, 0))
                    vsched.append(("qe", h, 1))
                    for u in range((NST * h) // H, (NST * (h + 1)) // H):
                        vsched.append(("red", u))
                    vsched.append(("pae", h, 0))
                    vsched.append(("pae", h, 1))
                vpos = {ev: k + 1 for k, ev in enumerate(vsched)}

                with nc.Block() as blk:
                    @blk.gpsimd
                    def _(g):
                        for h in range(H):
                            if h >= 2:
                                g.wait_ge(tS, tSb + 66 * (h - 1))
                            p = h % 2
                            g.dma_start(qw_b[:, p, :, :],
                                        wq3[:, :, h * 128:(h + 1) * 128]).then_inc(dP[p], 16)
                            g.dma_start(kt_b[:, p, :, :],
                                        ktT[h * 128:(h + 1) * 128, :]
                                        .rearrange("p (c d) -> p c d", d=128)).then_inc(dP[p], 16)
                            g.dma_start(v_b[:, p, :, :],
                                        v3[:, :, h * 128:(h + 1) * 128]).then_inc(dP[p], 16)
                    C['dP0'] = p0b + 48 * (H // 2)
                    C['dP1'] = p1b + 48 * (H // 2)

                    @blk.sync
                    def _(sy):
                        for u in range(NST):
                            if u >= 2:
                                sy.wait_ge(vS, vSb + vpos[("red", u - 2)])
                            if u < NSG:
                                src = gw3s[:, :, u * 128:(u + 1) * 128]
                                dst = pbuf[:, u % 2, :].rearrange("p (c f) -> p c f", f=128)
                            else:
                                src = dw3s[:, u - NSG, :]
                                dst = pbuf[:, u % 2, :]
                            sy.dma_start(dst, src).then_inc(dQ[u % 2], 16)
                    C['dQ0'] = q0b + 16 * (NST // 2)
                    C['dQ1'] = q1b + 16 * (NST // 2)

                    @blk.tensor
                    def _(t):
                        for h in range(H):
                            tSh = tSb + 66 * h
                            sSh = sSb + 32 * h
                            p = h % 2
                            t.wait_ge(sS, sSh)
                            if h >= 1:
                                t.wait_ge(vS, vSb + vpos[("pae", h - 1, 1)])
                            t.wait_ge(dP[p], headload_cnt(h))
                            for nb in range(NB):
                                for c in range(ECH):
                                    ins = t.matmul(psQ[nb], qw_b[:, p, c, :],
                                                   xn_b[:, c, nb * NBW:(nb + 1) * NBW],
                                                   start=(c == 0), stop=(c == ECH - 1),
                                                   skip_group_check=True)
                                ins.then_inc(tS, 1)
                            for mc in range(MCH):
                                if mc >= 3:
                                    t.wait_ge(sS, sSh + mc - 2)
                                if mc == 0:
                                    t.wait_ge(vS, vSb + vpos[("qe", h, 1)])
                                if mc == 2:
                                    t.wait_ge(vS, vSb + vpos[("qe", h, 1)])
                                t.matmul(psE2[mc % 3][:, 0:NBW], kt_b[:, p, mc, :],
                                         q_b[:, 0, :], start=True, stop=True,
                                         skip_group_check=True)
                                t.matmul(psE2[mc % 3][:, NBW:2 * NBW], kt_b[:, p, mc, :],
                                         q_b[:, 1, :], start=True, stop=True,
                                         skip_group_check=True).then_inc(tS, 1)
                                if mc >= 1:
                                    amc = mc - 1
                                    t.wait_ge(sS, sSh + amc + 1)
                                    t.matmul(psA[0][:, :], v_b[:, p, amc, :],
                                             eb[:, amc % 3, 0:NBW], start=(amc == 0),
                                             stop=False, skip_group_check=True)
                                    t.matmul(psA[1][:, :], v_b[:, p, amc, :],
                                             eb[:, amc % 3, NBW:2 * NBW], start=(amc == 0),
                                             stop=False, skip_group_check=True
                                             ).then_inc(tS, 1)
                            amc = MCH - 1
                            t.wait_ge(sS, sSh + amc + 1)
                            t.matmul(psA[0][:, :], v_b[:, p, amc, :],
                                     eb[:, amc % 3, 0:NBW], start=False, stop=True,
                                     skip_group_check=True)
                            t.matmul(psA[1][:, :], v_b[:, p, amc, :],
                                     eb[:, amc % 3, NBW:2 * NBW], start=False, stop=True,
                                     skip_group_check=True).then_inc(tS, 1)
                    C['tS'] = tSb + 66 * H

                    @blk.scalar
                    def _(s):
                        for h in range(H):
                            tSh = tSb + 66 * h
                            for mc in range(MCH):
                                s.wait_ge(tS, tSh + posE2(mc))
                                s.activation(eb[:, mc % 3, :], psE2[mc % 3][:, :],
                                             AF.Silu, scale=SQRT128).then_inc(sS, 1)
                    C['sS'] = sSb + 32 * H

                    @blk.vector
                    def _(v):
                        for ev in vsched:
                            if ev[0] == "qe":
                                _, h, nb = ev
                                v.wait_ge(tS, tSb + 66 * h + nb + 1)
                                v.tensor_copy(q_b[:, nb, :], psQ[nb]).then_inc(vS, 1)
                            elif ev[0] == "pae":
                                _, h, nb = ev
                                v.wait_ge(tS, tSb + 66 * h + 66)
                                v.tensor_scalar(a0[:, h, nb * NBW:(nb + 1) * NBW],
                                                psA[nb][:, :], SQRT128, None, ALU.mult
                                                ).then_inc(vS, 1)
                            else:
                                u = ev[1]
                                v.wait_ge(dQ[u % 2],
                                          (q0b if u % 2 == 0 else q1b) + 16 * (u // 2 + 1))
                                col = u if u < NSG else 96 + (u - NSG)
                                v.reduce_sum(sacc[:, col:col + 1], pbuf[:, u % 2, :],
                                             axis=AX.X, apply_absolute_value=True
                                             ).then_inc(vS, 1)
                    C['vS'] = vSb + len(vsched)

                gw_b = pc.enter_context(nc.sbuf_tensor("gw_b", [128, 2, ECH, 128], BF16))
                gt = pc.enter_context(nc.sbuf_tensor("gt", [128, 2, NBW], BF16))
                wg3 = wgT.rearrange("(c p) f -> p c f", p=128)
                dWb, tSb, sSb, vSb = C['dW'], C['tS'], C['sS'], C['vS']
                p0b, p1b = C['dP0'], C['dP1']
                with nc.Block() as blk:
                    @blk.gpsimd
                    def _(g):
                        for cb in range(ECH):
                            if cb >= 2:
                                g.wait_ge(tS, tSb + cb - 1)
                            g.dma_start(gw_b[:, cb % 2, :, :],
                                        wg3[:, :, cb * 128:(cb + 1) * 128]
                                        ).then_inc(dP[cb % 2], 16)
                    C['dP0'] = p0b + 16 * (ECH // 2)
                    C['dP1'] = p1b + 16 * (ECH // 2)

                    @blk.tensor
                    def _(t):
                        for cb in range(ECH):
                            t.wait_ge(dP[cb % 2],
                                      (p0b if cb % 2 == 0 else p1b) + 16 * (cb // 2 + 1))
                            if cb >= 2:
                                t.wait_ge(sS, sSb + cb - 1)
                            for nb in range(NB):
                                for c in range(ECH):
                                    ins = t.matmul(psE2[cb % 2][:, nb * NBW:(nb + 1) * NBW],
                                                   gw_b[:, cb % 2, c, :],
                                                   xn_b[:, c, nb * NBW:(nb + 1) * NBW],
                                                   start=(c == 0), stop=(c == ECH - 1),
                                                   skip_group_check=True)
                            ins.then_inc(tS, 1)
                    C['tS'] = tSb + ECH

                    @blk.scalar
                    def _(s):
                        for cb in range(ECH):
                            s.wait_ge(tS, tSb + cb + 1)
                            if cb >= 2:
                                s.wait_ge(vS, vSb + cb - 1)
                            s.activation(eb[:, cb % 2, :], psE2[cb % 2][:, :],
                                         AF.Silu).then_inc(sS, 1)
                    C['sS'] = sSb + ECH

                    @blk.vector
                    def _(v):
                        for cb in range(ECH):
                            v.wait_ge(sS, sSb + cb + 1)
                            v.tensor_mul(a0[:, cb, :], a0[:, cb, :],
                                         eb[:, cb % 2, :]).then_inc(vS, 1)
                    C['vS'] = vSb + ECH

                if debug:
                    with nc.Block() as blk:
                        @blk.sync
                        def _(sy):
                            sy.dma_start(dbg["a0"], a0[:, :, :]).then_inc(dX, 16)
                            C['dX'] += 16
                            sy.wait_ge(dX, C['dX'])

                wo_b = gw_b
                wo3 = woT.rearrange("(c p) f -> p c f", p=128)
                dWb, tSb, sSb = C['dW'], C['tS'], C['sS']
                dSb_base = [C['dSa'], C['dSb']]
                dS_p = [dSa, dSb]
                p0b, p1b = C['dP0'], C['dP1']

                with nc.Block() as blk:
                    @blk.gpsimd
                    def _(g):
                        for cb in range(ECH):
                            if cb >= 2:
                                g.wait_ge(tS, tSb + cb - 1)
                            g.dma_start(wo_b[:, cb % 2, :, :],
                                        wo3[:, :, cb * 128:(cb + 1) * 128]
                                        ).then_inc(dP[cb % 2], 16)
                            if cb >= 1:
                                g.wait_ge(sS, sSb + cb)
                                g.dma_start(attS[:, cb - 1, :], eb[:, (cb - 1) % 2, :]
                                            ).then_inc(dS_p[(cb - 1) % 2], 16)
                        g.wait_ge(sS, sSb + ECH)
                        g.dma_start(attS[:, ECH - 1, :], eb[:, (ECH - 1) % 2, :]
                                    ).then_inc(dS_p[(ECH - 1) % 2], 16)
                    C['dP0'] = p0b + 16 * (ECH // 2)
                    C['dP1'] = p1b + 16 * (ECH // 2)
                    C['dSa'] = dSb_base[0] + 16 * (ECH // 2)
                    C['dSb'] = dSb_base[1] + 16 * (ECH // 2)

                    @blk.tensor
                    def _(t):
                        for cb in range(ECH):
                            t.wait_ge(dP[cb % 2],
                                      (p0b if cb % 2 == 0 else p1b) + 16 * (cb // 2 + 1))
                            if cb >= 2:
                                t.wait_ge(sS, sSb + cb - 1)
                            for nb in range(NB):
                                for vc in range(ECH):
                                    ins = t.matmul(psE2[cb % 2][:, nb * NBW:(nb + 1) * NBW],
                                                   wo_b[:, cb % 2, vc, :],
                                                   a0[:, vc, nb * NBW:(nb + 1) * NBW],
                                                   start=(vc == 0), stop=(vc == ECH - 1),
                                                   skip_group_check=True)
                            ins.then_inc(tS, 1)
                    C['tS'] = tSb + ECH

                    @blk.scalar
                    def _(s):
                        for cb in range(ECH):
                            s.wait_ge(tS, tSb + cb + 1)
                            if cb >= 2:
                                s.wait_ge(dS_p[cb % 2],
                                          dSb_base[cb % 2] + 16 * ((cb - 2) // 2 + 1))
                            s.copy(eb[:, cb % 2, :], psE2[cb % 2][:, :]).then_inc(sS, 1)
                    C['sS'] = sSb + ECH

                with nc.Block() as blk:
                    @blk.gpsimd
                    def _(g):
                        g.wait_ge(dSa, C['dSa'])
                        g.wait_ge(dSb, C['dSb'])

                if True:
                    psW = psE2[2]
                    psR128 = psE2[2]

                    with nc.Block() as blk:
                        @blk.vector
                        def _(v):
                            v.reduce_sum(sw128[:, 0:1], sacc[:, 0:NSG], axis=AX.X)
                            v.reduce_sum(sw128[:, 1:2], sacc[:, 96:96 + NSD], axis=AX.X)

                    with nc.Block() as blk:
                        @blk.tensor
                        def _(t):
                            t.matmul(psW[0:1, 0:2], ones[:, 0:1], sw128[:, 0:2],
                                     start=True, stop=True)

                    with nc.Block() as blk:
                        @blk.scalar
                        def _(s):
                            s.copy(wstat[0:1, 0:2], psW[0:1, 0:2])

                    with nc.Block() as blk:
                        @blk.vector
                        def _(v):
                            v.tensor_scalar(wstat[0:1, 2:3], wstat[0:1, 0:1],
                                            1.0 / GW_NUMEL, EPS, ALU.mult, ALU.max)
                            v.tensor_scalar(wstat[0:1, 4:5], wstat[0:1, 1:2],
                                            1.0 / DW_NUMEL, EPS, ALU.mult, ALU.max)
                            v.reciprocal(wstat[0:1, 6:7], wstat[0:1, 2:3])
                            v.reciprocal(wstat[0:1, 7:8], wstat[0:1, 4:5])

                    with nc.Block() as blk:
                        @blk.tensor
                        def _(t):
                            t.matmul(psR128[:, 0:2], ones[0:1, :], wstat[0:1, 6:8],
                                     start=True, stop=True)

                    with nc.Block() as blk:
                        @blk.scalar
                        def _(s):
                            s.copy(sw128[:, 0:2], psR128[:, 0:2])

                if debug:
                    with nc.Block() as blk:
                        @blk.sync
                        def _(sy):
                            sy.dma_start(dbg["att"], attS).then_inc(dX, 16)
                            sy.dma_start(dbg["wstat"], wstat[0:1, :]).then_inc(dX, 16)
                            C['dX'] += 32
                            sy.wait_ge(dX, C['dX'])

        _sc.__exit__(None, None, None)
        _sc = nc.named_scope("PD01"); _sc.__enter__()
        pd0 = ExitStack()
        with pd0:
            h_b = pd0.enter_context(nc.sbuf_tensor("h_b", [128, ICH, NTOK], BF16))
            c2rep = pd0.enter_context(nc.sbuf_tensor("c2rep", [128, NTOK], F32))
            d2rep = pd0.enter_context(nc.sbuf_tensor("d2rep", [128, NTOK], F32))

            pd1 = ExitStack()
            with pd1:
                cgrep = d2rep
                gwf = pd1.enter_context(nc.sbuf_tensor("gwf", [128, 2, ECH, 128], F32))
                wqg = pd1.enter_context(nc.sbuf_tensor("wqg", [128, 2, ECH, 128], BF16))
                tg = pd1.enter_context(nc.sbuf_tensor("tg", [128, NBW], F32))
                tu = pd1.enter_context(nc.sbuf_tensor("tu", [128, NBW], F32))
                sg = pd1.enter_context(nc.sbuf_tensor("sg", [128, NBW], F32))
                tmv = tg
                mxv = pd1.enter_context(nc.sbuf_tensor("mxv", [128, 2, NBW], F32))
                mnv = pd1.enter_context(nc.sbuf_tensor("mnv", [128, 2, NBW], F32))
                mxTM2 = pd1.enter_context(nc.sbuf_tensor("mxTM2", [128, 8], F32))
                hsq = pd1.enter_context(nc.sbuf_tensor("hsq", [128, 2, 2, NBW], F32))

                cgs = ExitStack()
                with cgs:
                    psCa = cgs.enter_context(nc.psum_tensor("psCa", [128, NBW], F32))
                    psCb = cgs.enter_context(nc.psum_tensor("psCb", [128, NBW], F32))
                    with nc.Block() as blk:
                        @blk.vector
                        def _(v):
                            v.tensor_scalar(ROW(R_T3), ROW(R_RSTD),
                                            wstat[0:1, 2:3], 1.0 / 127.0,
                                            ALU.mult, ALU.mult)

                    with nc.Block() as blk:
                        @blk.tensor
                        def _(t):
                            t.matmul(psCa[:, :], ones[0:1, :], ROW(R_T3, slice(0, NBW)),
                                     start=True, stop=True)
                            t.matmul(psCb[:, :], ones[0:1, :], ROW(R_T3, slice(NBW, NTOK)),
                                     start=True, stop=True)

                    with nc.Block() as blk:
                        @blk.scalar
                        def _(s):
                            s.copy(cgrep[:, 0:NBW], psCa[:, :])
                            s.copy(cgrep[:, NBW:NTOK], psCb[:, :])

                gw3b = gwT.rearrange("(c p) f -> p c f", p=128)
                dXb, tSb, sSb, vSb = C['dX'], C['tS'], C['sS'], C['vS']
                NJ = ICH
                NS = 2 * NJ
                q0b, q1b = C['dQ0'], C['dQ1']

                vsched = [("vq", 0), ("vq", 1)]
                for j in range(NJ):
                    for s2 in (2 * j + 2, 2 * j + 3):
                        if s2 < NS:
                            vsched.append(("vq", s2))
                    for nb in range(NB):
                        vsched.append(("v1", j, nb))
                        vsched.append(("v2", j, nb))
                vpos = {ev: k + 1 for k, ev in enumerate(vsched)}

                def vq_cnt(s):
                    return vSb + vpos[("vq", s)]

                def v1_cnt(j, nb):
                    return vSb + vpos[("v1", j, nb)]

                def v2_cnt(j, nb):
                    return vSb + vpos[("v2", j, nb)]

                ssched = [("p3", 0), ("p3", 1)]
                for j in range(NJ):
                    for s2 in (2 * j + 2, 2 * j + 3):
                        if s2 < NS:
                            ssched.append(("p3", s2))
                    for nb in range(NB):
                        ssched.append(("silu", j, nb))
                spos = {ev: k + 1 for k, ev in enumerate(ssched)}

                def p3_cnt(s):
                    return sSb + spos[("p3", s)]

                def silu_cnt(j, nb):
                    return sSb + spos[("silu", j, nb)]

                def t1_cnt(s, nb):
                    j, hf = s // 2, s % 2
                    return tSb + 4 * j + (1 + nb if hf == 0 else 3 + nb)

                def fcol(s):
                    j, hf = s // 2, s % 2
                    return (j if hf == 0 else ICH + j) * 128

                lps = ExitStack()
                with lps:
                    psG = [[lps.enter_context(nc.psum_tensor(f"psG{p}{n}", [128, NBW], F32))
                            for n in range(2)] for p in range(2)]
                    psU = [[lps.enter_context(nc.psum_tensor(f"psU{p}{n}", [128, NBW], F32))
                            for n in range(2)] for p in range(2)]

                    with nc.Block() as blk:
                        @blk.sync
                        def _(sy):
                            for s_ in range(NS):
                                if s_ >= 2:
                                    sy.wait_ge(sS, p3_cnt(s_ - 2))
                                sy.dma_start(gwf[:, s_ % 2, :, :],
                                             gw3b[:, :, fcol(s_):fcol(s_) + 128]
                                             ).then_inc(dQ[s_ % 2], 16)
                        C['dQ0'] = q0b + 16 * (NS // 2)
                        C['dQ1'] = q1b + 16 * (NS // 2)

                        @blk.vector
                        def _(v):
                            v.memset(mxv[:, 0, :], 0.0)
                            v.memset(mxv[:, 1, :], 0.0)
                            v.memset(mnv[:, 0, :], 0.0)
                            v.memset(mnv[:, 1, :], 0.0)
                            for ev in vsched:
                                if ev[0] == "vq":
                                    s_ = ev[1]
                                    v.wait_ge(dQ[s_ % 2],
                                              (q0b if s_ % 2 == 0 else q1b)
                                              + 16 * (s_ // 2 + 1))
                                    v.tensor_scalar(gwf[:, s_ % 2, :, :],
                                                    gwf[:, s_ % 2, :, :],
                                                    sw128[:, 0:1], 1.0, ALU.mult, ALU.min)
                                    v.tensor_scalar(gwf[:, s_ % 2, :, :],
                                                    gwf[:, s_ % 2, :, :],
                                                    -1.0, MAGIC, ALU.max, ALU.add
                                                    ).then_inc(vS, 1)
                                elif ev[0] == "v1":
                                    _, j, nb = ev
                                    jp = j % 2
                                    v.wait_ge(tS, t1_cnt(2 * j + 1, nb))
                                    v.tensor_mul(tg[:, :], psG[jp][nb][:, :],
                                                 cgrep[:, nb * NBW:(nb + 1) * NBW])
                                    v.tensor_mul(tu[:, :], psU[jp][nb][:, :],
                                                 cgrep[:, nb * NBW:(nb + 1) * NBW]
                                                 ).then_inc(vS, 1)
                                else:
                                    _, j, nb = ev
                                    v.wait_ge(sS, silu_cnt(j, nb))
                                    hsl = h_b[:, j, nb * NBW:(nb + 1) * NBW]
                                    v.tensor_mul(hsl, sg[:, :], tu[:, :])
                                    v.tensor_scalar(tmv[:, :], hsl, dN[:, j:j + 1],
                                                    None, ALU.mult)
                                    v.tensor_tensor(mxv[:, nb, :], mxv[:, nb, :],
                                                    tmv[:, :], ALU.max)
                                    v.tensor_tensor(mnv[:, nb, :], mnv[:, nb, :],
                                                    tmv[:, :], ALU.min
                                                    ).then_inc(vS, 1)
                        C['vS'] = vSb + len(vsched)

                        @blk.scalar
                        def _(s):
                            for ev in ssched:
                                if ev[0] == "p3":
                                    s_ = ev[1]
                                    s.wait_ge(vS, vq_cnt(s_))
                                    if s_ >= 2:
                                        s.wait_ge(tS, t1_cnt(s_ - 2, 1))
                                    s.activation(wqg[:, s_ % 2, :, :],
                                                 gwf[:, s_ % 2, :, :],
                                                 AF.Copy, bias=-MAGIC).then_inc(sS, 1)
                                else:
                                    _, j, nb = ev
                                    s.wait_ge(vS, v1_cnt(j, nb))
                                    if nb == 0 and j >= 1:
                                        s.wait_ge(vS, v2_cnt(j - 1, 1))
                                    s.activation(sg[:, :], tg[:, :], AF.Silu
                                                 ).then_inc(sS, 1)
                        C['sS'] = sSb + len(ssched)

                        @blk.tensor
                        def _(t):
                            for s_ in range(NS):
                                j, hf = s_ // 2, s_ % 2
                                jp = j % 2
                                t.wait_ge(sS, p3_cnt(s_))
                                ps = psG[jp] if hf == 0 else psU[jp]
                                for nb in range(NB):
                                    if j >= 2:
                                        t.wait_ge(vS, v1_cnt(j - 2, nb))
                                    for c in range(ECH):
                                        ins = t.matmul(ps[nb][:, :], wqg[:, s_ % 2, c, :],
                                                       xq_bf[:, c, nb * NBW:(nb + 1) * NBW],
                                                       start=(c == 0), stop=(c == ECH - 1),
                                                       skip_group_check=True)
                                    ins.then_inc(tS, 1)
                        C['tS'] = tSb + 4 * NJ

                _sc.__exit__(None, None, None)
                _sc = nc.named_scope("PD1fin"); _sc.__enter__()
                fin = ExitStack()
                with fin:
                    psSh = [fin.enter_context(nc.psum_tensor(f"psSh{i}", [1, NBW], F32))
                            for i in range(2)]
                    psFa = fin.enter_context(nc.psum_tensor("psFa", [128, NBW], F32))
                    psFb = fin.enter_context(nc.psum_tensor("psFb", [128, NBW], F32))
                    psFc = fin.enter_context(nc.psum_tensor("psFc", [8, 128], F32))
                    psRa = fin.enter_context(nc.psum_tensor("psRa2", [128, NBW], F32))
                    psRb = fin.enter_context(nc.psum_tensor("psRb2", [128, NBW], F32))

                    sSb2, tSb2 = C['sS'], C['tS']
                    with nc.Block() as blk:
                        @blk.scalar
                        def _(s):
                            for j in range(ICH):
                                for nb in range(NB):
                                    if j >= 2:
                                        s.wait_ge(tS, tSb2 + 2 * (j - 2) + nb + 1)
                                    s.activation(hsq[:, j % 2, nb, :],
                                                 h_b[:, j, nb * NBW:(nb + 1) * NBW],
                                                 AF.Square).then_inc(sS, 1)

                        @blk.tensor
                        def _(t):
                            for j in range(ICH):
                                for nb in range(NB):
                                    t.wait_ge(sS, sSb2 + 2 * j + nb + 1)
                                    t.matmul(psSh[nb][:, :], ones[:, 0:1],
                                             hsq[:, j % 2, nb, :],
                                             start=(j == 0), stop=(j == ICH - 1),
                                             skip_group_check=True).then_inc(tS, 1)
                    C['sS'] = sSb2 + 2 * ICH
                    C['tS'] = tSb2 + 2 * ICH

                    with nc.Block() as blk:
                        @blk.scalar
                        def _(s):
                            s.copy(ROW(R_T, slice(0, NBW)), psSh[0][:, :])
                            s.copy(ROW(R_T, slice(NBW, NTOK)), psSh[1][:, :])

                    with nc.Block() as blk:
                        @blk.vector
                        def _(v):
                            for nb in range(NB):
                                v.tensor_scalar(mnv[:, nb, :], mnv[:, nb, :], -1.0,
                                                None, ALU.mult)
                                v.tensor_tensor(mxv[:, nb, :], mxv[:, nb, :],
                                                mnv[:, nb, :], ALU.max)

                    with nc.Block() as blk:
                        @blk.tensor
                        def _(t):
                            for nb in range(NB):
                                ps = psFa if nb == 0 else psFb
                                for c4 in range(4):
                                    t.transpose(ps[:, c4 * 128:(c4 + 1) * 128],
                                                mxv[:, nb, c4 * 128:(c4 + 1) * 128],
                                                ident[:, :])

                    with nc.Block() as blk:
                        @blk.vector
                        def _(v):
                            for nb in range(NB):
                                ps = psFa if nb == 0 else psFb
                                for c4 in range(4):
                                    v.reduce_max(mxTM2[:, nb * 4 + c4:nb * 4 + c4 + 1],
                                                 ps[:, c4 * 128:(c4 + 1) * 128], axis=AX.X)

                    with nc.Block() as blk:
                        @blk.tensor
                        def _(t):
                            t.transpose(psFc[:, :], mxTM2[:, 0:8], ident[:, :])

                    with nc.Block() as blk:
                        @blk.scalar
                        def _(s):
                            s.copy(tmp8[:, :], psFc[:, :])

                    with nc.Block() as blk:
                        @blk.gpsimd
                        def _(g):
                            g.dma_start(rowsc[:, :], tmp8[:, :]).then_inc(dW, 16)
                            C['dW'] += 16
                            g.wait_ge(dW, C['dW'])
                            g.dma_start(ROW(R_MX),
                                        rowsc.rearrange("a b -> (a b)")[None, 0:NTOK]
                                        ).then_inc(dW, 16)
                            C['dW'] += 16
                            g.wait_ge(dW, C['dW'])

                    with nc.Block() as blk:
                        @blk.vector
                        def _(v):
                            hs = ROW(R_T)
                            v.tensor_scalar(hs, hs, 1.0 / 5632.0, EPS, ALU.mult, ALU.add)

                    with nc.Block() as blk:
                        @blk.scalar
                        def _(s):
                            s.activation(ROW(R_T2), ROW(R_T), AF.Sqrt)

                    with nc.Block() as blk:
                        @blk.vector
                        def _(v):
                            w = ROW(R_T)
                            r = ROW(R_T2)
                            t3 = ROW(R_T3)
                            mx = ROW(R_MX)
                            v.reciprocal(r, r)
                            v.tensor_mul(t3, r, r)
                            v.tensor_mul(t3, t3, w)
                            v.tensor_scalar(t3, t3, -0.5, 1.5, ALU.mult, ALU.add)
                            v.tensor_mul(r, r, t3)
                            v.tensor_mul(t3, mx, r)
                            v.tensor_scalar(t3, t3, EPS, None, ALU.max)
                            v.reciprocal(w, t3)
                            v.tensor_mul(w, w, r)
                            v.tensor_scalar(ROW(R_T), w, 127.0, None, ALU.mult)
                            v.tensor_scalar(t3, t3, wstat[0:1, 4:5], 1.0 / 127.0,
                                            ALU.mult, ALU.mult)

                    with nc.Block() as blk:
                        @blk.tensor
                        def _(t):
                            t.matmul(psRa[:, :], ones[0:1, :], ROW(R_T, slice(0, NBW)),
                                     start=True, stop=True)
                            t.matmul(psRb[:, :], ones[0:1, :], ROW(R_T, slice(NBW, NTOK)),
                                     start=True, stop=True)
                            t.matmul(psFa[:, :], ones[0:1, :], ROW(R_T3, slice(0, NBW)),
                                     start=True, stop=True)
                            t.matmul(psFb[:, :], ones[0:1, :], ROW(R_T3, slice(NBW, NTOK)),
                                     start=True, stop=True)

                    with nc.Block() as blk:
                        @blk.scalar
                        def _(s):
                            s.copy(c2rep[:, 0:NBW], psRa[:, :])
                            s.copy(c2rep[:, NBW:NTOK], psRb[:, :])
                            s.copy(d2rep[:, 0:NBW], psFa[:, :])
                            s.copy(d2rep[:, NBW:NTOK], psFb[:, :])

                if debug:
                    with nc.Block() as blk:
                        @blk.sync
                        def _(sy):
                            sy.dma_start(dbg["h"], h_b[:, :, :]).then_inc(dX, 16)
                            sy.dma_start(dbg["wq"], wqg[:, :, :, :]).then_inc(dX, 16)
                            sy.dma_start(dbg["gwf"], gwf[:, :, :, :]).then_inc(dX, 16)
                            sy.dma_start(dbg["sw"], sw128[:, :]).then_inc(dX, 16)
                            C['dX'] += 64
                            sy.wait_ge(dX, C['dX'])

            sxq.close()

            _sc.__exit__(None, None, None)
            _sc = nc.named_scope("PD2"); _sc.__enter__()
            pd2 = ExitStack()
            with pd2:
                hq_b = pd2.enter_context(nc.sbuf_tensor("hq_b", [128, ICH, NBW], BF16))
                tqa = pd2.enter_context(nc.sbuf_tensor("tqa", [128, 3, NBW], F32))
                tqb = pd2.enter_context(nc.sbuf_tensor("tqb", [128, 2, NBW], F32))
                dwf = pd2.enter_context(nc.sbuf_tensor("dwf", [128, 3, 8, 128], F32))
                dwq = pd2.enter_context(nc.sbuf_tensor("dwq", [128, 3, 8, 128], BF16))
                xtile = pd2.enter_context(nc.sbuf_tensor("xtile", [128, 2, NBW], F32))
                atile = pd2.enter_context(nc.sbuf_tensor("atile", [128, 2, NBW], BF16))
                otile = pd2.enter_context(nc.sbuf_tensor("otile", [128, 2, NBW], F32))
                psD = [pd2.enter_context(nc.psum_tensor(f"psD{i}", [128, NBW], F32))
                       for i in range(8)]

                for ng in range(NB):
                    ngs = slice(ng * NBW, (ng + 1) * NBW)

                    for eg in range(2):
                        qb4 = [C[f'dQ{i}'] for i in range(4)]
                        vb, sb, gb, tb = C['vS'], C['sS'], C['gS'], C['tS']

                        if eg == 0:
                            vsc = [("A", 0), ("A", 1)]
                            ssc = []
                            for i in range(ICH):
                                vsc.append(("w", i))
                                if i + 2 < ICH:
                                    vsc.append(("A", i + 2))
                                vsc.append(("B", i))
                                ssc.append(("p3", i))
                                ssc.append(("ngq", i))
                            vp2 = {ev: k + 1 for k, ev in enumerate(vsc)}
                            sp2 = {ev: k + 1 for k, ev in enumerate(ssc)}
                        else:
                            vsc = [("w", i) for i in range(ICH)]
                            ssc = [("p3", i) for i in range(ICH)]
                            vp2 = {ev: k + 1 for k, ev in enumerate(vsc)}
                            sp2 = {ev: k + 1 for k, ev in enumerate(ssc)}

                        with nc.Block() as blk:
                            @blk.sync
                            def _(sy, sb=sb, eg=eg, sp2=sp2):
                                for i in range(ICH):
                                    if i >= 3:
                                        sy.wait_ge(sS, sb + sp2[("p3", i - 3)])
                                    src = dwT[i * 128:(i + 1) * 128,
                                              eg * 1024:(eg + 1) * 1024].rearrange(
                                        "p (c e) -> p c e", e=128)
                                    sy.dma_start(dwf[:, i % 3, :, :], src
                                                 ).then_inc(dQ[i % 3], 16)
                            for i in range(3):
                                C[f'dQ{i}'] = qb4[i] + 16 * len(range(i, ICH, 3))

                            @blk.vector
                            def _(v, qb4=qb4, vb=vb, sb=sb, eg=eg, ngs=ngs,
                                  vsc=vsc, sp2=sp2):
                                for ev in vsc:
                                    kind, i = ev
                                    if kind == "w":
                                        v.wait_ge(dQ[i % 3], qb4[i % 3] + 16 * (i // 3 + 1))
                                        v.tensor_scalar(dwf[:, i % 3, :, :],
                                                        dwf[:, i % 3, :, :],
                                                        sw128[:, 1:2], 1.0,
                                                        ALU.mult, ALU.min)
                                        v.tensor_scalar(dwf[:, i % 3, :, :],
                                                        dwf[:, i % 3, :, :],
                                                        -1.0, MAGIC, ALU.max, ALU.add
                                                        ).then_inc(vS, 1)
                                    elif kind == "A":
                                        if i >= 3:
                                            v.wait_ge(sS, sb + sp2[("ngq", i - 3)])
                                        v.tensor_scalar(tqa[:, i % 3, :], h_b[:, i, ngs],
                                                        dN[:, i:i + 1], None, ALU.mult)
                                        v.tensor_mul(tqa[:, i % 3, :], tqa[:, i % 3, :],
                                                     c2rep[:, ngs]).then_inc(vS, 1)
                                    else:
                                        v.wait_ge(sS, sb + sp2[("ngq", i)])
                                        v.tensor_scalar(tqb[:, i % 2, :], tqb[:, i % 2, :],
                                                        -MAGIC, 127.0, ALU.add, ALU.min)
                                        v.tensor_scalar(hq_b[:, i, :], tqb[:, i % 2, :],
                                                        -128.0, None, ALU.max
                                                        ).then_inc(vS, 1)
                            C['vS'] = vb + len(vsc)

                            @blk.scalar
                            def _(s, vb=vb, sb=sb, tb=tb, ssc=ssc, vp2=vp2):
                                for ev in ssc:
                                    kind, i = ev
                                    if kind == "p3":
                                        s.wait_ge(vS, vb + vp2[("w", i)])
                                        if i >= 3:
                                            s.wait_ge(tS, tb + i - 2)
                                        s.activation(dwq[:, i % 3, :, :],
                                                     dwf[:, i % 3, :, :],
                                                     AF.Copy, bias=-MAGIC).then_inc(sS, 1)
                                    else:
                                        s.wait_ge(vS, vb + vp2[("A", i)])
                                        s.activation(tqb[:, i % 2, :], tqa[:, i % 3, :],
                                                     AF.Copy, bias=MAGIC).then_inc(sS, 1)
                            C['sS'] = sb + len(ssc)

                            @blk.tensor
                            def _(t, sb=sb, vb=vb, eg=eg, sp2=sp2, vp2=vp2):
                                for i in range(ICH):
                                    t.wait_ge(sS, sb + sp2[("p3", i)])
                                    if eg == 0:
                                        t.wait_ge(vS, vb + vp2[("B", i)])
                                    for ec in range(8):
                                        ins = t.matmul(psD[ec][:, :], dwq[:, i % 3, ec, :],
                                                       hq_b[:, i, :], start=(i == 0),
                                                       stop=(i == ICH - 1),
                                                       skip_group_check=True)
                                    ins.then_inc(tS, 1)
                            C['tS'] = tb + ICH

                        q0b, q1b = C['dQ0'], C['dQ1']
                        x0b, x1b = C['dXs0'], C['dXs1']
                        vb = C['vS']
                        ev_seq = []
                        for ec in range(8):
                            ev_seq.append(("L", ec))
                            if ec >= 2:
                                ev_seq.append(("S", ec - 2))
                        ev_seq += [("S", 6), ("S", 7)]
                        with nc.Block() as blk:
                            @blk.sync
                            def _(sy, vb=vb, ng=ng, eg=eg, ngs=ngs, ev_seq=ev_seq):
                                for kind, ec in ev_seq:
                                    row = (eg * 8 + ec) * 128
                                    if kind == "L":
                                        if ec >= 2:
                                            sy.wait_ge(vS, vb + ec - 1)
                                        sy.dma_start(xtile[:, ec % 2, :],
                                                     xT[row:row + 128, ngs]
                                                     ).then_inc(dQ[ec % 2], 16)
                                        sy.dma_start(atile[:, ec % 2, :],
                                                     attS[:, eg * 8 + ec, ngs]
                                                     ).then_inc(dQ[ec % 2], 16)
                                    else:
                                        sy.wait_ge(vS, vb + ec + 1)
                                        sy.dma_start(outT[row:row + 128, ngs],
                                                     otile[:, ec % 2, :]
                                                     ).then_inc(dXs[ec % 2], 16)
                            C['dQ0'] = q0b + 32 * 4
                            C['dQ1'] = q1b + 32 * 4
                            C['dXs0'] = x0b + 16 * 4
                            C['dXs1'] = x1b + 16 * 4

                            @blk.vector
                            def _(v, q0b=q0b, q1b=q1b, x0b=x0b, x1b=x1b, vb=vb, ngs=ngs):
                                for ec in range(8):
                                    p = ec % 2
                                    v.wait_ge(dQ[p], (q0b if p == 0 else q1b)
                                              + 32 * (ec // 2 + 1))
                                    if ec >= 2:
                                        v.wait_ge(dXs[p], (x0b if p == 0 else x1b)
                                                  + 16 * (ec // 2))
                                    v.tensor_mul(otile[:, p, :], psD[ec][:, :],
                                                 d2rep[:, ngs])
                                    v.tensor_add(otile[:, p, :], otile[:, p, :],
                                                 atile[:, p, :])
                                    v.tensor_add(otile[:, p, :], otile[:, p, :],
                                                 xtile[:, p, :]).then_inc(vS, 1)
                            C['vS'] = vb + 8

        _sc.__exit__(None, None, None)
        with nc.Block() as blk:
            @blk.sync
            def _(sy):
                sy.wait_ge(dX, C['dX'])
                sy.wait_ge(dSa, C['dSa'])
                sy.wait_ge(dSb, C['dSb'])
                sy.wait_ge(dQ[0], C['dQ0'])
                sy.wait_ge(dQ[1], C['dQ1'])
                sy.wait_ge(dQ[2], C['dQ2'])
                sy.wait_ge(dQ[3], C['dQ3'])
                sy.wait_ge(dP[0], C['dP0'])
                sy.wait_ge(dP[1], C['dP1'])
                sy.wait_ge(dXs[0], C['dXs0'])
                sy.wait_ge(dXs[1], C['dXs1'])

            @blk.gpsimd
            def _(g):
                g.wait_ge(dW, C['dW'])

    return nc


_CACHED = {}


def _get_nc(debug):
    key = bool(debug)
    if key not in _CACHED:
        _CACHED[key] = build_core_program(debug=key)
    return _CACHED[key]


def kernel(L0_attn_x, L0_attn_Wq, L0_attn_k_weight, L0_attn_v_weight, L0_attn_Wg,
           L0_attn_Wout, L0_ffn_gate_w, L0_ffn_down_w, L0_ffn_gate_norm_w,
           L0_ffn_down_norm_w):
    x = np.asarray(L0_attn_x)
    B, N, E = x.shape
    xt = np.ascontiguousarray(x.reshape(N, E).T)

    wq = np.ascontiguousarray(np.asarray(L0_attn_Wq).T)
    kt = np.ascontiguousarray(np.asarray(L0_attn_k_weight).T)
    vw = np.ascontiguousarray(np.asarray(L0_attn_v_weight))
    wg = np.ascontiguousarray(np.asarray(L0_attn_Wg).T)
    wo = np.ascontiguousarray(np.asarray(L0_attn_Wout).T)
    gw = np.ascontiguousarray(np.asarray(L0_ffn_gate_w).T)
    dw = np.ascontiguousarray(np.asarray(L0_ffn_down_w).T)
    gn = np.ascontiguousarray(np.asarray(L0_ffn_gate_norm_w).reshape(16, 128).T)
    dn = np.ascontiguousarray(np.asarray(L0_ffn_down_norm_w).reshape(44, 128).T)

    cns = np.zeros((128, 16), np.float32)
    cns[:, 0] = EPS
    cns[:, 1] = SQRT128
    onesf = np.ones((128, 128), np.float32)
    identf = np.eye(128, dtype=np.float32)

    nc = _get_nc(DEBUG)
    in_maps = []
    for c in range(8):
        in_maps.append(dict(
            xT=np.ascontiguousarray(xt[:, c * NTOK:(c + 1) * NTOK]),
            wqT=wq, ktT=kt, vW=vw, wgT=wg, woT=wo, gwT=gw, dwT=dw,
            cns=cns, onesf=onesf, identf=identf, gN128=gn, dN128=dn,
        ))
    res = run_bass_kernel_spmd(nc, in_maps, core_ids=list(range(8)))
    out = np.concatenate([r["outT"] for r in res.results], axis=1)
    if DEBUG:
        kernel._dbg = res.results
    return np.ascontiguousarray(out.T).reshape(B, N, E).astype(np.float32)


# version 15
# speedup vs baseline: 1.0489x; 1.0188x over previous
import numpy as np
import ml_dtypes
from contextlib import ExitStack
import concourse.bass as bass
import concourse.mybir as mybir
from concourse.bass_utils import run_bass_kernel_spmd

F32 = mybir.dt.float32
BF16 = mybir.dt.bfloat16
AF = mybir.ActivationFunctionType
ALU = mybir.AluOpType
AX = mybir.AxisListType

MAGIC = 12582912.0
SQRT128 = float(np.float32(np.sqrt(np.float32(128.0))))
EPS = 1e-5
NTOK = 1024
NBW = 512
NB = 2
ECH = 16
H = 16
MCH = 32
ICH = 44
GW_NUMEL = 11264 * 2048
DW_NUMEL = 2048 * 5632

DEBUG = False
PD1_SERIAL = False


def build_core_program(debug=False):
    nc = bass.Bass("TRN2", target_bir_lowering=False)

    xT = nc.dram_tensor("xT", [2048, NTOK], F32, kind="ExternalInput").ap()
    wqT = nc.dram_tensor("wqT", [2048, 2048], F32, kind="ExternalInput").ap()
    ktT = nc.dram_tensor("ktT", [2048, 4096], F32, kind="ExternalInput").ap()
    vW = nc.dram_tensor("vW", [4096, 2048], F32, kind="ExternalInput").ap()
    wgT = nc.dram_tensor("wgT", [2048, 2048], F32, kind="ExternalInput").ap()
    woT = nc.dram_tensor("woT", [2048, 2048], F32, kind="ExternalInput").ap()
    gwT = nc.dram_tensor("gwT", [2048, 11264], F32, kind="ExternalInput").ap()
    dwT = nc.dram_tensor("dwT", [5632, 2048], F32, kind="ExternalInput").ap()
    cnsI = nc.dram_tensor("cns", [128, 16], F32, kind="ExternalInput").ap()
    onesI = nc.dram_tensor("onesf", [128, 128], F32, kind="ExternalInput").ap()
    identI = nc.dram_tensor("identf", [128, 128], F32, kind="ExternalInput").ap()
    gNI = nc.dram_tensor("gN128", [128, 16], F32, kind="ExternalInput").ap()
    dNI = nc.dram_tensor("dN128", [128, 44], F32, kind="ExternalInput").ap()

    outT = nc.dram_tensor("outT", [2048, NTOK], F32, kind="ExternalOutput").ap()
    attS = nc.dram_tensor("attS", [128, 16, NTOK], BF16, kind="Internal").ap()
    rowsc = nc.dram_tensor("rowsc", [8, 128], F32, kind="Internal").ap()

    dbg = {}
    if debug:
        dbg["xn"] = nc.dram_tensor("dbg_xn", [128, ECH, NTOK], BF16, kind="ExternalOutput").ap()
        dbg["xq"] = nc.dram_tensor("dbg_xq", [128, ECH, NTOK], BF16, kind="ExternalOutput").ap()
        dbg["a0"] = nc.dram_tensor("dbg_a0", [128, H, NTOK], BF16, kind="ExternalOutput").ap()
        dbg["att"] = nc.dram_tensor("dbg_att", [128, 16, NTOK], BF16, kind="ExternalOutput").ap()
        dbg["h"] = nc.dram_tensor("dbg_h", [128, ICH, NTOK], BF16, kind="ExternalOutput").ap()
        dbg["rows"] = nc.dram_tensor("dbg_rows", [1, 6 * NTOK], F32, kind="ExternalOutput").ap()
        dbg["wstat"] = nc.dram_tensor("dbg_wstat", [1, 8], F32, kind="ExternalOutput").ap()
        dbg["xq2"] = nc.dram_tensor("dbg_xq2", [128, ECH, NTOK], BF16, kind="ExternalOutput").ap()
        dbg["cg"] = nc.dram_tensor("dbg_cg", [128, NTOK], F32, kind="ExternalOutput").ap()
        dbg["wq"] = nc.dram_tensor("dbg_wq", [128, 2, ECH, 128], BF16, kind="ExternalOutput").ap()
        dbg["gwf"] = nc.dram_tensor("dbg_gwf", [128, 2, ECH, 128], F32, kind="ExternalOutput").ap()
        dbg["sw"] = nc.dram_tensor("dbg_sw", [128, 2], F32, kind="ExternalOutput").ap()

    st = ExitStack()
    with st:
        cns = st.enter_context(nc.sbuf_tensor("cns_s", [128, 16], F32))
        ones = st.enter_context(nc.sbuf_tensor("ones_s", [128, 128], F32))
        ident = st.enter_context(nc.sbuf_tensor("ident_s", [128, 128], F32))
        gN = st.enter_context(nc.sbuf_tensor("gN_s", [128, 16], F32))
        dN = st.enter_context(nc.sbuf_tensor("dN_s", [128, 44], F32))
        rowt = st.enter_context(nc.sbuf_tensor("rows_s", [1, 6 * NTOK], F32))
        wstat = st.enter_context(nc.sbuf_tensor("wstat_s", [1, 8], F32))
        sw128 = st.enter_context(nc.sbuf_tensor("sw128_s", [128, 2], F32))
        tmp8 = st.enter_context(nc.sbuf_tensor("tmp8_s", [8, 128], F32))

        def ROW(r, sl=None):
            sl = sl if sl is not None else slice(0, NTOK)
            return rowt[0:1, r * NTOK + sl.start: r * NTOK + sl.stop]

        sxq = ExitStack()
        xq_bf = sxq.enter_context(nc.sbuf_tensor("xq_bf", [128, ECH, NTOK], BF16,
                                                 side="right"))

        dW = st.enter_context(nc.semaphore("dW"))
        dX = st.enter_context(nc.semaphore("dX"))
        tS = st.enter_context(nc.semaphore("tS"))
        sS = st.enter_context(nc.semaphore("sS"))
        vS = st.enter_context(nc.semaphore("vS"))
        gS = st.enter_context(nc.semaphore("gS"))
        dSa = st.enter_context(nc.semaphore("dSa"))
        dSb = st.enter_context(nc.semaphore("dSb"))
        dQ = [st.enter_context(nc.semaphore(f"dQ{i}")) for i in range(4)]
        dP = [st.enter_context(nc.semaphore(f"dP{i}")) for i in range(2)]
        dXs = [st.enter_context(nc.semaphore(f"dXs{i}")) for i in range(2)]
        C = dict(dW=0, dX=0, tS=0, sS=0, vS=0, gS=0, dSa=0, dSb=0,
                 dQ0=0, dQ1=0, dQ2=0, dQ3=0, dP0=0, dP1=0, dXs0=0, dXs1=0)

        nc.const_aps.register(F32, EPS, cns[0:1, 0:1])
        nc.const_aps.register(F32, SQRT128, cns[0:1, 1:2])

        R_M, R_RSTD, R_R2, R_T, R_T2, R_T3 = 0, 1, 2, 3, 4, 5
        R_MX = R_M

        _sc = nc.named_scope("P0"); _sc.__enter__()
        with nc.Block() as blk:
            @blk.gpsimd
            def _(g):
                g.dma_start(cns[:, :], cnsI).then_inc(dW, 16)
                g.dma_start(ones[:, :], onesI).then_inc(dW, 16)
                g.dma_start(ident[:, :], identI).then_inc(dW, 16)
                g.dma_start(gN[:, :], gNI).then_inc(dW, 16)
                g.dma_start(dN[:, :], dNI).then_inc(dW, 16)
                C['dW'] += 80
                g.wait_ge(dW, C['dW'])

        _sc.__exit__(None, None, None)
        s2 = ExitStack()
        with s2:
            xn_b = s2.enter_context(nc.sbuf_tensor("xn_b", [128, ECH, NTOK], BF16))

            _sc = nc.named_scope("PA"); _sc.__enter__()
            pa = ExitStack()
            with pa:
                xA = pa.enter_context(nc.sbuf_tensor("xA", [128, NB, ECH, NBW], F32))
                xsq = pa.enter_context(nc.sbuf_tensor("xsq", [128, ECH, NBW], F32))
                maxacc = pa.enter_context(nc.sbuf_tensor("maxacc", [128, 2, NBW], F32))
                mrep = pa.enter_context(nc.sbuf_tensor("mrep", [128, NTOK], F32))
                rrep = pa.enter_context(nc.sbuf_tensor("rrep", [128, NTOK], F32))
                c1rep = pa.enter_context(nc.sbuf_tensor("c1rep", [128, NTOK], F32))
                mxTM = pa.enter_context(nc.sbuf_tensor("mxTM", [128, 8], F32))
                psS = [pa.enter_context(nc.psum_tensor(f"psS{i}", [1, NBW], F32))
                       for i in range(2)]
                psS2 = [pa.enter_context(nc.psum_tensor(f"psS2{i}", [1, NBW], F32))
                        for i in range(2)]
                psR0 = pa.enter_context(nc.psum_tensor("psR0", [128, NBW], F32))
                psR1 = pa.enter_context(nc.psum_tensor("psR1", [128, NBW], F32))
                psT = pa.enter_context(nc.psum_tensor("psT", [128, NBW], F32))
                psT2 = pa.enter_context(nc.psum_tensor("psT2", [8, 128], F32))

                x3 = xT.rearrange("(c p) n -> p c n", p=128)
                q0b, q1b = C['dQ0'], C['dQ1']
                tSb, sSb, vSb, dWb = C['tS'], C['sS'], C['vS'], C['dW']

                with nc.Block() as blk:
                    @blk.sync
                    def _(sy):
                        for nb in range(NB):
                            sy.dma_start(xA[:, nb, :, :],
                                         x3[:, :, nb * NBW:(nb + 1) * NBW]
                                         ).then_inc(dQ[nb], 16)
                    C['dQ0'] = q0b + 16
                    C['dQ1'] = q1b + 16

                    @blk.scalar
                    def _(s):
                        for nb in range(NB):
                            s.wait_ge(dQ[nb], (q0b if nb == 0 else q1b) + 16)
                            if nb == 1:
                                s.wait_ge(tS, tSb + 1)
                            s.activation(xsq[:, :, :], xA[:, nb, :, :], AF.Square
                                         ).then_inc(sS, 1)
                        for nb in range(NB):
                            s.wait_ge(tS, tSb + nb + 1)
                            nbs = slice(nb * NBW, (nb + 1) * NBW)
                            s.copy(ROW(R_T, nbs), psS[nb][:, :])
                            s.copy(ROW(R_T2, nbs), psS2[nb][:, :]).then_inc(sS, 1)
                        s.wait_ge(vS, vSb + 1)
                        s.activation(ROW(R_RSTD), ROW(R_T), AF.Sqrt).then_inc(sS, 1)
                        s.wait_ge(vS, vSb + 2)
                        s.activation(ROW(R_T), ROW(R_R2), AF.Sqrt).then_inc(sS, 1)
                        s.wait_ge(tS, tSb + 4)
                        s.copy(mrep[:, 0:NBW], psR0[:, :])
                        s.copy(mrep[:, NBW:NTOK], psR1[:, :]).then_inc(sS, 1)
                        s.wait_ge(tS, tSb + 6)
                        s.copy(rrep[:, 0:NBW], psR0[:, :])
                        s.copy(rrep[:, NBW:NTOK], psR1[:, :]).then_inc(sS, 1)
                        for nb in range(NB):
                            s.wait_ge(vS, vSb + 3 + 2 * nb + 1)
                            s.copy(xn_b[:, :, nb * NBW:(nb + 1) * NBW], xA[:, nb, :, :]
                                   ).then_inc(sS, 1)
                        s.wait_ge(tS, tSb + 6 + 8 + 1)
                        s.copy(tmp8[:, :], psT2[:, :]).then_inc(sS, 1)
                        s.wait_ge(tS, tSb + 17)
                        s.copy(c1rep[:, 0:NBW], psR0[:, :])
                        s.copy(c1rep[:, NBW:NTOK], psR1[:, :]).then_inc(sS, 1)
                        for nb in range(NB):
                            s.wait_ge(vS, vSb + 10 + nb)
                            s.activation(xA[:, nb, :, :], xA[:, nb, :, :],
                                         AF.Copy, bias=MAGIC).then_inc(sS, 1)

                    @blk.tensor
                    def _(t):
                        for nb in range(NB):
                            t.wait_ge(sS, sSb + nb + 1)
                            for c in range(ECH):
                                t.matmul(psS[nb][:, :], ones[:, 0:1], xA[:, nb, c, :],
                                         start=(c == 0), stop=(c == ECH - 1),
                                         skip_group_check=True)
                            for c in range(ECH):
                                ins = t.matmul(psS2[nb][:, :], ones[:, 0:1], xsq[:, c, :],
                                               start=(c == 0), stop=(c == ECH - 1),
                                               skip_group_check=True)
                            ins.then_inc(tS, 1)
                        t.wait_ge(vS, vSb + 1)
                        t.wait_ge(sS, sSb + 4)
                        t.matmul(psR0[:, :], ones[0:1, :], ROW(R_M, slice(0, NBW)),
                                 start=True, stop=True).then_inc(tS, 1)
                        t.matmul(psR1[:, :], ones[0:1, :], ROW(R_M, slice(NBW, NTOK)),
                                 start=True, stop=True).then_inc(tS, 1)
                        t.wait_ge(vS, vSb + 3)
                        t.wait_ge(sS, sSb + 7)
                        t.matmul(psR0[:, :], ones[0:1, :], ROW(R_RSTD, slice(0, NBW)),
                                 start=True, stop=True).then_inc(tS, 1)
                        t.matmul(psR1[:, :], ones[0:1, :], ROW(R_RSTD, slice(NBW, NTOK)),
                                 start=True, stop=True).then_inc(tS, 1)
                        for nb in range(NB):
                            t.wait_ge(vS, vSb + 3 + 2 * nb + 2)
                            if nb == 1:
                                t.wait_ge(sS, sSb + 8)
                            ps = psT if nb == 0 else psR0
                            for c4 in range(4):
                                t.transpose(ps[:, c4 * 128:(c4 + 1) * 128],
                                            maxacc[:, nb, c4 * 128:(c4 + 1) * 128],
                                            ident[:, :]).then_inc(tS, 1)
                        t.wait_ge(vS, vSb + 8)
                        t.transpose(psT2[:, :], mxTM[:, 0:8], ident[:, :]).then_inc(tS, 1)
                        t.wait_ge(vS, vSb + 9)
                        t.wait_ge(sS, sSb + 8)
                        t.matmul(psR0[:, :], ones[0:1, :], ROW(R_T2, slice(0, NBW)),
                                 start=True, stop=True).then_inc(tS, 1)
                        t.matmul(psR1[:, :], ones[0:1, :], ROW(R_T2, slice(NBW, NTOK)),
                                 start=True, stop=True).then_inc(tS, 1)

                    @blk.vector
                    def _(v):
                        v.wait_ge(sS, sSb + 4)
                        m = ROW(R_M); t_ = ROW(R_T); t2 = ROW(R_T2)
                        r2 = ROW(R_R2); t3 = ROW(R_T3); rstd = ROW(R_RSTD)
                        v.tensor_scalar(m, t_, 1.0 / 2048.0, None, ALU.mult)
                        v.tensor_scalar(t2, t2, 1.0 / 2048.0, None, ALU.mult)
                        v.tensor_mul(t_, m, m)
                        v.tensor_sub(t2, t2, t_)
                        v.tensor_scalar(t_, t2, EPS, None, ALU.add).then_inc(vS, 1)
                        v.wait_ge(sS, sSb + 5)
                        v.reciprocal(rstd, rstd)
                        v.tensor_mul(r2, rstd, rstd)
                        v.tensor_mul(r2, r2, t2)
                        v.tensor_scalar(r2, r2, EPS, None, ALU.add).then_inc(vS, 1)
                        v.wait_ge(sS, sSb + 6)
                        v.reciprocal(r2, t_).then_inc(vS, 1)
                        for nb in range(NB):
                            v.wait_ge(sS, sSb + 8)
                            nbs = slice(nb * NBW, (nb + 1) * NBW)
                            for c in range(ECH):
                                v.tensor_sub(xA[:, nb, c, :], xA[:, nb, c, :],
                                             mrep[:, nbs])
                                ins = v.tensor_mul(xA[:, nb, c, :], xA[:, nb, c, :],
                                                   rrep[:, nbs])
                            ins.then_inc(vS, 1)
                            for c in range(ECH):
                                v.tensor_scalar(xsq[:, c, :], xA[:, nb, c, :],
                                                gN[:, c:c + 1], None, ALU.mult)
                            v.reduce_max(maxacc[:, nb, :],
                                         bass.AP(xsq, 0, [[ECH * NBW, 128],
                                                          [1, NBW], [NBW, ECH]]),
                                         axis=AX.X, apply_absolute_value=True
                                         ).then_inc(vS, 1)
                        for nb in range(NB):
                            t_off = tSb + 6 + 4 * nb
                            ps = psT if nb == 0 else psR0
                            v.wait_ge(tS, t_off + 4)
                            for c4 in range(4):
                                ins = v.reduce_max(mxTM[:, nb * 4 + c4:nb * 4 + c4 + 1],
                                                   ps[:, c4 * 128:(c4 + 1) * 128],
                                                   axis=AX.X)
                        ins.then_inc(vS, 1)
                        v.wait_ge(dW, dWb + 32)
                        mx = ROW(R_MX)
                        v.tensor_mul(t_, mx, r2)
                        v.tensor_scalar(ROW(R_RSTD), t_, EPS, None, ALU.max)
                        v.reciprocal(t_, ROW(R_RSTD))
                        v.tensor_mul(t_, t_, r2)
                        v.tensor_scalar(t2, t_, 127.0, None, ALU.mult
                                        ).then_inc(vS, 1)
                        for nb in range(NB):
                            v.wait_ge(sS, sSb + 12)
                            nbs = slice(nb * NBW, (nb + 1) * NBW)
                            for c in range(ECH):
                                v.tensor_scalar(xA[:, nb, c, :], xA[:, nb, c, :],
                                                gN[:, c:c + 1], None, ALU.mult)
                            for c in range(ECH):
                                ins = v.tensor_mul(xA[:, nb, c, :], xA[:, nb, c, :],
                                                   c1rep[:, nbs])
                            ins.then_inc(vS, 1)
                        for nb in range(NB):
                            v.wait_ge(sS, sSb + 13 + nb)
                            nbs = slice(nb * NBW, (nb + 1) * NBW)
                            v.tensor_scalar(xA[:, nb, :, :], xA[:, nb, :, :],
                                            -MAGIC, 127.0, ALU.add, ALU.min)
                            v.tensor_scalar(xq_bf[:, :, nbs], xA[:, nb, :, :],
                                            -128.0, None, ALU.max).then_inc(vS, 1)

                    @blk.gpsimd
                    def _(g):
                        g.wait_ge(sS, sSb + 11)
                        g.dma_start(rowsc[:, :], tmp8[:, :]).then_inc(dW, 16)
                        g.wait_ge(dW, dWb + 16)
                        g.dma_start(ROW(R_MX),
                                    rowsc.rearrange("a b -> (a b)")[None, 0:NTOK]
                                    ).then_inc(dW, 16)
                    C['dW'] = dWb + 32

                C['tS'] = tSb + 17
                C['sS'] = sSb + 14
                C['vS'] = vSb + 13

                if debug:
                    with nc.Block() as blk:
                        @blk.sync
                        def _(sy):
                            sy.dma_start(dbg["xn"], xn_b[:, :, :]).then_inc(dX, 16)
                            sy.dma_start(dbg["xq"], xq_bf[:, :, :]).then_inc(dX, 16)
                            sy.dma_start(dbg["rows"], rowt[:, :]).then_inc(dX, 16)
                            C['dX'] += 48
                            sy.wait_ge(dX, C['dX'])

            _sc.__exit__(None, None, None)
            _sc = nc.named_scope("PC"); _sc.__enter__()
            NSG = 88
            NSD = 44
            NST = NSG + NSD
            pc = ExitStack()
            with pc:
                sacc = pc.enter_context(nc.sbuf_tensor("sacc", [128, 160], F32))
                pbuf = pc.enter_context(nc.sbuf_tensor("pbuf", [128, 2, 2048], F32))
                a0 = pc.enter_context(nc.sbuf_tensor("a0", [128, H, NTOK], BF16))
                qw_b = pc.enter_context(nc.sbuf_tensor("qw_b", [128, 2, ECH, 128], BF16))
                kt_b = pc.enter_context(nc.sbuf_tensor("kt_b", [128, 2, MCH, 128], BF16))
                v_b = pc.enter_context(nc.sbuf_tensor("v_b", [128, 2, MCH, 128], BF16))
                q_b = pc.enter_context(nc.sbuf_tensor("q_b", [128, NB, NBW], BF16))
                eb = pc.enter_context(nc.sbuf_tensor("eb", [128, 3, 2 * NBW], BF16))
                psE2 = [pc.enter_context(nc.psum_tensor(f"psE2{i}", [128, 2 * NBW], F32))
                        for i in range(3)]
                psA = [pc.enter_context(nc.psum_tensor(f"psA{i}", [128, NBW], F32))
                       for i in range(2)]
                psQ = [psE2[2][:, 0:NBW], psE2[2][:, NBW:2 * NBW]]

                wq3 = wqT.rearrange("(c p) f -> p c f", p=128)
                v3 = vW.rearrange("(c p) f -> p c f", p=128)
                gw3s = gwT.rearrange("(c p) f -> p c f", p=128)
                dw3s = dwT.rearrange("(c p) e -> p c e", p=128)

                dWb, tSb, sSb, vSb = C['dW'], C['tS'], C['sS'], C['vS']
                p0b, p1b = C['dP0'], C['dP1']
                q0b, q1b = C['dQ0'], C['dQ1']

                def headload_cnt(h):
                    return (p0b if h % 2 == 0 else p1b) + 48 * (h // 2 + 1)

                def posE2(mc):
                    return 3 if mc == 0 else 2 * mc + 2

                def posA2(mc):
                    return 2 * mc + 5 if mc <= 30 else 66

                vsched = []
                for h in range(H):
                    vsched.append(("qe", h, 0))
                    vsched.append(("qe", h, 1))
                    for u in range((NST * h) // H, (NST * (h + 1)) // H):
                        vsched.append(("red", u))
                    vsched.append(("pae", h, 0))
                    vsched.append(("pae", h, 1))
                vpos = {ev: k + 1 for k, ev in enumerate(vsched)}

                with nc.Block() as blk:
                    @blk.gpsimd
                    def _(g):
                        for h in range(H):
                            if h >= 2:
                                g.wait_ge(tS, tSb + 66 * (h - 1))
                            p = h % 2
                            g.dma_start(qw_b[:, p, :, :],
                                        wq3[:, :, h * 128:(h + 1) * 128]).then_inc(dP[p], 16)
                            g.dma_start(kt_b[:, p, :, :],
                                        ktT[h * 128:(h + 1) * 128, :]
                                        .rearrange("p (c d) -> p c d", d=128)).then_inc(dP[p], 16)
                            g.dma_start(v_b[:, p, :, :],
                                        v3[:, :, h * 128:(h + 1) * 128]).then_inc(dP[p], 16)
                    C['dP0'] = p0b + 48 * (H // 2)
                    C['dP1'] = p1b + 48 * (H // 2)

                    @blk.sync
                    def _(sy):
                        for u in range(NST):
                            if u >= 2:
                                sy.wait_ge(vS, vSb + vpos[("red", u - 2)])
                            if u < NSG:
                                src = gw3s[:, :, u * 128:(u + 1) * 128]
                                dst = pbuf[:, u % 2, :].rearrange("p (c f) -> p c f", f=128)
                            else:
                                src = dw3s[:, u - NSG, :]
                                dst = pbuf[:, u % 2, :]
                            sy.dma_start(dst, src).then_inc(dQ[u % 2], 16)
                    C['dQ0'] = q0b + 16 * (NST // 2)
                    C['dQ1'] = q1b + 16 * (NST // 2)

                    @blk.tensor
                    def _(t):
                        for h in range(H):
                            tSh = tSb + 66 * h
                            sSh = sSb + 32 * h
                            p = h % 2
                            t.wait_ge(sS, sSh)
                            if h >= 1:
                                t.wait_ge(vS, vSb + vpos[("pae", h - 1, 1)])
                            t.wait_ge(dP[p], headload_cnt(h))
                            for nb in range(NB):
                                for c in range(ECH):
                                    ins = t.matmul(psQ[nb], qw_b[:, p, c, :],
                                                   xn_b[:, c, nb * NBW:(nb + 1) * NBW],
                                                   start=(c == 0), stop=(c == ECH - 1),
                                                   skip_group_check=True)
                                ins.then_inc(tS, 1)
                            for mc in range(MCH):
                                if mc >= 3:
                                    t.wait_ge(sS, sSh + mc - 2)
                                if mc == 0:
                                    t.wait_ge(vS, vSb + vpos[("qe", h, 1)])
                                if mc == 2:
                                    t.wait_ge(vS, vSb + vpos[("qe", h, 1)])
                                t.matmul(psE2[mc % 3][:, 0:NBW], kt_b[:, p, mc, :],
                                         q_b[:, 0, :], start=True, stop=True,
                                         skip_group_check=True)
                                t.matmul(psE2[mc % 3][:, NBW:2 * NBW], kt_b[:, p, mc, :],
                                         q_b[:, 1, :], start=True, stop=True,
                                         skip_group_check=True).then_inc(tS, 1)
                                if mc >= 1:
                                    amc = mc - 1
                                    t.wait_ge(sS, sSh + amc + 1)
                                    t.matmul(psA[0][:, :], v_b[:, p, amc, :],
                                             eb[:, amc % 3, 0:NBW], start=(amc == 0),
                                             stop=False, skip_group_check=True)
                                    t.matmul(psA[1][:, :], v_b[:, p, amc, :],
                                             eb[:, amc % 3, NBW:2 * NBW], start=(amc == 0),
                                             stop=False, skip_group_check=True
                                             ).then_inc(tS, 1)
                            amc = MCH - 1
                            t.wait_ge(sS, sSh + amc + 1)
                            t.matmul(psA[0][:, :], v_b[:, p, amc, :],
                                     eb[:, amc % 3, 0:NBW], start=False, stop=True,
                                     skip_group_check=True)
                            t.matmul(psA[1][:, :], v_b[:, p, amc, :],
                                     eb[:, amc % 3, NBW:2 * NBW], start=False, stop=True,
                                     skip_group_check=True).then_inc(tS, 1)
                    C['tS'] = tSb + 66 * H

                    @blk.scalar
                    def _(s):
                        for h in range(H):
                            tSh = tSb + 66 * h
                            for mc in range(MCH):
                                s.wait_ge(tS, tSh + posE2(mc))
                                s.activation(eb[:, mc % 3, :], psE2[mc % 3][:, :],
                                             AF.Silu, scale=SQRT128).then_inc(sS, 1)
                    C['sS'] = sSb + 32 * H

                    @blk.vector
                    def _(v):
                        for ev in vsched:
                            if ev[0] == "qe":
                                _, h, nb = ev
                                v.wait_ge(tS, tSb + 66 * h + nb + 1)
                                v.tensor_copy(q_b[:, nb, :], psQ[nb]).then_inc(vS, 1)
                            elif ev[0] == "pae":
                                _, h, nb = ev
                                v.wait_ge(tS, tSb + 66 * h + 66)
                                v.tensor_scalar(a0[:, h, nb * NBW:(nb + 1) * NBW],
                                                psA[nb][:, :], SQRT128, None, ALU.mult
                                                ).then_inc(vS, 1)
                            else:
                                u = ev[1]
                                v.wait_ge(dQ[u % 2],
                                          (q0b if u % 2 == 0 else q1b) + 16 * (u // 2 + 1))
                                col = u if u < NSG else 96 + (u - NSG)
                                v.reduce_sum(sacc[:, col:col + 1], pbuf[:, u % 2, :],
                                             axis=AX.X, apply_absolute_value=True
                                             ).then_inc(vS, 1)
                    C['vS'] = vSb + len(vsched)

                gw_b = pc.enter_context(nc.sbuf_tensor("gw_b", [128, 2, ECH, 128], BF16))
                gt = pc.enter_context(nc.sbuf_tensor("gt", [128, 2, NBW], BF16))
                wg3 = wgT.rearrange("(c p) f -> p c f", p=128)
                dWb, tSb, sSb, vSb = C['dW'], C['tS'], C['sS'], C['vS']
                p0b, p1b = C['dP0'], C['dP1']
                with nc.Block() as blk:
                    @blk.gpsimd
                    def _(g):
                        for cb in range(ECH):
                            if cb >= 2:
                                g.wait_ge(tS, tSb + cb - 1)
                            g.dma_start(gw_b[:, cb % 2, :, :],
                                        wg3[:, :, cb * 128:(cb + 1) * 128]
                                        ).then_inc(dP[cb % 2], 16)
                    C['dP0'] = p0b + 16 * (ECH // 2)
                    C['dP1'] = p1b + 16 * (ECH // 2)

                    @blk.tensor
                    def _(t):
                        for cb in range(ECH):
                            t.wait_ge(dP[cb % 2],
                                      (p0b if cb % 2 == 0 else p1b) + 16 * (cb // 2 + 1))
                            if cb >= 2:
                                t.wait_ge(sS, sSb + cb - 1)
                            for nb in range(NB):
                                for c in range(ECH):
                                    ins = t.matmul(psE2[cb % 2][:, nb * NBW:(nb + 1) * NBW],
                                                   gw_b[:, cb % 2, c, :],
                                                   xn_b[:, c, nb * NBW:(nb + 1) * NBW],
                                                   start=(c == 0), stop=(c == ECH - 1),
                                                   skip_group_check=True)
                            ins.then_inc(tS, 1)
                    C['tS'] = tSb + ECH

                    @blk.scalar
                    def _(s):
                        for cb in range(ECH):
                            s.wait_ge(tS, tSb + cb + 1)
                            if cb >= 2:
                                s.wait_ge(vS, vSb + cb - 1)
                            s.activation(eb[:, cb % 2, :], psE2[cb % 2][:, :],
                                         AF.Silu).then_inc(sS, 1)
                    C['sS'] = sSb + ECH

                    @blk.vector
                    def _(v):
                        for cb in range(ECH):
                            v.wait_ge(sS, sSb + cb + 1)
                            v.tensor_mul(a0[:, cb, :], a0[:, cb, :],
                                         eb[:, cb % 2, :]).then_inc(vS, 1)
                    C['vS'] = vSb + ECH

                if debug:
                    with nc.Block() as blk:
                        @blk.sync
                        def _(sy):
                            sy.dma_start(dbg["a0"], a0[:, :, :]).then_inc(dX, 16)
                            C['dX'] += 16
                            sy.wait_ge(dX, C['dX'])

                wo_b = gw_b
                wo3 = woT.rearrange("(c p) f -> p c f", p=128)
                dWb, tSb, sSb = C['dW'], C['tS'], C['sS']
                dSb_base = [C['dSa'], C['dSb']]
                dS_p = [dSa, dSb]
                p0b, p1b = C['dP0'], C['dP1']

                with nc.Block() as blk:
                    @blk.gpsimd
                    def _(g):
                        for cb in range(ECH):
                            if cb >= 2:
                                g.wait_ge(tS, tSb + cb - 1)
                            g.dma_start(wo_b[:, cb % 2, :, :],
                                        wo3[:, :, cb * 128:(cb + 1) * 128]
                                        ).then_inc(dP[cb % 2], 16)
                            if cb >= 1:
                                g.wait_ge(sS, sSb + cb)
                                g.dma_start(attS[:, cb - 1, :], eb[:, (cb - 1) % 2, :]
                                            ).then_inc(dS_p[(cb - 1) % 2], 16)
                        g.wait_ge(sS, sSb + ECH)
                        g.dma_start(attS[:, ECH - 1, :], eb[:, (ECH - 1) % 2, :]
                                    ).then_inc(dS_p[(ECH - 1) % 2], 16)
                    C['dP0'] = p0b + 16 * (ECH // 2)
                    C['dP1'] = p1b + 16 * (ECH // 2)
                    C['dSa'] = dSb_base[0] + 16 * (ECH // 2)
                    C['dSb'] = dSb_base[1] + 16 * (ECH // 2)

                    @blk.tensor
                    def _(t):
                        for cb in range(ECH):
                            t.wait_ge(dP[cb % 2],
                                      (p0b if cb % 2 == 0 else p1b) + 16 * (cb // 2 + 1))
                            if cb >= 2:
                                t.wait_ge(sS, sSb + cb - 1)
                            for nb in range(NB):
                                for vc in range(ECH):
                                    ins = t.matmul(psE2[cb % 2][:, nb * NBW:(nb + 1) * NBW],
                                                   wo_b[:, cb % 2, vc, :],
                                                   a0[:, vc, nb * NBW:(nb + 1) * NBW],
                                                   start=(vc == 0), stop=(vc == ECH - 1),
                                                   skip_group_check=True)
                            ins.then_inc(tS, 1)
                    C['tS'] = tSb + ECH

                    @blk.scalar
                    def _(s):
                        for cb in range(ECH):
                            s.wait_ge(tS, tSb + cb + 1)
                            if cb >= 2:
                                s.wait_ge(dS_p[cb % 2],
                                          dSb_base[cb % 2] + 16 * ((cb - 2) // 2 + 1))
                            s.copy(eb[:, cb % 2, :], psE2[cb % 2][:, :]).then_inc(sS, 1)
                    C['sS'] = sSb + ECH

                with nc.Block() as blk:
                    @blk.gpsimd
                    def _(g):
                        g.wait_ge(dSa, C['dSa'])
                        g.wait_ge(dSb, C['dSb'])

                if True:
                    psW = psE2[2]
                    psR128 = psE2[2]

                    with nc.Block() as blk:
                        @blk.vector
                        def _(v):
                            v.reduce_sum(sw128[:, 0:1], sacc[:, 0:NSG], axis=AX.X)
                            v.reduce_sum(sw128[:, 1:2], sacc[:, 96:96 + NSD], axis=AX.X)

                    with nc.Block() as blk:
                        @blk.tensor
                        def _(t):
                            t.matmul(psW[0:1, 0:2], ones[:, 0:1], sw128[:, 0:2],
                                     start=True, stop=True)

                    with nc.Block() as blk:
                        @blk.scalar
                        def _(s):
                            s.copy(wstat[0:1, 0:2], psW[0:1, 0:2])

                    with nc.Block() as blk:
                        @blk.vector
                        def _(v):
                            v.tensor_scalar(wstat[0:1, 2:3], wstat[0:1, 0:1],
                                            1.0 / GW_NUMEL, EPS, ALU.mult, ALU.max)
                            v.tensor_scalar(wstat[0:1, 4:5], wstat[0:1, 1:2],
                                            1.0 / DW_NUMEL, EPS, ALU.mult, ALU.max)
                            v.reciprocal(wstat[0:1, 6:7], wstat[0:1, 2:3])
                            v.reciprocal(wstat[0:1, 7:8], wstat[0:1, 4:5])

                    with nc.Block() as blk:
                        @blk.tensor
                        def _(t):
                            t.matmul(psR128[:, 0:2], ones[0:1, :], wstat[0:1, 6:8],
                                     start=True, stop=True)

                    with nc.Block() as blk:
                        @blk.scalar
                        def _(s):
                            s.copy(sw128[:, 0:2], psR128[:, 0:2])

                if debug:
                    with nc.Block() as blk:
                        @blk.sync
                        def _(sy):
                            sy.dma_start(dbg["att"], attS).then_inc(dX, 16)
                            sy.dma_start(dbg["wstat"], wstat[0:1, :]).then_inc(dX, 16)
                            C['dX'] += 32
                            sy.wait_ge(dX, C['dX'])

        _sc.__exit__(None, None, None)
        _sc = nc.named_scope("PD01"); _sc.__enter__()
        pd0 = ExitStack()
        with pd0:
            h_b = pd0.enter_context(nc.sbuf_tensor("h_b", [128, ICH, NTOK], BF16))
            c2rep = pd0.enter_context(nc.sbuf_tensor("c2rep", [128, NTOK], F32))
            d2rep = pd0.enter_context(nc.sbuf_tensor("d2rep", [128, NTOK], F32))

            pd1 = ExitStack()
            with pd1:
                cgrep = d2rep
                gwf = pd1.enter_context(nc.sbuf_tensor("gwf", [128, 2, ECH, 128], F32))
                wqg = pd1.enter_context(nc.sbuf_tensor("wqg", [128, 2, ECH, 128], BF16))
                tg = pd1.enter_context(nc.sbuf_tensor("tg", [128, NBW], F32))
                tu = pd1.enter_context(nc.sbuf_tensor("tu", [128, NBW], F32))
                sg = pd1.enter_context(nc.sbuf_tensor("sg", [128, NBW], F32))
                tmv = tg
                mxv = pd1.enter_context(nc.sbuf_tensor("mxv", [128, 2, NBW], F32))
                mnv = pd1.enter_context(nc.sbuf_tensor("mnv", [128, 2, NBW], F32))
                mxTM2 = pd1.enter_context(nc.sbuf_tensor("mxTM2", [128, 8], F32))
                hsq = pd1.enter_context(nc.sbuf_tensor("hsq", [128, 2, 2, NBW], F32))

                cgs = ExitStack()
                with cgs:
                    psCa = cgs.enter_context(nc.psum_tensor("psCa", [128, NBW], F32))
                    psCb = cgs.enter_context(nc.psum_tensor("psCb", [128, NBW], F32))
                    with nc.Block() as blk:
                        @blk.vector
                        def _(v):
                            v.tensor_scalar(ROW(R_T3), ROW(R_RSTD),
                                            wstat[0:1, 2:3], 1.0 / 127.0,
                                            ALU.mult, ALU.mult)

                    with nc.Block() as blk:
                        @blk.tensor
                        def _(t):
                            t.matmul(psCa[:, :], ones[0:1, :], ROW(R_T3, slice(0, NBW)),
                                     start=True, stop=True)
                            t.matmul(psCb[:, :], ones[0:1, :], ROW(R_T3, slice(NBW, NTOK)),
                                     start=True, stop=True)

                    with nc.Block() as blk:
                        @blk.scalar
                        def _(s):
                            s.copy(cgrep[:, 0:NBW], psCa[:, :])
                            s.copy(cgrep[:, NBW:NTOK], psCb[:, :])

                gw3b = gwT.rearrange("(c p) f -> p c f", p=128)
                dXb, tSb, sSb, vSb = C['dX'], C['tS'], C['sS'], C['vS']
                NJ = ICH
                NS = 2 * NJ
                q0b, q1b = C['dQ0'], C['dQ1']

                vsched = [("vq", 0), ("vq", 1)]
                for j in range(NJ):
                    for s2 in (2 * j + 2, 2 * j + 3):
                        if s2 < NS:
                            vsched.append(("vq", s2))
                    for nb in range(NB):
                        vsched.append(("v1", j, nb))
                        vsched.append(("v2", j, nb))
                vpos = {ev: k + 1 for k, ev in enumerate(vsched)}

                def vq_cnt(s):
                    return vSb + vpos[("vq", s)]

                def v1_cnt(j, nb):
                    return vSb + vpos[("v1", j, nb)]

                def v2_cnt(j, nb):
                    return vSb + vpos[("v2", j, nb)]

                ssched = [("p3", 0), ("p3", 1)]
                for j in range(NJ):
                    for s2 in (2 * j + 2, 2 * j + 3):
                        if s2 < NS:
                            ssched.append(("p3", s2))
                    for nb in range(NB):
                        ssched.append(("silu", j, nb))
                spos = {ev: k + 1 for k, ev in enumerate(ssched)}

                def p3_cnt(s):
                    return sSb + spos[("p3", s)]

                def silu_cnt(j, nb):
                    return sSb + spos[("silu", j, nb)]

                def t1_cnt(s, nb):
                    j, hf = s // 2, s % 2
                    return tSb + 4 * j + (1 + nb if hf == 0 else 3 + nb)

                def fcol(s):
                    j, hf = s // 2, s % 2
                    return (j if hf == 0 else ICH + j) * 128

                lps = ExitStack()
                with lps:
                    psG = [[lps.enter_context(nc.psum_tensor(f"psG{p}{n}", [128, NBW], F32))
                            for n in range(2)] for p in range(2)]
                    psU = [[lps.enter_context(nc.psum_tensor(f"psU{p}{n}", [128, NBW], F32))
                            for n in range(2)] for p in range(2)]

                    with nc.Block() as blk:
                        @blk.sync
                        def _(sy):
                            for s_ in range(NS):
                                if s_ >= 2:
                                    sy.wait_ge(sS, p3_cnt(s_ - 2))
                                sy.dma_start(gwf[:, s_ % 2, :, :],
                                             gw3b[:, :, fcol(s_):fcol(s_) + 128]
                                             ).then_inc(dQ[s_ % 2], 16)
                        C['dQ0'] = q0b + 16 * (NS // 2)
                        C['dQ1'] = q1b + 16 * (NS // 2)

                        @blk.vector
                        def _(v):
                            v.memset(mxv[:, 0, :], 0.0)
                            v.memset(mxv[:, 1, :], 0.0)
                            v.memset(mnv[:, 0, :], 0.0)
                            v.memset(mnv[:, 1, :], 0.0)
                            for ev in vsched:
                                if ev[0] == "vq":
                                    s_ = ev[1]
                                    v.wait_ge(dQ[s_ % 2],
                                              (q0b if s_ % 2 == 0 else q1b)
                                              + 16 * (s_ // 2 + 1))
                                    v.tensor_scalar(gwf[:, s_ % 2, :, :],
                                                    gwf[:, s_ % 2, :, :],
                                                    sw128[:, 0:1], 1.0, ALU.mult, ALU.min)
                                    v.tensor_scalar(gwf[:, s_ % 2, :, :],
                                                    gwf[:, s_ % 2, :, :],
                                                    -1.0, MAGIC, ALU.max, ALU.add
                                                    ).then_inc(vS, 1)
                                elif ev[0] == "v1":
                                    _, j, nb = ev
                                    jp = j % 2
                                    v.wait_ge(tS, t1_cnt(2 * j + 1, nb))
                                    v.tensor_mul(tg[:, :], psG[jp][nb][:, :],
                                                 cgrep[:, nb * NBW:(nb + 1) * NBW])
                                    v.tensor_mul(tu[:, :], psU[jp][nb][:, :],
                                                 cgrep[:, nb * NBW:(nb + 1) * NBW]
                                                 ).then_inc(vS, 1)
                                else:
                                    _, j, nb = ev
                                    v.wait_ge(sS, silu_cnt(j, nb))
                                    hsl = h_b[:, j, nb * NBW:(nb + 1) * NBW]
                                    v.tensor_mul(hsl, sg[:, :], tu[:, :])
                                    v.tensor_scalar(tmv[:, :], hsl, dN[:, j:j + 1],
                                                    None, ALU.mult)
                                    v.tensor_tensor(mxv[:, nb, :], mxv[:, nb, :],
                                                    tmv[:, :], ALU.max)
                                    v.tensor_tensor(mnv[:, nb, :], mnv[:, nb, :],
                                                    tmv[:, :], ALU.min
                                                    ).then_inc(vS, 1)
                        C['vS'] = vSb + len(vsched)

                        @blk.scalar
                        def _(s):
                            for ev in ssched:
                                if ev[0] == "p3":
                                    s_ = ev[1]
                                    s.wait_ge(vS, vq_cnt(s_))
                                    if s_ >= 2:
                                        s.wait_ge(tS, t1_cnt(s_ - 2, 1))
                                    s.activation(wqg[:, s_ % 2, :, :],
                                                 gwf[:, s_ % 2, :, :],
                                                 AF.Copy, bias=-MAGIC).then_inc(sS, 1)
                                else:
                                    _, j, nb = ev
                                    s.wait_ge(vS, v1_cnt(j, nb))
                                    if nb == 0 and j >= 1:
                                        s.wait_ge(vS, v2_cnt(j - 1, 1))
                                    s.activation(sg[:, :], tg[:, :], AF.Silu
                                                 ).then_inc(sS, 1)
                        C['sS'] = sSb + len(ssched)

                        @blk.tensor
                        def _(t):
                            for s_ in range(NS):
                                j, hf = s_ // 2, s_ % 2
                                jp = j % 2
                                t.wait_ge(sS, p3_cnt(s_))
                                ps = psG[jp] if hf == 0 else psU[jp]
                                for nb in range(NB):
                                    if j >= 2:
                                        t.wait_ge(vS, v1_cnt(j - 2, nb))
                                    for c in range(ECH):
                                        ins = t.matmul(ps[nb][:, :], wqg[:, s_ % 2, c, :],
                                                       xq_bf[:, c, nb * NBW:(nb + 1) * NBW],
                                                       start=(c == 0), stop=(c == ECH - 1),
                                                       skip_group_check=True)
                                    ins.then_inc(tS, 1)
                        C['tS'] = tSb + 4 * NJ

                _sc.__exit__(None, None, None)
                _sc = nc.named_scope("PD1fin"); _sc.__enter__()
                fin = ExitStack()
                with fin:
                    psSh = [fin.enter_context(nc.psum_tensor(f"psSh{i}", [1, NBW], F32))
                            for i in range(2)]
                    psFa = fin.enter_context(nc.psum_tensor("psFa", [128, NBW], F32))
                    psFb = fin.enter_context(nc.psum_tensor("psFb", [128, NBW], F32))
                    psFc = fin.enter_context(nc.psum_tensor("psFc", [8, 128], F32))
                    psRa = fin.enter_context(nc.psum_tensor("psRa2", [128, NBW], F32))
                    psRb = fin.enter_context(nc.psum_tensor("psRb2", [128, NBW], F32))

                    sSb2, tSb2 = C['sS'], C['tS']
                    with nc.Block() as blk:
                        @blk.scalar
                        def _(s):
                            for j in range(ICH):
                                for nb in range(NB):
                                    if j >= 2:
                                        s.wait_ge(tS, tSb2 + 2 * (j - 2) + nb + 1)
                                    s.activation(hsq[:, j % 2, nb, :],
                                                 h_b[:, j, nb * NBW:(nb + 1) * NBW],
                                                 AF.Square).then_inc(sS, 1)

                        @blk.tensor
                        def _(t):
                            for j in range(ICH):
                                for nb in range(NB):
                                    t.wait_ge(sS, sSb2 + 2 * j + nb + 1)
                                    t.matmul(psSh[nb][:, :], ones[:, 0:1],
                                             hsq[:, j % 2, nb, :],
                                             start=(j == 0), stop=(j == ICH - 1),
                                             skip_group_check=True).then_inc(tS, 1)
                    C['sS'] = sSb2 + 2 * ICH
                    C['tS'] = tSb2 + 2 * ICH

                    with nc.Block() as blk:
                        @blk.scalar
                        def _(s):
                            s.copy(ROW(R_T, slice(0, NBW)), psSh[0][:, :])
                            s.copy(ROW(R_T, slice(NBW, NTOK)), psSh[1][:, :])

                    with nc.Block() as blk:
                        @blk.vector
                        def _(v):
                            for nb in range(NB):
                                v.tensor_scalar(mnv[:, nb, :], mnv[:, nb, :], -1.0,
                                                None, ALU.mult)
                                v.tensor_tensor(mxv[:, nb, :], mxv[:, nb, :],
                                                mnv[:, nb, :], ALU.max)

                    with nc.Block() as blk:
                        @blk.tensor
                        def _(t):
                            for nb in range(NB):
                                ps = psFa if nb == 0 else psFb
                                for c4 in range(4):
                                    t.transpose(ps[:, c4 * 128:(c4 + 1) * 128],
                                                mxv[:, nb, c4 * 128:(c4 + 1) * 128],
                                                ident[:, :])

                    with nc.Block() as blk:
                        @blk.vector
                        def _(v):
                            for nb in range(NB):
                                ps = psFa if nb == 0 else psFb
                                for c4 in range(4):
                                    v.reduce_max(mxTM2[:, nb * 4 + c4:nb * 4 + c4 + 1],
                                                 ps[:, c4 * 128:(c4 + 1) * 128], axis=AX.X)

                    with nc.Block() as blk:
                        @blk.tensor
                        def _(t):
                            t.transpose(psFc[:, :], mxTM2[:, 0:8], ident[:, :])

                    with nc.Block() as blk:
                        @blk.scalar
                        def _(s):
                            s.copy(tmp8[:, :], psFc[:, :])

                    with nc.Block() as blk:
                        @blk.gpsimd
                        def _(g):
                            g.dma_start(rowsc[:, :], tmp8[:, :]).then_inc(dW, 16)
                            C['dW'] += 16
                            g.wait_ge(dW, C['dW'])
                            g.dma_start(ROW(R_MX),
                                        rowsc.rearrange("a b -> (a b)")[None, 0:NTOK]
                                        ).then_inc(dW, 16)
                            C['dW'] += 16
                            g.wait_ge(dW, C['dW'])

                    with nc.Block() as blk:
                        @blk.vector
                        def _(v):
                            hs = ROW(R_T)
                            v.tensor_scalar(hs, hs, 1.0 / 5632.0, EPS, ALU.mult, ALU.add)

                    with nc.Block() as blk:
                        @blk.scalar
                        def _(s):
                            s.activation(ROW(R_T2), ROW(R_T), AF.Sqrt)

                    with nc.Block() as blk:
                        @blk.vector
                        def _(v):
                            w = ROW(R_T)
                            r = ROW(R_T2)
                            t3 = ROW(R_T3)
                            mx = ROW(R_MX)
                            v.reciprocal(r, r)
                            v.tensor_mul(t3, mx, r)
                            v.tensor_scalar(t3, t3, EPS, None, ALU.max)
                            v.reciprocal(w, t3)
                            v.tensor_mul(w, w, r)
                            v.tensor_scalar(ROW(R_T), w, 127.0, None, ALU.mult)
                            v.tensor_scalar(t3, t3, wstat[0:1, 4:5], 1.0 / 127.0,
                                            ALU.mult, ALU.mult)

                    with nc.Block() as blk:
                        @blk.tensor
                        def _(t):
                            t.matmul(psRa[:, :], ones[0:1, :], ROW(R_T, slice(0, NBW)),
                                     start=True, stop=True)
                            t.matmul(psRb[:, :], ones[0:1, :], ROW(R_T, slice(NBW, NTOK)),
                                     start=True, stop=True)
                            t.matmul(psFa[:, :], ones[0:1, :], ROW(R_T3, slice(0, NBW)),
                                     start=True, stop=True)
                            t.matmul(psFb[:, :], ones[0:1, :], ROW(R_T3, slice(NBW, NTOK)),
                                     start=True, stop=True)

                    with nc.Block() as blk:
                        @blk.scalar
                        def _(s):
                            s.copy(c2rep[:, 0:NBW], psRa[:, :])
                            s.copy(c2rep[:, NBW:NTOK], psRb[:, :])
                            s.copy(d2rep[:, 0:NBW], psFa[:, :])
                            s.copy(d2rep[:, NBW:NTOK], psFb[:, :])

                if debug:
                    with nc.Block() as blk:
                        @blk.sync
                        def _(sy):
                            sy.dma_start(dbg["h"], h_b[:, :, :]).then_inc(dX, 16)
                            sy.dma_start(dbg["wq"], wqg[:, :, :, :]).then_inc(dX, 16)
                            sy.dma_start(dbg["gwf"], gwf[:, :, :, :]).then_inc(dX, 16)
                            sy.dma_start(dbg["sw"], sw128[:, :]).then_inc(dX, 16)
                            C['dX'] += 64
                            sy.wait_ge(dX, C['dX'])

            sxq.close()

            _sc.__exit__(None, None, None)
            _sc = nc.named_scope("PD2"); _sc.__enter__()
            pd2 = ExitStack()
            with pd2:
                hq_b = pd2.enter_context(nc.sbuf_tensor("hq_b", [128, ICH, NBW], BF16))
                tqa = pd2.enter_context(nc.sbuf_tensor("tqa", [128, 3, NBW], F32))
                tqb = pd2.enter_context(nc.sbuf_tensor("tqb", [128, 2, NBW], F32))
                dwf = pd2.enter_context(nc.sbuf_tensor("dwf", [128, 3, 8, 128], F32))
                dwq = pd2.enter_context(nc.sbuf_tensor("dwq", [128, 3, 8, 128], BF16))
                xtile = pd2.enter_context(nc.sbuf_tensor("xtile", [128, 2, NBW], F32))
                atile = pd2.enter_context(nc.sbuf_tensor("atile", [128, 2, NBW], BF16))
                otile = pd2.enter_context(nc.sbuf_tensor("otile", [128, 2, NBW], F32))
                psD = [pd2.enter_context(nc.psum_tensor(f"psD{i}", [128, NBW], F32))
                       for i in range(8)]

                for ng in range(NB):
                    ngs = slice(ng * NBW, (ng + 1) * NBW)

                    for eg in range(2):
                        qb4 = [C[f'dQ{i}'] for i in range(4)]
                        vb, sb, gb, tb = C['vS'], C['sS'], C['gS'], C['tS']

                        if eg == 0:
                            vsc = [("A", 0), ("A", 1)]
                            ssc = []
                            for i in range(ICH):
                                vsc.append(("w", i))
                                if i + 2 < ICH:
                                    vsc.append(("A", i + 2))
                                vsc.append(("B", i))
                                ssc.append(("p3", i))
                                ssc.append(("ngq", i))
                            vp2 = {ev: k + 1 for k, ev in enumerate(vsc)}
                            sp2 = {ev: k + 1 for k, ev in enumerate(ssc)}
                        else:
                            vsc = [("w", i) for i in range(ICH)]
                            ssc = [("p3", i) for i in range(ICH)]
                            vp2 = {ev: k + 1 for k, ev in enumerate(vsc)}
                            sp2 = {ev: k + 1 for k, ev in enumerate(ssc)}

                        with nc.Block() as blk:
                            @blk.sync
                            def _(sy, sb=sb, eg=eg, sp2=sp2):
                                for i in range(ICH):
                                    if i >= 3:
                                        sy.wait_ge(sS, sb + sp2[("p3", i - 3)])
                                    src = dwT[i * 128:(i + 1) * 128,
                                              eg * 1024:(eg + 1) * 1024].rearrange(
                                        "p (c e) -> p c e", e=128)
                                    sy.dma_start(dwf[:, i % 3, :, :], src
                                                 ).then_inc(dQ[i % 3], 16)
                            for i in range(3):
                                C[f'dQ{i}'] = qb4[i] + 16 * len(range(i, ICH, 3))

                            @blk.vector
                            def _(v, qb4=qb4, vb=vb, sb=sb, eg=eg, ngs=ngs,
                                  vsc=vsc, sp2=sp2):
                                for ev in vsc:
                                    kind, i = ev
                                    if kind == "w":
                                        v.wait_ge(dQ[i % 3], qb4[i % 3] + 16 * (i // 3 + 1))
                                        v.tensor_scalar(dwf[:, i % 3, :, :],
                                                        dwf[:, i % 3, :, :],
                                                        sw128[:, 1:2], 1.0,
                                                        ALU.mult, ALU.min)
                                        v.tensor_scalar(dwf[:, i % 3, :, :],
                                                        dwf[:, i % 3, :, :],
                                                        -1.0, MAGIC, ALU.max, ALU.add
                                                        ).then_inc(vS, 1)
                                    elif kind == "A":
                                        if i >= 3:
                                            v.wait_ge(sS, sb + sp2[("ngq", i - 3)])
                                        v.tensor_scalar(tqa[:, i % 3, :], h_b[:, i, ngs],
                                                        dN[:, i:i + 1], None, ALU.mult)
                                        v.tensor_mul(tqa[:, i % 3, :], tqa[:, i % 3, :],
                                                     c2rep[:, ngs]).then_inc(vS, 1)
                                    else:
                                        v.wait_ge(sS, sb + sp2[("ngq", i)])
                                        v.tensor_scalar(tqb[:, i % 2, :], tqb[:, i % 2, :],
                                                        -MAGIC, 127.0, ALU.add, ALU.min)
                                        v.tensor_scalar(hq_b[:, i, :], tqb[:, i % 2, :],
                                                        -128.0, None, ALU.max
                                                        ).then_inc(vS, 1)
                            C['vS'] = vb + len(vsc)

                            @blk.scalar
                            def _(s, vb=vb, sb=sb, tb=tb, ssc=ssc, vp2=vp2):
                                for ev in ssc:
                                    kind, i = ev
                                    if kind == "p3":
                                        s.wait_ge(vS, vb + vp2[("w", i)])
                                        if i >= 3:
                                            s.wait_ge(tS, tb + i - 2)
                                        s.activation(dwq[:, i % 3, :, :],
                                                     dwf[:, i % 3, :, :],
                                                     AF.Copy, bias=-MAGIC).then_inc(sS, 1)
                                    else:
                                        s.wait_ge(vS, vb + vp2[("A", i)])
                                        s.activation(tqb[:, i % 2, :], tqa[:, i % 3, :],
                                                     AF.Copy, bias=MAGIC).then_inc(sS, 1)
                            C['sS'] = sb + len(ssc)

                            @blk.tensor
                            def _(t, sb=sb, vb=vb, eg=eg, sp2=sp2, vp2=vp2):
                                for i in range(ICH):
                                    t.wait_ge(sS, sb + sp2[("p3", i)])
                                    if eg == 0:
                                        t.wait_ge(vS, vb + vp2[("B", i)])
                                    for ec in range(8):
                                        ins = t.matmul(psD[ec][:, :], dwq[:, i % 3, ec, :],
                                                       hq_b[:, i, :], start=(i == 0),
                                                       stop=(i == ICH - 1),
                                                       skip_group_check=True)
                                    ins.then_inc(tS, 1)
                            C['tS'] = tb + ICH

                        q0b, q1b = C['dQ0'], C['dQ1']
                        x0b, x1b = C['dXs0'], C['dXs1']
                        vb = C['vS']
                        ev_seq = []
                        for ec in range(8):
                            ev_seq.append(("L", ec))
                            if ec >= 2:
                                ev_seq.append(("S", ec - 2))
                        ev_seq += [("S", 6), ("S", 7)]
                        with nc.Block() as blk:
                            @blk.sync
                            def _(sy, vb=vb, ng=ng, eg=eg, ngs=ngs, ev_seq=ev_seq):
                                for kind, ec in ev_seq:
                                    row = (eg * 8 + ec) * 128
                                    if kind == "L":
                                        if ec >= 2:
                                            sy.wait_ge(vS, vb + ec - 1)
                                        sy.dma_start(xtile[:, ec % 2, :],
                                                     xT[row:row + 128, ngs]
                                                     ).then_inc(dQ[ec % 2], 16)
                                        sy.dma_start(atile[:, ec % 2, :],
                                                     attS[:, eg * 8 + ec, ngs]
                                                     ).then_inc(dQ[ec % 2], 16)
                                    else:
                                        sy.wait_ge(vS, vb + ec + 1)
                                        sy.dma_start(outT[row:row + 128, ngs],
                                                     otile[:, ec % 2, :]
                                                     ).then_inc(dXs[ec % 2], 16)
                            C['dQ0'] = q0b + 32 * 4
                            C['dQ1'] = q1b + 32 * 4
                            C['dXs0'] = x0b + 16 * 4
                            C['dXs1'] = x1b + 16 * 4

                            @blk.vector
                            def _(v, q0b=q0b, q1b=q1b, x0b=x0b, x1b=x1b, vb=vb, ngs=ngs):
                                for ec in range(8):
                                    p = ec % 2
                                    v.wait_ge(dQ[p], (q0b if p == 0 else q1b)
                                              + 32 * (ec // 2 + 1))
                                    if ec >= 2:
                                        v.wait_ge(dXs[p], (x0b if p == 0 else x1b)
                                                  + 16 * (ec // 2))
                                    v.tensor_mul(otile[:, p, :], psD[ec][:, :],
                                                 d2rep[:, ngs])
                                    v.tensor_add(otile[:, p, :], otile[:, p, :],
                                                 atile[:, p, :])
                                    v.tensor_add(otile[:, p, :], otile[:, p, :],
                                                 xtile[:, p, :]).then_inc(vS, 1)
                            C['vS'] = vb + 8

        _sc.__exit__(None, None, None)
        with nc.Block() as blk:
            @blk.sync
            def _(sy):
                sy.wait_ge(dX, C['dX'])
                sy.wait_ge(dSa, C['dSa'])
                sy.wait_ge(dSb, C['dSb'])
                sy.wait_ge(dQ[0], C['dQ0'])
                sy.wait_ge(dQ[1], C['dQ1'])
                sy.wait_ge(dQ[2], C['dQ2'])
                sy.wait_ge(dQ[3], C['dQ3'])
                sy.wait_ge(dP[0], C['dP0'])
                sy.wait_ge(dP[1], C['dP1'])
                sy.wait_ge(dXs[0], C['dXs0'])
                sy.wait_ge(dXs[1], C['dXs1'])

            @blk.gpsimd
            def _(g):
                g.wait_ge(dW, C['dW'])

    return nc


_CACHED = {}


def _get_nc(debug):
    key = bool(debug)
    if key not in _CACHED:
        _CACHED[key] = build_core_program(debug=key)
    return _CACHED[key]


def kernel(L0_attn_x, L0_attn_Wq, L0_attn_k_weight, L0_attn_v_weight, L0_attn_Wg,
           L0_attn_Wout, L0_ffn_gate_w, L0_ffn_down_w, L0_ffn_gate_norm_w,
           L0_ffn_down_norm_w):
    x = np.asarray(L0_attn_x)
    B, N, E = x.shape
    xt = np.ascontiguousarray(x.reshape(N, E).T)

    wq = np.ascontiguousarray(np.asarray(L0_attn_Wq).T)
    kt = np.ascontiguousarray(np.asarray(L0_attn_k_weight).T)
    vw = np.ascontiguousarray(np.asarray(L0_attn_v_weight))
    wg = np.ascontiguousarray(np.asarray(L0_attn_Wg).T)
    wo = np.ascontiguousarray(np.asarray(L0_attn_Wout).T)
    gw = np.ascontiguousarray(np.asarray(L0_ffn_gate_w).T)
    dw = np.ascontiguousarray(np.asarray(L0_ffn_down_w).T)
    gn = np.ascontiguousarray(np.asarray(L0_ffn_gate_norm_w).reshape(16, 128).T)
    dn = np.ascontiguousarray(np.asarray(L0_ffn_down_norm_w).reshape(44, 128).T)

    cns = np.zeros((128, 16), np.float32)
    cns[:, 0] = EPS
    cns[:, 1] = SQRT128
    onesf = np.ones((128, 128), np.float32)
    identf = np.eye(128, dtype=np.float32)

    nc = _get_nc(DEBUG)
    in_maps = []
    for c in range(8):
        in_maps.append(dict(
            xT=np.ascontiguousarray(xt[:, c * NTOK:(c + 1) * NTOK]),
            wqT=wq, ktT=kt, vW=vw, wgT=wg, woT=wo, gwT=gw, dwT=dw,
            cns=cns, onesf=onesf, identf=identf, gN128=gn, dN128=dn,
        ))
    res = run_bass_kernel_spmd(nc, in_maps, core_ids=list(range(8)))
    out = np.concatenate([r["outT"] for r in res.results], axis=1)
    if DEBUG:
        kernel._dbg = res.results
    return np.ascontiguousarray(out.T).reshape(B, N, E).astype(np.float32)
